# version 04
# baseline (speedup 1.0000x reference)
import numpy as np
import ml_dtypes
import concourse.bass as bass
import concourse.mybir as mybir
from concourse.bass_utils import run_bass_kernel_spmd
from concourse.alu_op_type import AluOpType as OP

F32 = mybir.dt.float32
BF16 = mybir.dt.bfloat16
U32 = mybir.dt.uint32
AF = mybir.ActivationFunctionType
AX = mybir.AxisListType
BF = ml_dtypes.bfloat16

NCORE = 8
TCORE = 4096
TGLOB = 32768
D = 1024
NCHK = 8
NTT = 8
NTC = 32
H, DH = 16, 64
E, ELOC, CAP = 32, 4, 1280
INTER = 2048
NG, AI = 16, 128
EPS = 1e-5
SCALE_QK = 0.125
BISECT = 28

ENGINES = ("tensor", "vector", "scalar", "gpsimd", "sync")


class Sched:

    def __init__(self, nc, ctx):
        self.nc = nc
        self.ops = {e: [] for e in ENGINES}
        self.sems = {e: ctx.enter_context(nc.semaphore("sem_" + e))
                     for e in ENGINES}
        self.cc_sem = ctx.enter_context(nc.semaphore("sem_cc"))
        self.cc_count = 0

    def op(self, eng, emit, deps=(), amt=1, tag=""):
        tok = (eng, len(self.ops[eng]))
        self.ops[eng].append(dict(emit=emit, deps=list(deps), amt=amt,
                                  marked=False, tag=tag))
        return tok

    def dma(self, eng, emit, deps=(), tag=""):
        return self.op(eng, emit, deps, amt=16, tag=tag)

    def barrier_tokens(self):
        toks = []
        for e in ENGINES:
            if self.ops[e]:
                toks.append((e, len(self.ops[e]) - 1))
        return toks

    def finalize(self, block):
        for e in ENGINES:
            for o in self.ops[e]:
                if o["amt"] == 16:
                    o["marked"] = True
                for (pe, pi) in o["deps"]:
                    self.ops[pe][pi]["marked"] = True
        for e in ENGINES:
            c = 0
            for o in self.ops[e]:
                if o["marked"]:
                    c += o["amt"]
                o["cum"] = c
        ptr = {e: 0 for e in ENGINES}
        done_cum = {e: 0 for e in ENGINES}
        total = sum(len(v) for v in self.ops.values())
        ndone = 0
        while ndone < total:
            progressed = False
            for e in ENGINES:
                while ptr[e] < len(self.ops[e]):
                    o = self.ops[e][ptr[e]]
                    if all(self.ops[pe][pi]["cum"] <= done_cum[pe]
                           for (pe, pi) in o["deps"]):
                        done_cum[e] = o["cum"]
                        ptr[e] += 1
                        ndone += 1
                        progressed = True
                    else:
                        break
            if not progressed:
                stuck = {e: (ptr[e], len(self.ops[e])) for e in ENGINES}
                for e in ENGINES:
                    if ptr[e] < len(self.ops[e]):
                        o = self.ops[e][ptr[e]]
                        raise RuntimeError(
                            f"deadlock: {e}[{ptr[e]}] tag={o['tag']} deps="
                            f"{[(pe, pi, self.ops[pe][pi]['tag']) for pe, pi in o['deps']]} {stuck}")
        sched = self

        def emit_engine(name):
            def body(eng):
                waited = {e: 0 for e in ENGINES}
                for o in sched.ops[name]:
                    need = {}
                    for (pe, pi) in o["deps"]:
                        v = sched.ops[pe][pi]["cum"]
                        if v > need.get(pe, 0):
                            need[pe] = v
                    for pe, v in need.items():
                        if v > waited[pe]:
                            eng.wait_ge(sched.sems[pe], v)
                            waited[pe] = v
                    instr = o["emit"](eng)
                    if o["marked"]:
                        assert instr is not None, f"marked op {o['tag']} returned None"
                        instr.then_inc(sched.sems[name], o["amt"])
            return body

        for name in ENGINES:
            getattr(block, name)(emit_engine(name))


def build_nc(debug=()):
    nc = bass.Bass("TRN2", target_bir_lowering=False)

    def din(name, shape, dtype=F32):
        return nc.dram_tensor(name, shape, dtype, kind="ExternalInput").ap()

    xh = din("xh", [TCORE, D])
    wq = din("wq", [D, D]); wk = din("wk", [D, D]); wt = din("wt", [D, D])
    bqc = din("bqc", [128, 8]); bkc = din("bkc", [128, 8]); btc = din("btc", [128, 8])
    wqa = din("wqa", [D, H]); wka = din("wka", [D, H])
    bqa = din("bqa", [H, 1]); bka = din("bka", [H, 1])
    r1wt = din("r1wt", [1040, 512]); r1bc = din("r1bc", [128, 4])
    r2wt = din("r2wt", [512, 32])
    wupt = din("wupt", [ELOC, D, 2 * INTER])
    wdnt = din("wdnt", [ELOC, INTER, D])
    aupt = din("aupt", [NG, D, 2 * AI])
    adnt = din("adnt", [NG, AI, D])
    c_mask8 = din("c_mask8", [8, 1])
    c_tokbase = din("c_tokbase", [128, 1])
    c_poolA = din("c_poolA", [16, 1], U32)
    c_poolB = din("c_poolB", [16, 1], U32)
    c_sgem = din("c_sgem", [128, 1], U32)
    c_idxl = din("c_idxl", [40, 1], U32)
    c_eoff = din("c_eoff", [1, 32])

    o_out = nc.dram_tensor("o_out", [TCORE, D], F32, kind="ExternalOutput").ap()
    dbg = {}
    dbg_shapes = {
        "x1": ([TCORE, D], BF16), "stats": ([8, TCORE], BF16),
        "probs": ([TCORE, E], F32), "theta": ([E, 8], F32),
        "mfull": ([TCORE, E], F32), "cnt": ([NCORE, E], F32),
        "gscat": ([TCORE, 2], F32), "idxg": ([41088, 2], F32),
        "yw": ([ELOC * CAP, D], BF16), "advec": ([TCORE, D], BF16),
        "hrow": ([16, TCORE], BF16), "lgt": ([E, TCORE], F32),
        "xtm": ([TCORE, D], BF16), "ypred": ([D, TCORE], BF16),
        "poolcq": ([128, 8], F32), "poolck": ([128, 8], F32),
        "mqd": ([128, 4096], BF16),
        "x1fm": ([128, 32768], BF16), "stats2": ([16, 4096], BF16),
        "hfm": ([128, 16384], BF16), "lgtt": ([32, 4096], F32),
        "w01": ([128, 64], F32), "y2pre": ([TCORE, D], F32),
        "xe0": ([CAP, D], BF16), "idsT": ([128, 40], U32),
        "xe2": ([CAP, D], BF16),
        "xef2": ([128, 10240], BF16), "hact2": ([128, 20480], BF16),
    }
    for name in debug:
        shp, dt = dbg_shapes[name]
        dbg[name] = nc.dram_tensor("dbg_" + name, shp, dt, kind="ExternalOutput").ap()

    def dint(name, shape, dtype):
        return nc.dram_tensor(name, shape, dtype, kind="Internal").ap()

    d_xtm = dint("d_xtm", [TCORE, D], BF16)
    d_ypre = dint("d_ypre", [D, TCORE], BF16)
    d_x1tm = dint("d_x1tm", [TCORE, D], BF16)
    d_nq = dint("d_nq", [8, 128], F32)
    d_pin = [dint(f"d_pin{i}", [16, 72], F32) for i in range(2)]
    d_pout = [dint(f"d_pout{i}", [128, 72], F32) for i in range(2)]
    d_sem_ = dint("d_sem", [E, TCORE], F32)
    d_sgem = dint("d_sgem", [NCORE * E, TCORE], F32)
    d_thin = dint("d_thin", [ELOC, 8], F32)
    d_thout = dint("d_thout", [E, 8], F32)
    d_cnin = dint("d_cnin", [1, E], F32)
    d_cnout = dint("d_cnout", [NCORE, E], F32)
    MROW = 41088
    d_idx = dint("d_idx", [MROW, 2], F32)
    d_idxg = dint("d_idxg", [MROW, 2], F32)
    d_xg = dint("d_xg", [TGLOB, D], BF16)
    d_yw = dint("d_yw", [ELOC * CAP, D], BF16)
    d_ywg = dint("d_ywg", [NCORE * ELOC * CAP + 128, D], BF16)
    d_ad = dint("d_ad", [TCORE, D], BF16)
    d_xe = [dint(f"d_xe{i}", [CAP, D], BF16) for i in range(2)]
    d_yefm = dint("d_yefm", [D, CAP], BF16)
    d_stat8 = dint("d_stat8", [TCORE, 8], BF16)

    from contextlib import ExitStack
    with ExitStack() as ctx:
        def sbt(name, shape, dtype=F32):
            return ctx.enter_context(nc.sbuf_tensor(name, shape, dtype))

        def pst(name, shape, dtype=F32):
            return ctx.enter_context(nc.psum_tensor(name, shape, dtype))

        G1 = sbt("G1", [128, 16384]); G1b = G1.ap().bitcast(BF16); G1f = G1.ap()
        G2 = sbt("G2", [128, 16384]); G2b = G2.ap().bitcast(BF16); G2f = G2.ap()
        G3 = sbt("G3", [128, 5120]);  G3b = G3.ap().bitcast(BF16); G3f = G3.ap()
        G4 = sbt("G4", [128, 8192]);  G4b = G4.ap().bitcast(BF16); G4f = G4.ap()

        ip128 = sbt("ip128", [128, 1])
        if128 = sbt("if128", [128, 128])
        identf = sbt("identf", [128, 128])
        identb = sbt("identb", [128, 128], BF16)
        lstrict = sbt("lstrict", [128, 128])
        dup64 = sbt("dup64", [64, 128])
        maps16 = sbt("maps16", [16, 1024], BF16)
        iexp = sbt("iexp", [128, 32])
        itokg = sbt("itokg", [128, 32])
        blockones = sbt("blockones", [128, 4])
        blockT = sbt("blockT", [4, 128])
        scr4 = sbt("scr4", [4, 128])
        ip4 = sbt("ip4", [4, 1])
        i128r4 = sbt("i128r4", [4, 128])
        i32m = sbt("i32m", [128, 4])
        ip16 = sbt("ip16", [16, 1])
        ip64 = sbt("ip64", [64, 1])
        m64 = sbt("m64", [64, 128])
        ones_col = sbt("ones_col", [128, 1])
        ones_row = sbt("ones_row", [1, 128])
        wqa_b = sbt("wqa_b", [128, 8 * H], BF16)
        wka_b = sbt("wka_b", [128, 8 * H], BF16)
        r2w_b = sbt("r2w_b", [128, 4 * E], BF16)
        bqa_s = sbt("bqa_s", [16, 1])
        bka_s = sbt("bka_s", [16, 1])
        bq_sb = sbt("bq_sb", [128, 8])
        bk_sb = sbt("bk_sb", [128, 8])
        bt_sb = sbt("bt_sb", [128, 8])
        r1b_sb = sbt("r1b_sb", [128, 4])
        mask8 = sbt("mask8", [8, 1])
        tokbase = sbt("tokbase", [128, 1])
        poolA_i = sbt("poolA_i", [16, 1], U32)
        poolB_i = sbt("poolB_i", [16, 1], U32)
        sgem_i = sbt("sgem_i", [128, 1], U32)
        idxl_i = sbt("idxl_i", [40, 1], U32)
        eoff_s = sbt("eoff_s", [1, 32])
        probs_s = sbt("probs_s", [128, NTC * E])
        mfull_s = sbt("mfull_s", [128, NTC * E])
        rts_s = sbt("rts_s", [128, NTC])
        t2_s = sbt("t2_s", [128, NTC])
        e12_s = sbt("e12_s", [128, NTC * 2])
        w01_s = sbt("w01_s", [128, NTC * 2])
        gsc_u = sbt("gsc_u", [128, NTC * 2], U32)
        threp = sbt("threp", [128, E])
        colA = sbt("colA", [128, 1]); colB = sbt("colB", [128, 1])
        colC = sbt("colC", [128, 1]); colD = sbt("colD", [128, 1])
        colE = sbt("colE", [128, 1]); colF = sbt("colF", [128, 1])
        colG = sbt("colG", [128, 1]); colH = sbt("colH", [128, 1])
        col_ep = sbt("col_ep", [128, 1]); col_en = sbt("col_en", [128, 1])
        bn6 = sbt("bn6", [128, 12])
        mv2 = sbt("mv2", [128, 2])
        bn6b = sbt("bn6b", [128, 12])
        mv2b = sbt("mv2b", [128, 2])
        cols2 = sbt("cols2", [128, 8])
        cols3 = sbt("cols3", [128, 4])
        stat8tb = sbt("stat8tb", [128, 8], BF16)
        cnt8 = sbt("cnt8", [8, 32])
        scv = sbt("scv", [128, 128])
        sl2 = sbt("sl2", [128, NTC * 2])
        oh16 = sbt("oh16", [16, 2048], BF16)
        idsT = sbt("idsT", [128, 40], U32)
        idxl4 = [sbt(f"idxl4_{k}", [10, 1], U32) for k in range(4)]
        poolCq_s = sbt("poolCq_s", [128, 8])
        poolCk_s = sbt("poolCk_s", [128, 8])
        stat8t = sbt("stat8t", [128, 8], BF16)

        PS = [pst(f"ps{i}", [128, 512]) for i in range(8)]

        imap = G3f[0:16, 0:1024]
        stats16 = G2b[0:16, 20480:24576]
        i16b = G1f[0:16, 0:2048]
        gwT_b = G3b[0:16, 5632:9728]
        wrow = G3f[0:1, 0:1280]

        block = ctx.enter_context(nc.Block())
        S = Sched(nc, ctx)

        def T(emit, deps=(), tag=""):
            return S.op("tensor", emit, deps, 1, tag)

        def V(emit, deps=(), tag=""):
            return S.op("vector", emit, deps, 1, tag)

        def SC(emit, deps=(), tag=""):
            return S.op("scalar", emit, deps, 1, tag)

        def GP(emit, deps=(), tag=""):
            return S.op("gpsimd", emit, deps, 1, tag)

        def QS(emit, deps=(), tag=""):
            return S.op("sync", emit, deps, 16, tag)

        def QG(emit, deps=(), tag=""):
            return S.op("gpsimd", emit, deps, 16, tag)

        def QV(emit, deps=(), tag=""):
            return S.op("scalar", emit, deps, 16, tag)

        def CC(kind, op_, ins, outs, deps=(), tag="cc"):
            def em(g):
                return g.collective_compute(kind, op_, ins=ins, outs=outs,
                                            replica_groups=[list(range(NCORE))])
            return S.op("gpsimd", em, deps, 1, tag)

        rB_map = nc.gpsimd.alloc_register()
        t_regs = GP(lambda g: g.reg_mov(rB_map, E * CAP), tag="regsetup")
        IK = dict(allow_small_or_imprecise_dtypes=True)
        t_iota = []
        t_iota.append(GP(lambda g: g.iota(ip128[:], pattern=[[0, 1]], base=0, channel_multiplier=1, **IK), tag="iota"))
        t_iota.append(GP(lambda g: g.iota(if128[:], pattern=[[1, 128]], base=0, channel_multiplier=0, **IK)))
        t_iota.append(GP(lambda g: g.iota(iexp[:], pattern=[[1, 32]], base=0, channel_multiplier=0, **IK)))
        t_iota.append(GP(lambda g: g.iota(itokg[:], pattern=[[128, 32]], base=0, channel_multiplier=1, **IK)))
        t_iota.append(GP(lambda g: g.iota(imap[:], pattern=[[2, 8], [1, 2], [0, 64]], base=0, channel_multiplier=0, **IK)))
        t_iota.append(GP(lambda g: g.iota(ip16[:], pattern=[[0, 1]], base=0, channel_multiplier=1, **IK)))
        t_iota.append(GP(lambda g: g.iota(ip4[:], pattern=[[0, 1]], base=0, channel_multiplier=1, **IK)))
        t_iota.append(GP(lambda g: g.iota(i128r4[:], pattern=[[1, 128]], base=0, channel_multiplier=0, **IK)))
        t_iota.append(GP(lambda g: g.iota(i32m[:], pattern=[[32, 4]], base=0, channel_multiplier=0, **IK)))
        t_iota.append(GP(lambda g: g.iota(ip64[:], pattern=[[0, 1]], base=0, channel_multiplier=1, **IK)))
        t_iota.append(GP(lambda g: g.iota(m64[:], pattern=[[0, 2], [1, 64]], base=0, channel_multiplier=0, **IK)))
        t_iota.append(GP(lambda g: g.memset(ones_col[:], 1.0)))
        t_iota.append(GP(lambda g: g.memset(ones_row[:], 1.0)))
        t_iota.append(GP(lambda g: g.memset(col_ep[:], 1e-6)))
        t_iota.append(GP(lambda g: g.memset(col_en[:], -1e-6)))
        t_iota.append(GP(lambda g: g.iota(i16b[:], pattern=[[1, 16], [0, 128]], base=0, channel_multiplier=0, **IK)))

        t_xload = QG(lambda g: g.dma_start(out=G2b[:, :].rearrange("p (c d) -> p c d", d=D), in_=xh.rearrange("(c p) d -> p c d", p=128)), tag="xload")
        t_wqld = QG(lambda g: g.dma_start(out=G4b[:, 0:8192].rearrange("p (k o) -> p k o", o=D), in_=wq.rearrange("(k p) o -> p k o", p=128)), tag="wq")
        t_wkld = QG(lambda g: g.dma_start(out=G4b[:, 8192:16384].rearrange("p (k o) -> p k o", o=D), in_=wk.rearrange("(k p) o -> p k o", p=128)), tag="wk")
        t_wqald = QG(lambda g: g.dma_start(out=wqa_b.ap().rearrange("p (k h) -> p k h", h=H), in_=wqa.rearrange("(k p) h -> p k h", p=128)))
        t_wkald = QG(lambda g: g.dma_start(out=wka_b.ap().rearrange("p (k h) -> p k h", h=H), in_=wka.rearrange("(k p) h -> p k h", p=128)))

        t_small = []
        for dst, src in [(bq_sb, bqc), (bk_sb, bkc), (bt_sb, btc),
                         (bqa_s, bqa), (bka_s, bka), (r1b_sb, r1bc),
                         (mask8, c_mask8), (tokbase, c_tokbase),
                         (poolA_i, c_poolA), (poolB_i, c_poolB),
                         (sgem_i, c_sgem), (idxl_i, c_idxl),
                         (eoff_s, c_eoff)]:
            t_small.append(QS(lambda s, dst=dst, src=src: s.dma_start(out=dst[:], in_=src[:, :]), tag="small"))
        t_idxl4 = [QS(lambda s, k=k: s.dma_start(out=idxl4[k][:], in_=c_idxl[10 * k:10 * (k + 1), :]), tag=f"idxl4{k}")
                   for k in range(4)]

        t_idf = V(lambda v: v.tensor_scalar(identf[:], if128[:], ip128[:, :], None, OP.is_equal), deps=t_iota[:2], tag="identf")
        t_idb = V(lambda v: v.tensor_scalar(identb[:], if128[:], ip128[:, :], None, OP.is_equal), deps=t_iota[:2])
        t_ls = V(lambda v: v.tensor_scalar(lstrict[:], if128[:], ip128[:, :], None, OP.is_gt), deps=t_iota[:2])
        t_dup = V(lambda v: v.tensor_scalar(dup64[:], m64[:], ip64[:, :], None, OP.is_equal), deps=t_iota)
        t_maps = V(lambda v: v.tensor_scalar(maps16[:], imap[:], ip16[:, :], None, OP.is_equal), deps=t_iota)
        b1 = V(lambda v: v.tensor_scalar(blockones[:], i32m[:], ip128[:, :], None, OP.is_le), deps=t_iota)
        b2 = V(lambda v: v.tensor_scalar(i32m[:], i32m[:], 32.0, ip128[:, :], OP.add, OP.is_gt), deps=t_iota + [b1])
        t_bo = V(lambda v: v.tensor_tensor(blockones[:], blockones[:], i32m[:], OP.mult), deps=[b1, b2])
        c1 = V(lambda v: v.tensor_scalar(ip4[:], ip4[:], 32.0, None, OP.mult), deps=t_iota)
        c2 = V(lambda v: v.tensor_scalar(blockT[:], i128r4[:], ip4[:, :], None, OP.is_ge), deps=[c1])
        c3 = V(lambda v: v.tensor_scalar(scr4[:], i128r4[:], ip4[:, :], 32.0, OP.subtract, OP.is_lt), deps=[c1])
        t_bt = V(lambda v: v.tensor_tensor(blockT[:], blockT[:], scr4[:], OP.mult), deps=[c2, c3])
        t_bqa = V(lambda v: v.tensor_scalar(bqa_s[:], bqa_s[:], SCALE_QK, None, OP.mult), deps=[t_small[3]])
        t_bka = V(lambda v: v.tensor_scalar(bka_s[:], bka_s[:], SCALE_QK, None, OP.mult), deps=[t_small[4]])
        t_itok = V(lambda v: v.tensor_scalar(itokg[:], itokg[:], tokbase[:, :], None, OP.add), deps=t_iota + [t_small[7]])
        t_oh16 = V(lambda v: v.tensor_scalar(oh16[:], i16b[:], ip16[:, :], None, OP.is_equal), deps=t_iota)

        t_xtm = QS(lambda s: s.dma_start(out=d_xtm.rearrange("(c p) d -> p c d", p=128), in_=G2b[:, :].rearrange("p (c d) -> p c d", d=D)), deps=[t_xload], tag="xtm")

        XT = [0, 4096]
        t_xtile = []
        t_projq = {}
        t_evq = {}
        t_projk = {}
        t_evk = {}

        for tt in range(NTT):
            deps = [t_xtm, t_maps, t_dup]
            if tt >= 2:
                deps = [t_projk[(tt - 2, NCHK - 1)]]
            last = None
            for ci in range(NCHK):
                last = QS(lambda s, tt=tt, ci=ci: s.dma_start(
                    out=G3b[:, XT[tt % 2] + ci * 512: XT[tt % 2] + (ci + 1) * 512],
                    in_=d_xtm[tt * 512:(tt + 1) * 512, ci * 128:(ci + 1) * 128],
                    transpose=True), deps=deps if ci == 0 else [], tag=f"xt{tt}")
            t_xtile.append(last)

            for mo in range(NCHK):
                bank = mo % 2
                depq = [t_xtile[tt], t_wqld]
                if (tt, mo) not in [(0, 0), (0, 1)]:
                    prev = (tt, mo - 2) if mo >= 2 else (tt - 1, NCHK - 2 + mo)
                    depq.append(t_evq[prev])
                tok = None
                for ki in range(NCHK):
                    tok = T(lambda t, tt=tt, mo=mo, ki=ki, bank=bank: t.matmul(
                        PS[bank][:, :],
                        G4b[:, ki * 1024 + mo * 128: ki * 1024 + (mo + 1) * 128],
                        G3b[:, XT[tt % 2] + ki * 512: XT[tt % 2] + (ki + 1) * 512],
                        start=(ki == 0), stop=(ki == NCHK - 1)),
                        deps=depq if ki == 0 else [], tag=f"mq{tt},{mo}")
                t_projq[(tt, mo)] = tok
                t_evq[(tt, mo)] = SC(lambda s, tt=tt, mo=mo, bank=bank: s.activation(
                    out=G1b[:, mo * 4096 + tt * 512: mo * 4096 + (tt + 1) * 512],
                    in_=PS[bank][:, :], func=AF.Identity, bias=bq_sb[:, mo:mo + 1]),
                    deps=[tok, t_xtm], tag=f"evq{tt},{mo}")

                bank2 = 2 + mo % 2
                depk = [t_xtile[tt], t_wkld, t_xtm]
                if (tt, mo) not in [(0, 0), (0, 1)]:
                    prev = (tt, mo - 2) if mo >= 2 else (tt - 1, NCHK - 2 + mo)
                    depk.append(t_evk[prev])
                tok = None
                for ki in range(NCHK):
                    tok = T(lambda t, tt=tt, mo=mo, ki=ki, bank2=bank2: t.matmul(
                        PS[bank2][:, :],
                        G4b[:, 8192 + ki * 1024 + mo * 128: 8192 + ki * 1024 + (mo + 1) * 128],
                        G3b[:, XT[tt % 2] + ki * 512: XT[tt % 2] + (ki + 1) * 512],
                        start=(ki == 0), stop=(ki == NCHK - 1)),
                        deps=depk if ki == 0 else [], tag=f"mk{tt},{mo}")
                t_projk[(tt, mo)] = tok
                t_evk[(tt, mo)] = SC(lambda s, tt=tt, mo=mo, bank2=bank2: s.activation(
                    out=G2b[:, mo * 4096 + tt * 512: mo * 4096 + (tt + 1) * 512],
                    in_=PS[bank2][:, :], func=AF.Identity, bias=bk_sb[:, mo:mo + 1]),
                    deps=[tok], tag=f"evk{tt},{mo}")

        proj_done = [t_evq[(tt, mo)] for tt in range(NTT) for mo in range(NCHK)] + \
                    [t_evk[(tt, mo)] for tt in range(NTT) for mo in range(NCHK)]

        qs = G4f[0:16, 0:4096]
        e_bf = G4b[0:16, 12288:16384]
        c_m = G4f[0:16, 4100:4101]
        c_negm = G4f[0:16, 4101:4102]
        c_den = G4f[0:16, 4102:4103]
        npart = G4f[:, 4104:4112]
        n_all = G4f[:, 4112:4120]
        pool2 = G4f[:, 4120:4136]
        poolCq = poolCq_s.ap()
        poolCk = poolCk_s.ap()
        nq_tr = G4f[0:8, 4160:4288]
        pldA = G4f[0:16, 4288:4360]
        pldB = G4f[0:16, 4360:4432]
        pld = G4f[0:16, 4432:4504]
        pooled_hm = G4f[0:16, 4504:4568]
        c_eA = G4f[0:16, 4568:4569]
        c_eB = G4f[0:16, 4569:4570]
        c_M2 = G4f[0:16, 4570:4571]
        scrP = G4f[0:16, 4576:4640]
        pooledT_s = G4f[0:64, 4640:4656]

        def pool_phase(which, wab, bias_s, src_b, poolC, pre_deps):
            t_sc = {}
            t_ev = {}
            for tt in range(NTT):
                deps = list(pre_deps) + [t_wqald, t_wkald, t_bqa, t_bka]
                if tt >= 1:
                    deps.append(t_ev[tt - 1])
                tok = None
                for ki in range(NCHK):
                    tok = T(lambda t, tt=tt, ki=ki: t.matmul(
                        PS[4][0:16, :],
                        wab[:, ki * H:(ki + 1) * H],
                        src_b[:, ki * 4096 + tt * 512: ki * 4096 + (tt + 1) * 512],
                        start=(ki == 0), stop=(ki == NCHK - 1)),
                        deps=deps if ki == 0 else [], tag=f"sc{which}{tt}")
                t_sc[tt] = tok
                t_ev[tt] = SC(lambda s, tt=tt: s.activation(
                    out=qs[:, tt * 512:(tt + 1) * 512], in_=PS[4][0:16, :],
                    func=AF.Identity, bias=bias_s[:, :], scale=SCALE_QK),
                    deps=[tok], tag=f"sev{which}{tt}")

            ev_all = [t_ev[tt] for tt in range(NTT)]
            t_max = V(lambda v: v.tensor_reduce(c_m, qs, AX.X, OP.max), deps=ev_all, tag="smax")
            t_neg = V(lambda v: v.tensor_scalar(c_negm, c_m, -1.0, None, OP.mult), deps=[t_max])
            t_exp = SC(lambda s: s.activation(out=qs, in_=qs, func=AF.Exp,
                                              bias=c_negm[:, :], accum_out=c_den),
                       deps=[t_neg], tag="sexp")
            t_ebf = V(lambda v: v.tensor_copy(e_bf, qs), deps=[t_exp], tag="ebf")

            t_stt = {}
            t_nred = []
            for ci in range(NCHK):
                for tt in range(NTT):
                    bank = 5 + (tt % 2)
                    depb = [t_ebf, t_maps]
                    idx = ci * NTT + tt
                    if idx >= 2:
                        pi = idx - 2
                        depb.append(t_stt[(pi // NTT, pi % NTT)])
                    t_bc = T(lambda t, ci=ci, tt=tt, bank=bank: t.matmul(
                        PS[bank][:, :], maps16[:, ci * 128:(ci + 1) * 128],
                        e_bf[:, tt * 512:(tt + 1) * 512], start=True, stop=True),
                        deps=depb, tag=f"ebc{ci},{tt}")
                    t_stt[(ci, tt)] = V(lambda v, ci=ci, tt=tt, bank=bank: v.scalar_tensor_tensor(
                        out=G3b[:, 8192 + (tt % 4) * 512:8192 + (tt % 4 + 1) * 512],
                        in0=src_b[:, ci * 4096 + tt * 512: ci * 4096 + (tt + 1) * 512],
                        scalar=1.0, in1=PS[bank][:, :], op0=OP.mult, op1=OP.mult,
                        accum_out=npart[:, tt:tt + 1]),
                        deps=[t_bc], tag=f"stt{ci},{tt}")
                t_nred.append(V(lambda v, ci=ci: v.tensor_reduce(
                    n_all[:, ci:ci + 1], npart, AX.X, OP.add),
                    deps=[t_stt[(ci, tt)] for tt in range(NTT)], tag=f"nred{ci}"))

            t_ntr = T(lambda t: t.transpose(PS[7][0:8, 0:128], n_all, identf[:]),
                      deps=t_nred + [t_idf], tag="ntr")
            t_ntc = V(lambda v: v.tensor_copy(nq_tr, PS[7][0:8, 0:128]), deps=[t_ntr])
            t_nst = QS(lambda s: s.dma_start(out=d_nq[:, :], in_=nq_tr), deps=[t_ntc])
            t_nld = QS(lambda s: s.dma_start(
                out=pooled_hm, in_=d_nq.rearrange("a b -> (a b)").rearrange("(h d) -> h d", h=16)),
                deps=[t_nst], tag="nld")

            p1 = V(lambda v: v.tensor_copy(pld[:, 0:64], pooled_hm), deps=[t_nld])
            p2 = V(lambda v: v.tensor_copy(pld[:, 64:65], c_m), deps=[t_max, p1])
            p3 = V(lambda v: v.tensor_copy(pld[:, 65:66], c_den), deps=[t_exp, p2])
            p4 = V(lambda v: v.memset(pld[:, 66:72], 0.0), deps=[p3])
            t_pst = QS(lambda s: s.dma_start(out=d_pin[which][:, :], in_=pld), deps=[p1, p2, p3, p4])
            t_cc = CC("AllGather", OP.bypass, [d_pin[which][:, :]], [d_pout[which][:, :]],
                      deps=[t_pst], tag=f"agpool{which}")
            gA = QG(lambda g: g.indirect_dma_start(
                pldA, None, d_pout[which][:, :],
                bass.IndirectOffsetOnAxis(ap=poolA_i[:, :], axis=0)), deps=[t_cc])
            gB = QG(lambda g: g.indirect_dma_start(
                pldB, None, d_pout[which][:, :],
                bass.IndirectOffsetOnAxis(ap=poolB_i[:, :], axis=0)), deps=[t_cc])

            m1 = V(lambda v: v.tensor_tensor(c_M2, pldA[:, 64:65], pldB[:, 64:65], OP.max), deps=[gA, gB])
            m2 = V(lambda v: v.tensor_scalar(c_M2, c_M2, -1.0, None, OP.mult), deps=[m1])
            e1 = SC(lambda s: s.activation(out=c_eA, in_=pldA[:, 64:65], func=AF.Exp, bias=c_M2[:, :]), deps=[m2])
            e2 = SC(lambda s: s.activation(out=c_eB, in_=pldB[:, 64:65], func=AF.Exp, bias=c_M2[:, :]), deps=[m2, e1])
            d1 = V(lambda v: v.tensor_scalar(c_den, pldA[:, 65:66], c_eA[:, :], None, OP.mult), deps=[e1])
            d2 = V(lambda v: v.scalar_tensor_tensor(c_den, pldB[:, 65:66], c_eB[:, :], c_den, OP.mult, OP.add), deps=[d1, e2])
            n1 = V(lambda v: v.tensor_scalar(scrP, pldA[:, 0:64], c_eA[:, :], None, OP.mult), deps=[e1, d2])
            n2 = V(lambda v: v.scalar_tensor_tensor(pooled_hm, pldB[:, 0:64], c_eB[:, :], scrP, OP.mult, OP.add), deps=[n1])
            r1_ = V(lambda v: v.reciprocal(c_den, c_den), deps=[d2, n2])
            pm = V(lambda v: v.tensor_scalar(pooled_hm, pooled_hm, c_den[:, :], None, OP.mult), deps=[r1_], tag="pooled")

            tr1 = T(lambda t: t.transpose(PS[7][0:64, 0:16], pooled_hm, identf[0:16, 0:16]), deps=[pm], tag="ptr")
            tc1 = V(lambda v: v.tensor_copy(pooledT_s, PS[7][0:64, 0:16]), deps=[tr1])
            tr2 = T(lambda t: t.matmul(PS[7][:, 0:16], dup64[:], pooledT_s, start=True, stop=True), deps=[tc1, t_dup], tag="pdup")
            tc2 = V(lambda v: v.tensor_copy(pool2, PS[7][:, 0:16]), deps=[tr2])
            f1 = V(lambda v: v.tensor_copy(poolC[0:64, :], pool2[0:64, 0:16:2]), deps=[tc2])
            f2 = V(lambda v: v.tensor_copy(poolC[64:128, :], pool2[64:128, 1:16:2]), deps=[tc2, f1], tag="poolC")
            return f2

        t_qpool = pool_phase(0, wqa_b, bqa_s, G1b, poolCq, proj_done)

        t_mqk = []
        for ci in range(NCHK):
            t_mqk.append(V(lambda v, ci=ci: v.tensor_scalar(
                G2b[:, ci * 4096:(ci + 1) * 4096], G2b[:, ci * 4096:(ci + 1) * 4096],
                poolCq[:, ci:ci + 1], None, OP.mult), deps=[t_qpool], tag=f"mqk{ci}"))

        t_kpool = pool_phase(1, wka_b, bka_s, G2b, poolCk, t_mqk)

        t_wv = []
        for ci in range(NCHK):
            t_wv.append(V(lambda v, ci=ci: v.tensor_scalar(
                G2b[:, ci * 4096:(ci + 1) * 4096], G1b[:, ci * 4096:(ci + 1) * 4096],
                poolCk[:, ci:ci + 1], None, OP.mult), deps=[t_kpool], tag=f"wv{ci}"))

        t_wtld = QG(lambda g: g.dma_start(out=G4b[:, 0:8192].rearrange("p (k o) -> p k o", o=D),
                                          in_=wt.rearrange("(k p) o -> p k o", p=128)),
                    deps=[t_kpool], tag="wtld")

        t_yev = {}
        for tt in range(NTT):
            for mo in range(NCHK):
                bank = mo % 2
                deps = [t_wtld] + t_wv
                if (tt, mo) not in [(0, 0), (0, 1)]:
                    prev = (tt, mo - 2) if mo >= 2 else (tt - 1, NCHK - 2 + mo)
                    deps.append(t_yev[prev])
                tok = None
                for ki in range(NCHK):
                    tok = T(lambda t, tt=tt, mo=mo, ki=ki, bank=bank: t.matmul(
                        PS[bank][:, :],
                        G4b[:, ki * 1024 + mo * 128: ki * 1024 + (mo + 1) * 128],
                        G2b[:, ki * 4096 + tt * 512: ki * 4096 + (tt + 1) * 512],
                        start=(ki == 0), stop=(ki == NCHK - 1)),
                        deps=deps if ki == 0 else [], tag=f"wt{tt},{mo}")
                t_yev[(tt, mo)] = V(lambda v, tt=tt, mo=mo, bank=bank: v.scalar_tensor_tensor(
                    out=G1b[:, mo * 4096 + tt * 512: mo * 4096 + (tt + 1) * 512],
                    in0=PS[bank][:, :], scalar=bt_sb[:, mo:mo + 1],
                    in1=G1b[:, mo * 4096 + tt * 512: mo * 4096 + (tt + 1) * 512],
                    op0=OP.add, op1=OP.add), deps=[tok], tag=f"yev{tt},{mo}")

        t_ypst = QS(lambda s: s.dma_start(
            out=d_ypre.rearrange("(c p) t -> p c t", p=128), in_=G1b[:, :].rearrange("p (c t) -> p c t", t=TCORE)),
            deps=[t_yev[(tt, mo)] for tt in range(NTT) for mo in range(NCHK)], tag="ypst")

        YPo = [0, 2048]; XCo = [4096, 6144]; X1o = [8192, 9216]
        t_z16 = V(lambda v: v.memset(stats16[:, :], 0.0),
                  deps=[t_yev[(NTT - 1, NCHK - 1)]], tag="z16")
        colP = [[colA, colB, colC, colD, colE, colF, colG, colH],
                [cols2[:, i:i + 1] for i in range(8)]]
        bn6P_ = [bn6, bn6b]; mv2P_ = [mv2, mv2b]; st8P = [stat8t, stat8tb]
        t_lnst = []
        t_x1st = []
        t_sgn = []
        lhist = {}

        def ln_chunk(tc):
            pb = tc % 2
            yp = G3b[:, YPo[pb]:YPo[pb] + 1024]
            xc = G3b[:, XCo[pb]:XCo[pb] + 1024]
            x1t = G3b[:, X1o[pb]:X1o[pb] + 1024]
            cA, cB, cC, cD, cE, cF, cG, cH = colP[pb]
            bnb, mvb, s8t = bn6P_[pb], mv2P_[pb], st8P[pb]
            pv = lhist.get(tc - 2, {})
            depl = [t_ypst]
            if tc >= 2:
                depl.append(t_sgn[tc - 2])
            l1 = QS(lambda s, tc=tc, yp=yp: s.dma_start(
                out=yp, in_=d_ypre[:, tc * 128:(tc + 1) * 128], transpose=True),
                deps=depl, tag=f"ypl{tc}")
            l2 = QG(lambda s, tc=tc, xc=xc: s.dma_start(
                out=xc, in_=d_xtm[tc * 128:(tc + 1) * 128, :]), deps=depl, tag=f"xcl{tc}")
            a1 = V(lambda v, yp=yp, xc=xc: v.tensor_tensor(yp, yp, xc, OP.add), deps=[l1, l2], tag=f"lnadd{tc}")
            a2 = V(lambda v, yp=yp, bnb=bnb: v.bn_stats(bnb[:, 0:6], yp[:, 0:512]), deps=[a1])
            a3 = V(lambda v, yp=yp, bnb=bnb: v.bn_stats(bnb[:, 6:12], yp[:, 512:1024]), deps=[a2])
            a4 = V(lambda v, bnb=bnb, mvb=mvb: v.bn_aggr(mvb[:], bnb[:]), deps=[a3], tag=f"bn{tc}")
            a5 = V(lambda v, yp=yp, cC=cC: v.tensor_reduce(cC[:], yp, AX.X, OP.min), deps=[a4])
            a6 = V(lambda v, yp=yp, cD=cD: v.tensor_reduce(cD[:], yp, AX.X, OP.max), deps=[a5])
            a7 = V(lambda v, cA=cA, mvb=mvb: v.tensor_scalar(cA[:], mvb[:, 1:2], EPS, None, OP.add),
                   deps=[a6] + ([pv["s8"]] if pv else []))
            s1 = SC(lambda s, cA=cA, cB=cB: s.activation(out=cB[:], in_=cA[:], func=AF.Ln), deps=[a7], tag=f"ln{tc}")
            s2 = SC(lambda s, cB=cB: s.activation(out=cB[:], in_=cB[:], func=AF.Exp, scale=-0.5), deps=[s1])
            depx1 = [s2]
            if tc >= 2:
                depx1 += [t_x1st[tc - 2], t_sgn[tc - 2]]
            a8 = V(lambda v, yp=yp, x1t=x1t, cB=cB, mvb=mvb: v.tensor_scalar(
                x1t, yp, mvb[:, 0:1], cB[:, :], OP.subtract, OP.mult), deps=depx1, tag=f"x1{tc}")
            a9a = V(lambda v, cA=cA, cE=cE: v.reciprocal(cE[:], cA[:]), deps=[a8])
            a9 = V(lambda v, cE=cE, mvb=mvb: v.tensor_tensor(cE[:], cE[:], mvb[:, 1:2], OP.mult), deps=[a9a])
            s3 = SC(lambda s, cE=cE, cF=cF: s.activation(out=cF[:], in_=cE[:], func=AF.Ln, scale=1024.0 / 1023.0), deps=[a9])
            s4 = SC(lambda s, cF=cF: s.activation(out=cF[:], in_=cF[:], func=AF.Exp, scale=0.5), deps=[s3], tag=f"std{tc}")
            s5 = SC(lambda s, cE=cE, cG=cG: s.activation(out=cG[:], in_=cE[:], func=AF.Ln, scale=1024.0), deps=[s4])
            s6 = SC(lambda s, cG=cG: s.activation(out=cG[:], in_=cG[:], func=AF.Exp, scale=0.5), deps=[s5])
            a10 = V(lambda v, cC=cC, cB=cB, mvb=mvb: v.tensor_scalar(cC[:], cC[:], mvb[:, 0:1], cB[:, :], OP.subtract, OP.mult), deps=[s2, a8])
            a11 = V(lambda v, cD=cD, cB=cB, mvb=mvb: v.tensor_scalar(cD[:], cD[:], mvb[:, 0:1], cB[:, :], OP.subtract, OP.mult), deps=[a10])
            s7 = SC(lambda s, x1t=x1t, xc=xc, cH=cH: s.activation(out=xc, in_=x1t, func=AF.Sign,
                                                    bias=col_ep[:, :], accum_out=cH[:]), deps=[a8, s6], tag=f"sp1{tc}")
            s8 = SC(lambda s, x1t=x1t, xc=xc, cA=cA: s.activation(out=xc, in_=x1t, func=AF.Sign,
                                                    bias=col_en[:, :], accum_out=cA[:]), deps=[s7], tag=f"sp2{tc}")
            t_sgn.append(s8)
            a12 = V(lambda v, cH=cH, cA=cA: v.tensor_scalar(cH[:], cH[:], cA[:, :], 1.0 / 2048.0, OP.subtract, OP.mult), deps=[s8])
            b0 = V(lambda v, s8t=s8t: v.memset(s8t[:, 0:1], 0.0), deps=[a12])
            b1_ = V(lambda v, s8t=s8t, cF=cF: v.tensor_copy(s8t[:, 1:2], cF[:]), deps=[s4, b0])
            b2_ = V(lambda v, s8t=s8t, cC=cC: v.tensor_copy(s8t[:, 2:3], cC[:]), deps=[b1_])
            b3_ = V(lambda v, s8t=s8t, cD=cD: v.tensor_copy(s8t[:, 3:4], cD[:]), deps=[b2_])
            b4_ = V(lambda v, s8t=s8t, cG=cG: v.tensor_copy(s8t[:, 4:5], cG[:]), deps=[s6, b3_])
            b5_ = V(lambda v, s8t=s8t, cH=cH: v.tensor_copy(s8t[:, 5:6], cH[:]), deps=[b4_])
            b6_ = V(lambda v, s8t=s8t: v.memset(s8t[:, 6:8], 0.0), deps=[b5_])
            tr = T(lambda t, s8t=s8t, pb=pb: t.transpose(
                PS[7].bitcast(BF16)[0:8, pb * 128:(pb + 1) * 128], s8t[:], identb[:]),
                deps=[b6_, t_idb] + ([pv["te"]] if pv else []), tag=f"sttr{tc}")
            te = V(lambda v, tc=tc, pb=pb: v.tensor_copy(stats16[0:8, tc * 128:(tc + 1) * 128],
                                                  PS[7].bitcast(BF16)[0:8, pb * 128:(pb + 1) * 128]), deps=[tr, t_z16])
            t_lnst.append(te)
            st = QS(lambda s, tc=tc, x1t=x1t: s.dma_start(
                out=d_x1tm[tc * 128:(tc + 1) * 128, :], in_=x1t), deps=[a8, a12], tag=f"x1st{tc}")
            t_x1st.append(st)
            lhist[tc] = dict(s8=s8, te=te)

        t_r1w = QG(lambda g: g.dma_start(
            out=G2b[:, 0:4096].rearrange("p (k m) -> p k m", m=512),
            in_=r1wt[0:1024, :].rearrange("(k p) m -> p k m", p=128)),
            deps=[t_ypst], tag="r1w")
        t_r1wt2 = QG(lambda g: g.dma_start(
            out=G2b[0:16, 4096:4608], in_=r1wt[1024:1040, :]), deps=[t_r1w], tag="r1wt2")
        t_r2w = QG(lambda g: g.dma_start(
            out=r2w_b.ap().rearrange("p (k e) -> p k e", e=E),
            in_=r2wt.rearrange("(k p) e -> p k e", p=128)), deps=[], tag="r2w")
        t_x1fm_t = {}

        def x1fm_tile(tt):
            for ci in range(NCHK):
                t_x1fm_t[(ci, tt)] = QV(lambda s, ci=ci, tt=tt: s.dma_start(
                    out=G1b[:, ci * 4096 + tt * 512: ci * 4096 + (tt + 1) * 512],
                    in_=d_x1tm[tt * 512:(tt + 1) * 512, ci * 128:(ci + 1) * 128], transpose=True),
                    deps=([t_ypst] + [t_x1st[4 * tt + j] for j in range(4)]) if ci == 0 else [],
                    tag=f"x1fm{ci},{tt}")

        t_hev = {}
        hstate = dict(n=0, order=[])

        def h_tile(tt):
            for mo2 in range(4):
                bank = hstate["n"] % 2
                deps = [t_x1fm_t[(ki2, tt)] for ki2 in range(NCHK)] + \
                    [t_r1w, t_r1wt2, t_z16] + [t_lnst[4 * tt + j] for j in range(4)]
                if hstate["n"] >= 2:
                    deps.append(t_hev[hstate["order"][hstate["n"] - 2]])
                tok = None
                for ki in range(NCHK):
                    tok = T(lambda t, mo2=mo2, tt=tt, ki=ki, bank=bank: t.matmul(
                        PS[bank][:, :],
                        G2b[:, ki * 512 + mo2 * 128: ki * 512 + (mo2 + 1) * 128],
                        G1b[:, ki * 4096 + tt * 512: ki * 4096 + (tt + 1) * 512],
                        start=(ki == 0), stop=False),
                        deps=deps if ki == 0 else [], tag=f"r1m{mo2},{tt}")
                tok = T(lambda t, mo2=mo2, tt=tt, bank=bank: t.matmul(
                    PS[bank][:, :],
                    G2b[0:16, 4096 + mo2 * 128: 4096 + (mo2 + 1) * 128],
                    stats16[0:16, tt * 512:(tt + 1) * 512],
                    start=False, stop=True), tag=f"r1s{mo2},{tt}")
                t_hev[(mo2, tt)] = SC(lambda s, mo2=mo2, tt=tt, bank=bank: s.activation(
                    out=G4b[:, mo2 * 4096 + tt * 512: mo2 * 4096 + (tt + 1) * 512],
                    in_=PS[bank][:, :], func=AF.Gelu, bias=r1b_sb[:, mo2:mo2 + 1]),
                    deps=[tok], tag=f"hev{mo2},{tt}")
                hstate["order"].append((mo2, tt))
                hstate["n"] += 1

        lgt = G2f[0:32, 2304:6400]
        t_lgt = {}

        def lgt_tile(tt):
            deps = [t_r2w] + [t_hev[(k2, tt)] for k2 in range(4)]
            if tt >= 1:
                deps.append(t_lgt[tt - 1])
            tok = None
            for k2 in range(4):
                tok = T(lambda t, tt=tt, k2=k2: t.matmul(
                    PS[2][0:32, :],
                    r2w_b[:, k2 * E:(k2 + 1) * E],
                    G4b[:, k2 * 4096 + tt * 512: k2 * 4096 + (tt + 1) * 512],
                    start=(k2 == 0), stop=(k2 == 3)),
                    deps=deps if k2 == 0 else [], tag=f"r2m{tt}")
            t_lgt[tt] = V(lambda v, tt=tt: v.tensor_copy(
                lgt[:, tt * 512:(tt + 1) * 512], PS[2][0:32, :]), deps=[tok], tag=f"lev{tt}")

        sem_b = G2f[0:32, 12288:16384]
        ptm = G3f[:, 512:544]
        stile = G3f[:, 544:576]
        rtile = G3f[:, 576:608]
        mx8c = G3f[:, 608:616]
        ix8c = G3f[:, 616:624].bitcast(U32)[:, 0:8]
        sm_a = cols3[:, 0:1]
        sm_b = cols3[:, 1:2]
        sm_c = cols3[:, 2:3]
        t_sem = []
        t_probs = []
        pstate = dict(prev=None)

        def probs_chunk(tc):
            prev = pstate["prev"]
            deps = [t_lgt[tc // 4]]
            if prev is not None:
                deps.append(prev)
            tr = T(lambda t, tc=tc: t.transpose(
                PS[3][:, 0:32], lgt[:, tc * 128:(tc + 1) * 128], identf[0:32, 0:32]),
                deps=deps + [t_idf], tag=f"ltr{tc}")
            ev = V(lambda v: v.tensor_copy(ptm, PS[3][:, 0:32]), deps=[tr])
            r1_ = V(lambda v: v.tensor_reduce(sm_a[:], ptm, AX.X, OP.max), deps=[ev])
            r2_ = V(lambda v: v.tensor_scalar(sm_a[:], sm_a[:], -1.0, None, OP.mult), deps=[r1_])
            ex = SC(lambda s: s.activation(out=ptm, in_=ptm, func=AF.Exp,
                                           bias=sm_a[:, :], accum_out=sm_b[:]), deps=[r2_], tag=f"pexp{tc}")
            rc = V(lambda v: v.reciprocal(sm_b[:], sm_b[:]), deps=[ex])
            pn = V(lambda v, tc=tc: v.tensor_scalar(
                probs_s[:, tc * E:(tc + 1) * E], ptm, sm_b[:, :], None, OP.mult),
                deps=[rc], tag=f"probs{tc}")
            t_probs.append(pn)
            m8 = V(lambda v, tc=tc: v.max(mx8c, probs_s[:, tc * E:(tc + 1) * E]), deps=[pn])
            c1_ = V(lambda v, tc=tc: v.tensor_copy(t2_s[:, tc:tc + 1], mx8c[:, 1:2]), deps=[m8])
            c2_ = V(lambda v, tc=tc: v.tensor_tensor(sm_c[:], mx8c[:, 0:1], mx8c[:, 1:2], OP.add), deps=[m8, c1_])
            c3_ = V(lambda v, tc=tc: v.reciprocal(rts_s[:, tc:tc + 1], sm_c[:]), deps=[c2_])
            i8 = V(lambda v, tc=tc: v.max_index(ix8c, mx8c, probs_s[:, tc * E:(tc + 1) * E]), deps=[m8, c3_])
            e1c = V(lambda v, tc=tc: v.tensor_copy(e12_s[:, tc * 2:tc * 2 + 1], ix8c[:, 0:1]), deps=[i8])
            e2c = V(lambda v, tc=tc: v.tensor_copy(e12_s[:, tc * 2 + 1:tc * 2 + 2], ix8c[:, 1:2]), deps=[i8, e1c])
            rr = V(lambda v, tc=tc: v.tensor_scalar(rtile, probs_s[:, tc * E:(tc + 1) * E],
                                                    t2_s[:, tc:tc + 1], None, OP.is_ge), deps=[c1_, pn, e2c])
            s1_ = V(lambda v, tc=tc: v.scalar_tensor_tensor(
                stile, probs_s[:, tc * E:(tc + 1) * E], 1.0, rtile, OP.add, OP.mult), deps=[rr])
            s2_ = V(lambda v: v.tensor_scalar(stile, stile, 1.0, None, OP.subtract), deps=[s1_])
            str_ = T(lambda t, tc=tc: t.transpose(PS[3][0:32, 0:128], stile, identf[:]),
                     deps=[s2_, tr], tag=f"str{tc}")
            sev = V(lambda v, tc=tc: v.tensor_copy(sem_b[:, tc * 128:(tc + 1) * 128],
                                                   PS[3][0:32, 0:128]), deps=[str_], tag=f"sev{tc}")
            t_sem.append(sev)
            pstate["prev"] = sev

        for tt in range(NTT):
            for j in range(4):
                ln_chunk(4 * tt + j)
            x1fm_tile(tt)
            h_tile(tt)
            lgt_tile(tt)
            for j in range(4):
                probs_chunk(4 * tt + j)
        ln_done = t_x1st + [t_lnst[-1]]
        t_agX = CC("AllGather", OP.bypass, [d_x1tm[:, :]], [d_xg[:, :]],
                   deps=list(t_x1st), tag="agX")
        t_x1fm = [t_x1fm_t[(ci, NTT - 1)] for ci in range(NCHK)]
        t_dbg_x1fm = []
        if "x1fm" in dbg:
            t_dbg_x1fm = [QS(lambda s: s.dma_start(out=dbg["x1fm"][:, :], in_=G1b[:, :]),
                             deps=list(t_x1fm), tag="dbgx1fm")]
        t_dbg_stats2 = []
        if "stats2" in dbg:
            t_dbg_stats2 = [QS(lambda s: s.dma_start(out=dbg["stats2"][:, :], in_=stats16[:, :]),
                               deps=list(t_lnst), tag="dbgstats2")]

        t_dbg_hfm = []
        if "hfm" in dbg:
            t_dbg_hfm = [QS(lambda s: s.dma_start(out=dbg["hfm"][:, :], in_=G4b[:, 0:16384]),
                            deps=[t_hev[(m2, t2)] for m2 in range(4) for t2 in range(NTT)],
                            tag="dbghfm")]
        t_dbg_lgt = []
        if "lgtt" in dbg:
            t_dbg_lgt = [QS(lambda s: s.dma_start(out=dbg["lgtt"][:, :], in_=lgt),
                            deps=[t_lgt[t2] for t2 in range(NTT)], tag="dbglgtt")]
        t_sst = QS(lambda s: s.dma_start(out=d_sem_[:, :], in_=sem_b), deps=t_sem, tag="sst")
        t_agS = CC("AllGather", OP.bypass, [d_sem_[:, :]], [d_sgem[:, :]], deps=[t_sst], tag="agS")

        svals = G3f[:, 0:1024]
        t_sv = QG(lambda g: g.indirect_dma_start(
            svals, None, d_sgem.rearrange("r (b j) -> (r b) j", j=1024),
            bass.IndirectOffsetOnAxis(ap=sgem_i[:, :], axis=0)), deps=[t_agS], tag="svgather")

        bi_lo, bi_mid = colB, colA
        z3 = V(lambda v: v.memset(bi_mid[:], 0.0), deps=[t_sv], tag="bi0")
        prev = [z3]
        cntscr = G3f[:, 2048:3072]
        for it in range(BISECT):
            q1 = V(lambda v: v.tensor_scalar(cntscr, svals, bi_mid[:, :], 0.0,
                                             OP.is_ge, OP.add, accum_out=colD[:]),
                   deps=prev, tag=f"bi{it}")
            q2 = T(lambda t: t.matmul(PS[6][0:4, 0:1], blockones[:], colD[:],
                                      start=True, stop=True), deps=[q1, t_bo], tag=f"bim{it}")
            q3 = V(lambda v: v.tensor_copy(scr4[:, 0:1], PS[6][0:4, 0:1]), deps=[q2])
            q4 = T(lambda t: t.matmul(PS[6][:, 0:1], blockT[:], scr4[:, 0:1],
                                      start=True, stop=True), deps=[q3, t_bt], tag=f"bib{it}")
            q5 = V(lambda v: v.tensor_scalar(colF[:], PS[6][:, 0:1], float(CAP), None, OP.is_ge), deps=[q4])
            w_it = 2.0 * (0.5 ** it)
            u1 = V(lambda v, w_it=w_it: v.tensor_scalar(colG[:], colF[:], 0.5, w_it, OP.subtract, OP.mult), deps=[q5])
            u2 = V(lambda v: v.tensor_tensor(bi_mid[:], bi_mid[:], colG[:], OP.add), deps=[u1])
            prev = [u2]
        w_fin = 2.0 * (0.5 ** (BISECT - 1))
        tl_ = V(lambda v: v.tensor_scalar(bi_lo[:], bi_mid[:], w_fin, None, OP.subtract), deps=prev, tag="bifin")
        tq = T(lambda t: t.matmul(PS[6][0:4, 0:1], blockones[:], bi_lo[:],
                                  start=True, stop=True), deps=[tl_], tag="thmm")
        th8 = scr4[0:4, 1:9]
        tz = V(lambda v: v.memset(scr4[:, 1:9], 0.0), deps=[tl_])
        tv = V(lambda v: v.tensor_scalar(scr4[0:4, 1:2], PS[6][0:4, 0:1], 1.0 / 32.0, None, OP.mult), deps=[tq, tz], tag="theval")
        t_thst = QS(lambda s: s.dma_start(out=d_thin[:, :], in_=th8), deps=[tv], tag="thst")
        t_agT = CC("AllGather", OP.bypass, [d_thin[:, :]], [d_thout[:, :]], deps=[t_thst], tag="agT")
        t_thld = QS(lambda s: s.dma_start(out=G3f[0:32, 3072:3080], in_=d_thout[:, :]), deps=[t_agT], tag="thld")
        th32 = G3f[0:32, 3072:3073]
        tt1 = T(lambda t: t.transpose(PS[3][0:1, 0:32], th32, identf[0:32, 0:32]), deps=[t_thld], tag="thtr")
        tt2 = V(lambda v: v.tensor_copy(G3f[0:1, 3080:3112], PS[3][0:1, 0:32]), deps=[tt1])
        tt3 = T(lambda t: t.matmul(PS[3][:, 0:32], ones_row[:], G3f[0:1, 3080:3112],
                                   start=True, stop=True), deps=[tt2], tag="thbc")
        tt4 = V(lambda v: v.tensor_copy(threp[:], PS[3][:, 0:32]), deps=[tt3], tag="threp")

        selt = G3f[:, 1024:1056]
        rloc = G3f[:, 1056:1088]
        ohot = G3f[:, 1088:1120]
        base_row = G3f[0:1, 3112:3144]
        zb = V(lambda v: v.memset(base_row, 0.0), deps=[tt4], tag="zbase")
        t_cnt_mm = None
        rl_cols = G3f[:, 3200:3264]
        sl_cols = sl2.ap()
        prev = zb
        for tc in range(NTC):
            p0 = V(lambda v, tc=tc: v.tensor_tensor(selt, probs_s[:, tc * E:(tc + 1) * E],
                                                    threp[:], OP.is_ge), deps=[prev], tag=f"sel{tc}")
            p1 = V(lambda v, tc=tc: v.tensor_scalar(rloc, probs_s[:, tc * E:(tc + 1) * E],
                                                    t2_s[:, tc:tc + 1], None, OP.is_ge), deps=[p0])
            p2 = V(lambda v: v.tensor_tensor(selt, selt, rloc, OP.mult), deps=[p1], tag=f"selm{tc}")
            p3 = V(lambda v, tc=tc: v.scalar_tensor_tensor(
                mfull_s[:, tc * E:(tc + 1) * E], probs_s[:, tc * E:(tc + 1) * E],
                rts_s[:, tc:tc + 1], selt, OP.mult, OP.mult), deps=[p2], tag=f"mf{tc}")
            t_cnt_mm = T(lambda t, tc=tc: t.matmul(PS[4][0:1, 0:32], ones_col[:], selt,
                                                   start=(tc == 0), stop=(tc == NTC - 1)),
                         deps=[p3], tag=f"cnt{tc}")
            r1m = T(lambda t: t.matmul(PS[5][:, 0:32], lstrict[:], selt,
                                       start=True, stop=False), deps=[p3, t_ls], tag=f"rk{tc}")
            r2m = T(lambda t: t.matmul(PS[5][:, 0:32], ones_row[:], base_row,
                                       start=False, stop=True), deps=[r1m], tag=f"rkb{tc}")
            b1m = T(lambda t: t.matmul(PS[6][0:1, 0:32], ones_col[:], selt,
                                       start=True, stop=True), deps=[p3, r2m], tag=f"cs{tc}")
            b2v = V(lambda v: v.tensor_tensor(base_row, base_row, PS[6][0:1, 0:32], OP.add), deps=[b1m], tag=f"bup{tc}")
            exts = []
            for par in range(2):
                o1 = V(lambda v, tc=tc, par=par: v.tensor_scalar(
                    ohot, iexp[:], e12_s[:, tc * 2 + par:tc * 2 + par + 1], None, OP.is_equal),
                    deps=[b2v] if par == 0 else [exts[-1]], tag=f"oh{tc},{par}")
                o2 = V(lambda v, tc=tc, par=par: v.scalar_tensor_tensor(
                    cntscr[:, 0:32], ohot, 1.0, PS[5][:, 0:32], OP.mult, OP.mult,
                    accum_out=rl_cols[:, tc * 2 + par:tc * 2 + par + 1]), deps=[o1], tag=f"rl{tc},{par}")
                o3 = V(lambda v, tc=tc, par=par: v.scalar_tensor_tensor(
                    cntscr[:, 0:32], ohot, 1.0, selt, OP.mult, OP.mult,
                    accum_out=sl_cols[:, tc * 2 + par:tc * 2 + par + 1]), deps=[o2])
                o4 = V(lambda v, tc=tc, par=par: v.scalar_tensor_tensor(
                    cntscr[:, 0:32], ohot, 1.0, mfull_s[:, tc * E:(tc + 1) * E], OP.mult, OP.mult,
                    accum_out=w01_s[:, tc * 2 + par:tc * 2 + par + 1]), deps=[o3], tag=f"wp{tc},{par}")
                exts.append(o4)
            prev = exts[-1]
        sel_done = prev

        cnt_ev = V(lambda v: v.tensor_copy(G3f[0:1, 3144:3176], PS[4][0:1, 0:32]), deps=[t_cnt_mm], tag="cntev")
        t_cst = QS(lambda s: s.dma_start(out=d_cnin[:, :], in_=G3f[0:1, 3144:3176]), deps=[cnt_ev])
        t_agC = CC("AllGather", OP.bypass, [d_cnin[:, :]], [d_cnout[:, :]], deps=[t_cst], tag="agC")
        t_cld = QS(lambda s: s.dma_start(out=cnt8[:], in_=d_cnout[:, :]), deps=[t_agC], tag="cld")
        g1m = T(lambda t: t.matmul(PS[6][0:1, 0:32], mask8[:], cnt8[:], start=True, stop=True),
                deps=[t_cld, t_small[6]], tag="gbase")
        g2v = V(lambda v: v.tensor_tensor(G3f[0:1, 4000:4032], eoff_s[:],
                                          PS[6][0:1, 0:32], OP.add),
                deps=[g1m, t_small[12]], tag="gbrow")
        g3m = T(lambda t: t.matmul(PS[6][:, 0:32], ones_row[:], G3f[0:1, 4000:4032],
                                   start=True, stop=True), deps=[g2v], tag="gbbc")
        gbrep = G3f[:, 3328:3360]
        g4v = V(lambda v: v.tensor_copy(gbrep, PS[6][:, 0:32]), deps=[g3m], tag="gbrep")
        limrep = G3f[:, 3264:3296]
        lm1 = V(lambda v: v.tensor_scalar(G3f[0:1, 4032:4064], eoff_s[:], float(CAP), None, OP.add),
                deps=[t_small[12], g4v], tag="limrow")
        lm2 = T(lambda t: t.matmul(PS[6][:, 0:32], ones_row[:], G3f[0:1, 4032:4064],
                                   start=True, stop=True), deps=[lm1, g4v], tag="limbc")
        lm3 = V(lambda v: v.tensor_copy(limrep, PS[6][:, 0:32]), deps=[lm2], tag="limrep")

        ZROWI = float(E * CAP)
        t_gsc = []
        prev = lm3
        for tc in range(NTC):
            for par in range(2):
                o1 = V(lambda v, tc=tc, par=par: v.tensor_scalar(
                    ohot, iexp[:], e12_s[:, tc * 2 + par:tc * 2 + par + 1], None, OP.is_equal),
                    deps=[prev], tag=f"goh{tc},{par}")
                o2 = V(lambda v: v.scalar_tensor_tensor(
                    cntscr[:, 0:32], ohot, 1.0, gbrep, OP.mult, OP.mult,
                    accum_out=colA[:]), deps=[o1])
                o3 = V(lambda v, tc=tc, par=par: v.tensor_tensor(
                    colB[:], rl_cols[:, tc * 2 + par:tc * 2 + par + 1], colA[:], OP.add), deps=[o2])
                o4b = V(lambda v: v.scalar_tensor_tensor(
                    cntscr[:, 0:32], ohot, 1.0, limrep, OP.mult, OP.mult,
                    accum_out=colE[:]), deps=[o3])
                o4c = V(lambda v: v.tensor_tensor(colF[:], colB[:], colE[:], OP.is_lt), deps=[o4b])
                o4 = V(lambda v, tc=tc: v.tensor_scalar(colC[:], itokg[:, tc:tc + 1], 0.0, ZROWI, OP.mult, OP.add), deps=[o4c])
                o5a = V(lambda v, tc=tc, par=par: v.tensor_tensor(
                    colH[:], sl_cols[:, tc * 2 + par:tc * 2 + par + 1], colF[:], OP.mult), deps=[o4])
                o5 = V(lambda v: v.tensor_scalar(
                    colD[:], colH[:], -1.0, 1.0, OP.mult, OP.add),
                    deps=[o5a])
                o6 = V(lambda v: v.tensor_tensor(colC[:], colC[:], colB[:], OP.subtract), deps=[o5])
                o7 = V(lambda v: v.scalar_tensor_tensor(colB[:], colC[:], colD[:, :], colB[:], OP.mult, OP.add), deps=[o6])
                o8 = V(lambda v, tc=tc, par=par: v.tensor_copy(
                    gsc_u[:, tc * 2 + par:tc * 2 + par + 1], colB[:]), deps=[o7], tag=f"gsc{tc},{par}")
                t_gsc.append(o8)
                prev = o8
        gsc_done = prev

        zt = V(lambda v: v.memset(G3f[:, 3360:4002], 0.0), deps=[t_sv], tag="zidx")
        t_zidx = QS(lambda s: s.dma_start(
            out=d_idx.rearrange("r c -> (r c)").rearrange("(p q) -> p q", p=128),
            in_=G3f[:, 3360:4002]), deps=[zt], tag="zidxst")
        t_scat = []
        prevs = [t_zidx, gsc_done, t_itok, t_regs]
        for tc in range(NTC):
            for par in range(2):
                j = tc * 2 + par
                slot8 = j % 8
                depv = list(prevs)
                if len(t_scat) >= 8:
                    depv.append(t_scat[-8])
                b1v = V(lambda v, tc=tc, slot8=slot8: v.tensor_copy(
                    scv[:, slot8 * 2:slot8 * 2 + 1], itokg[:, tc:tc + 1]), deps=depv, tag=f"svi{j}")
                b2v2 = V(lambda v, j=j, slot8=slot8: v.tensor_copy(
                    scv[:, slot8 * 2 + 1:slot8 * 2 + 2], w01_s[:, j:j + 1]), deps=[b1v], tag=f"svw{j}")
                sc_ = QG(lambda g, j=j, slot8=slot8: g.indirect_dma_start(
                    d_idx[:, :], bass.IndirectOffsetOnAxis(ap=gsc_u[:, j:j + 1], axis=0),
                    scv[:, slot8 * 2:slot8 * 2 + 2], None,
                    bounds_check=rB_map, oob_is_err=False),
                    deps=[b2v2], tag=f"scat{j}")
                t_scat.append(sc_)
                prevs = []
        t_arI = CC("AllReduce", OP.add, [d_idx[:, :]], [d_idxg[:, :]],
                   deps=[t_scat[-1]], tag="arI")

        stageb_done = [t_arI, gsc_done, sel_done]

        t_aupl = []
        for g_ in range(NG):
            t_aupl.append(QG(lambda g, g_=g_: g.dma_start(
                out=G2b[:, g_ * 2048:(g_ + 1) * 2048].rearrange("p (k j) -> p k j", j=256),
                in_=aupt[g_].rearrange("(k p) j -> p k j", p=128)),
                deps=[t_sst, t_lgt[NTT - 1]] + t_dbg_stats2 + t_dbg_lgt, tag=f"aupl{g_}"))
        t_adnl = []
        for g_ in range(NG):
            t_adnl.append(QG(lambda g, g_=g_: g.dma_start(
                out=G4b[:, g_ * 1024:(g_ + 1) * 1024],
                in_=adnt[g_]), deps=[t_lgt[NTT - 1], sel_done] + t_dbg_hfm, tag=f"adnl{g_}"))

        gwtmp = G3f[:, 1024:1040]
        t_gw = []
        prevg = [sel_done, t_oh16]
        for tc in range(NTC):
            w1 = V(lambda v, tc=tc: v.tensor_tensor(
                gwtmp, mfull_s[:, tc * E:(tc + 1) * E:2], mfull_s[:, tc * E + 1:(tc + 1) * E:2], OP.add),
                deps=prevg, tag=f"gw{tc}")
            w2 = V(lambda v: v.tensor_scalar(gwtmp, gwtmp, 0.05, None, OP.mult), deps=[w1])
            w3 = T(lambda t: t.transpose(PS[7][0:16, 0:128], gwtmp, identf[:]), deps=[w2], tag=f"gwt{tc}")
            w4 = V(lambda v, tc=tc: v.tensor_copy(gwT_b[0:16, tc * 128:(tc + 1) * 128], PS[7][0:16, 0:128]), deps=[w3])
            prevg = [w4]
            t_gw.append(w4)

        gwsb = G3b[:, 0:4096]
        silsb = G3b[:, 4096:4352]
        ahp = G3b[:, 4352:4608]
        adt0 = G3b[:, 4608:5632]
        probs_bf = probs_s.ap().bitcast(BF16)
        mfull_bf = mfull_s.ap().bitcast(BF16)
        t_ad_st = []
        t_up_last = None
        t_dn_last = None
        prev_gwev = [t_gw[-1], lm3]
        prev_dn_ev = None
        for sg in range(16):
            gw_evs = []
            for g_ in range(16):
                mmgw = T(lambda t, sg=sg, g_=g_: t.matmul(
                    PS[6][:, 0:256], oh16[:, g_ * 128:(g_ + 1) * 128],
                    gwT_b[0:16, sg * 256:(sg + 1) * 256], start=True, stop=True),
                    deps=prev_gwev + [t_gw[-1]], tag=f"gwm{sg},{g_}")
                ev = V(lambda v, g_=g_: v.tensor_copy(gwsb[:, g_ * 256:(g_ + 1) * 256], PS[6][:, 0:256]),
                       deps=[mmgw], tag=f"gwe{sg},{g_}")
                prev_gwev = [ev]
                gw_evs.append(ev)
            for g_ in range(16):
                depg = [t_aupl[g_], t_x1fm[-1] if False else t_x1fm[0], gw_evs[g_]]
                depg += t_x1fm
                if t_up_last is not None:
                    depg.append(t_up_last)
                tokg = None
                for ki in range(NCHK):
                    tokg = T(lambda t, sg=sg, g_=g_, ki=ki: t.matmul(
                        PS[0][:, 0:256],
                        G2b[:, g_ * 2048 + ki * 256: g_ * 2048 + ki * 256 + 128],
                        G1b[:, ki * 4096 + sg * 256: ki * 4096 + (sg + 1) * 256],
                        start=(ki == 0), stop=(ki == NCHK - 1)),
                        deps=depg if ki == 0 else [], tag=f"aug{sg},{g_}")
                tokv = None
                for ki in range(NCHK):
                    tokv = T(lambda t, sg=sg, g_=g_, ki=ki: t.matmul(
                        PS[1][:, 0:256],
                        G2b[:, g_ * 2048 + ki * 256 + 128: g_ * 2048 + ki * 256 + 256],
                        G1b[:, ki * 4096 + sg * 256: ki * 4096 + (sg + 1) * 256],
                        start=(ki == 0), stop=(ki == NCHK - 1)), tag=f"auv{sg},{g_}")
                s_sil = SC(lambda s: s.activation(out=silsb, in_=PS[0][:, 0:256], func=AF.Silu),
                           deps=[tokg], tag=f"asil{sg},{g_}")
                p1 = V(lambda v: v.tensor_tensor(ahp, silsb, PS[1][:, 0:256], OP.mult),
                       deps=[s_sil, tokv], tag=f"ap1{sg},{g_}")
                p2 = V(lambda v, g_=g_: v.tensor_tensor(ahp, ahp, gwsb[:, g_ * 256:(g_ + 1) * 256], OP.mult),
                       deps=[p1, gw_evs[g_]], tag=f"ap2{sg},{g_}")
                t_up_last = p2
                dn_dep = [p2, t_adnl[g_]]
                if g_ == 0 and prev_dn_ev is not None:
                    dn_dep.append(prev_dn_ev)
                tokd = None
                for tg in range(2):
                    for nt in range(2):
                        tokd = T(lambda t, sg=sg, g_=g_, tg=tg, nt=nt: t.matmul(
                            PS[2 + tg * 2 + nt][:, 0:512],
                            ahp[:, tg * 128:(tg + 1) * 128],
                            G4b[:, g_ * 1024 + nt * 512: g_ * 1024 + (nt + 1) * 512],
                            start=(g_ == 0), stop=(g_ == NG - 1)),
                            deps=dn_dep if (tg == 0 and nt == 0) else [], tag=f"adn{sg},{g_}")
                t_dn_last = tokd
            evs = []
            for tg in range(2):
                e1_ = V(lambda v, tg=tg: v.tensor_copy(adt0[:, 0:512], PS[2 + tg * 2][:, 0:512]),
                        deps=[t_dn_last] + evs, tag=f"ade{sg},{tg}a")
                e2_ = V(lambda v, tg=tg: v.tensor_copy(adt0[:, 512:1024], PS[2 + tg * 2 + 1][:, 0:512]),
                        deps=[e1_], tag=f"ade{sg},{tg}b")
                st_ = QS(lambda s, sg=sg, tg=tg: s.dma_start(
                    out=d_ad[sg * 256 + tg * 128: sg * 256 + (tg + 1) * 128, :], in_=adt0),
                    deps=[e2_], tag=f"adst{sg},{tg}")
                evs = [st_]
                t_ad_st.append(st_)
            prev_dn_ev = t_ad_st[-1]
        adapters_done = [t_ad_st[-1], t_up_last]

        ST3 = [(0, 512), (512, 512), (1024, 256)]
        t_yw_st = []
        prev_upmm = adapters_done[1]
        prev_dnev = None
        prev_xe_tp = {}
        prev_gather_store = None
        t_wupl = {}
        t_wdnl = {}
        t_idsT = {}
        expert_wst = {}

        t_stage_sx = {}

        def declare_stage(k):
            stage_region = ((probs_bf[:, 0:1024], probs_bf[:, 1024:2048]) if k == 0
                            else (G3b[:, 0:1024], G3b[:, 1024:2048]))
            slab4 = mfull_s.ap()[0:10, 0:256] if k == 0 else G3f[0:10, 1024:1280]
            ids10 = mfull_s.ap()[0:10, 256:384] if k == 0 else G3f[0:10, 1280:1408]
            depsl = [t_arI, t_idxl4[k], tq, t_gw[-1]]
            if k == 0:
                depsl += [sel_done]
            else:
                depsl += [gsc_done]
            if k >= 1:
                depsl.append(t_idsT[k - 1])
            gsl = QG(lambda g, k=k, slab4=slab4: g.indirect_dma_start(
                slab4, None, d_idxg.rearrange("r c -> (r c)").rearrange("(b q) -> b q", q=256),
                bass.IndirectOffsetOnAxis(ap=idxl4[k][:, :], axis=0)), deps=depsl, tag=f"slab{k}")
            c1_ = V(lambda v, slab4=slab4, ids10=ids10: v.tensor_copy(ids10, slab4[:, 0:256:2]), deps=[gsl], tag=f"ids{k}")
            trk = T(lambda t, ids10=ids10: t.transpose(PS[7][:, 0:10], ids10, identf[0:10, 0:10]),
                    deps=[c1_], tag=f"idtr{k}")
            t_idsT[k] = V(lambda v, k=k: v.tensor_copy(idsT[:, k * 10:(k + 1) * 10], PS[7][:, 0:10]),
                          deps=[trk], tag=f"idsT{k}")
            t_sx = []
            for gi in range(10):
                stg = stage_region[gi % 2]
                depg = [t_idsT[k], t_agX]
                if gi >= 2:
                    depg.append(t_sx[gi - 2])
                if gi < 2 and k >= 2:
                    depg.append(t_tp[k - 2][-1])
                gx = QG(lambda g, k=k, gi=gi, stg=stg: g.indirect_dma_start(
                    stg, None, d_xg[:, :],
                    bass.IndirectOffsetOnAxis(ap=idsT[:, k * 10 + gi:k * 10 + gi + 1], axis=0)),
                    deps=depg, tag=f"xeg{k},{gi}")
                sx = QG(lambda s, k=k, gi=gi, stg=stg: s.dma_start(
                    out=d_xe[k % 2][gi * 128:(gi + 1) * 128, :], in_=stg), deps=[gx], tag=f"xes{k},{gi}")
                t_sx.append(sx)
            t_stage_sx[k] = t_sx
            if k == 0 and "xe0" in dbg:
                ln_done.append(QS(lambda s: s.dma_start(out=dbg["xe0"][:, :], in_=d_xe[0][:, :]),
                                  deps=list(t_sx), tag="dbgxe0"))
            if k == 2 and "xe2" in dbg:
                ln_done.append(QS(lambda s: s.dma_start(out=dbg["xe2"][:, :], in_=d_xe[0][:, :]),
                                  deps=list(t_sx), tag="dbgxe2"))

        t_tp = {}
        dbg_gate = {}

        def declare_tp(k, extra_deps):
            tps = []
            for ci in range(NCHK):
                dept = list(t_stage_sx[k]) + list(extra_deps) if ci == 0 else []
                if ci == 0 and (k - 1) in dbg_gate:
                    dept += dbg_gate[k - 1]
                tx = QV(lambda s, k=k, ci=ci: s.dma_start(
                    out=G2b[:, ci * 1280:(ci + 1) * 1280],
                    in_=d_xe[k % 2][:, ci * 128:(ci + 1) * 128], transpose=True),
                    deps=dept, tag=f"xet{k},{ci}")
                tps.append(tx)
            t_tp[k] = tps
            if k == 2 and "xef2" in dbg:
                tk = QS(lambda s: s.dma_start(out=dbg["xef2"][:, :], in_=G2b[:, 0:10240]),
                        deps=list(tps), tag="dbgxef2")
                dbg_gate[2] = dbg_gate.get(2, []) + [tk]
                ln_done.append(tk)
            return tps

        declare_stage(0)
        declare_stage(1)
        declare_tp(0, [adapters_done[1]])
        t_agYk = {}

        for k in range(ELOC):
            t_xe_tp = t_tp[k]
            if k == 0:
                t_wupl[0] = QG(lambda g: g.dma_start(
                    out=G1b[:, :].rearrange("p (a j) -> p a j", j=4096),
                    in_=wupt[0].rearrange("(a p) j -> p a j", p=128)),
                    deps=[adapters_done[1]] + t_dbg_x1fm, tag="wup0")
                t_wdnl[0] = QG(lambda g: g.dma_start(
                    out=G4b[:, :].rearrange("p (a j) -> p a j", j=1024),
                    in_=wdnt[0].rearrange("(a p) j -> p a j", p=128)),
                    deps=adapters_done, tag="wdn0")

            t_silu = {}
            hact_done = []
            for jo2 in range(16):
                depu = [t_wupl[k]] + t_xe_tp
                if prev_upmm is not None:
                    depu.append(prev_upmm)
                if jo2 == 0:
                    depu.append(t_ad_st[-1] if k == 0 else prev_dnev)
                    depu += dbg_gate.get(("h", k - 1), [])
                tokg = None
                for ki in range(NCHK):
                    for sti, (so, sn) in enumerate(ST3):
                        tokg = T(lambda t, jo2=jo2, ki=ki, so=so, sn=sn, sti=sti: t.matmul(
                            PS[sti][:, 0:sn],
                            G1b[:, ki * 4096 + jo2 * 128: ki * 4096 + (jo2 + 1) * 128],
                            G2b[:, ki * 1280 + so: ki * 1280 + so + sn],
                            start=(ki == 0), stop=(ki == NCHK - 1)),
                            deps=depu if (ki == 0 and sti == 0) else [], tag=f"eu{k},{jo2},{ki},{sti}")
                sil3 = []
                for sti, (so, sn) in enumerate(ST3):
                    sil3.append(SC(lambda s, so=so, sn=sn, sti=sti: s.activation(
                        out=G2b[:, 10240 + so: 10240 + so + sn], in_=PS[sti][:, 0:sn], func=AF.Silu),
                        deps=[tokg], tag=f"esil{k},{jo2},{sti}"))
                tokv = None
                for ki in range(NCHK):
                    for sti, (so, sn) in enumerate(ST3):
                        tokv = T(lambda t, jo2=jo2, ki=ki, so=so, sn=sn, sti=sti: t.matmul(
                            PS[3 + sti][:, 0:sn],
                            G1b[:, ki * 4096 + (16 + jo2) * 128: ki * 4096 + (17 + jo2) * 128],
                            G2b[:, ki * 1280 + so: ki * 1280 + so + sn],
                            start=(ki == 0), stop=(ki == NCHK - 1)),
                            deps=[sil3[-1]] if (ki == 0 and sti == 0) else [], tag=f"ev{k},{jo2},{ki},{sti}")
                hp = []
                for sti, (so, sn) in enumerate(ST3):
                    hp.append(V(lambda v, jo2=jo2, so=so, sn=sn, sti=sti: v.tensor_tensor(
                        G2b[:, 11520 + jo2 * 1280 + so: 11520 + jo2 * 1280 + so + sn],
                        G2b[:, 10240 + so: 10240 + so + sn],
                        PS[3 + sti][:, 0:sn], OP.mult),
                        deps=[tokv] + sil3 if sti == 0 else [], tag=f"eh{k},{jo2},{sti}"))
                prev_upmm = hp[-1]
                hact_done.append(hp[-1])

            if k == 2 and "hact2" in dbg:
                tk = QS(lambda s: s.dma_start(out=dbg["hact2"][:, :], in_=G2b[:, 11520:32000]),
                        deps=list(hact_done), tag="dbghact2")
                dbg_gate[("h", 2)] = [tk]
                ln_done.append(tk)

            if k + 2 < ELOC:
                declare_stage(k + 2)
            if k + 1 < ELOC:
                declare_tp(k + 1, [prev_upmm])
                t_wupl[k + 1] = QG(lambda g, k=k: g.dma_start(
                    out=G1b[:, :].rearrange("p (a j) -> p a j", j=4096),
                    in_=wupt[k + 1].rearrange("(a p) j -> p a j", p=128)),
                    deps=[prev_upmm, t_wupl[k]], tag=f"wup{k + 1}")

            ye_reg = [G3b[:, 5632:6912], G3b[:, 6912:8192]]
            sty_par = {}
            for oo in range(NCHK):
                depd = [t_wdnl[k]] + hact_done
                if prev_dnev is not None:
                    depd.append(prev_dnev)
                tokd = None
                for ki16 in range(16):
                    for sti, (so, sn) in enumerate(ST3):
                        tokd = T(lambda t, oo=oo, ki16=ki16, so=so, sn=sn, sti=sti: t.matmul(
                            PS[(oo % 2) * 3 + sti][:, 0:sn],
                            G4b[:, ki16 * 1024 + oo * 128: ki16 * 1024 + (oo + 1) * 128],
                            G2b[:, 11520 + ki16 * 1280 + so: 11520 + ki16 * 1280 + so + sn],
                            start=(ki16 == 0), stop=(ki16 == 15)),
                            deps=depd if (ki16 == 0 and sti == 0) else [], tag=f"ed{k},{oo},{ki16},{sti}")
                ev3 = []
                for sti, (so, sn) in enumerate(ST3):
                    evd = [tokd]
                    if sti == 0 and oo % 2 in sty_par:
                        evd.append(sty_par[oo % 2])
                    if sti == 0 and oo < 2 and k > 0 and t_yw_st:
                        evd.append(t_yw_st[-1])
                    ev3.append(V(lambda v, oo=oo, so=so, sn=sn, sti=sti: v.tensor_copy(
                        ye_reg[oo % 2][:, so:so + sn], PS[(oo % 2) * 3 + sti][:, 0:sn]),
                        deps=evd, tag=f"ee{k},{oo},{sti}"))
                prev_dnev = ev3[-1]
                sty = QS(lambda s, oo=oo, k=k: s.dma_start(
                    out=d_yefm[oo * 128:(oo + 1) * 128, :], in_=ye_reg[oo % 2]),
                    deps=ev3 + ([t_yw_st[-1]] if (k > 0 or oo > 1) and t_yw_st else []), tag=f"yest{k},{oo}")
                t_yw_st.append(sty)
                sty_par[oo % 2] = sty
            ywt = [G3b[:, 8192:9216], G3b[:, 9216:10240]]
            ye_stores = t_yw_st[-NCHK:]
            yws_k = []
            for gi in range(10):
                dept = list(ye_stores) if gi == 0 else []
                if gi >= 2:
                    dept.append(t_yw_st[-1])
                tl = QS(lambda s, gi=gi, k=k: s.dma_start(
                    out=ywt[gi % 2], in_=d_yefm[:, gi * 128:(gi + 1) * 128], transpose=True),
                    deps=dept, tag=f"ywt{k},{gi}")
                st_ = QS(lambda s, gi=gi, k=k: s.dma_start(
                    out=d_yw[k * CAP + gi * 128: k * CAP + (gi + 1) * 128, :], in_=ywt[gi % 2]),
                    deps=[tl], tag=f"yws{k},{gi}")
                t_yw_st.append(st_)
                yws_k.append(st_)
            t_agYk[k] = CC("AllGather", OP.bypass,
                           [d_yw[k * CAP:(k + 1) * CAP, :]],
                           [d_ywg[k * NCORE * CAP:(k + 1) * NCORE * CAP, :]],
                           deps=yws_k, tag=f"agY{k}")
            if k + 1 < ELOC:
                t_wdnl[k + 1] = QG(lambda g, k=k: g.dma_start(
                    out=G4b[:, :].rearrange("p (a j) -> p a j", j=1024),
                    in_=wdnt[k + 1].rearrange("(a p) j -> p a j", p=128)),
                    deps=[prev_dnev, t_wdnl[k]], tag=f"wdn{k + 1}")

        if "idsT" in dbg:
            ln_done.append(QS(lambda s: s.dma_start(out=dbg["idsT"][:, :], in_=idsT[:]),
                              deps=[t_idsT[kk] for kk in range(ELOC)], tag="dbgidsT"))

        zr_s = G3b[0:1, 4096:5120]
        zr1 = V(lambda v: v.memset(zr_s, 0.0), deps=[gsc_done, prev_upmm], tag="zr1")
        zr2 = QS(lambda s: s.dma_start(out=d_ywg[E * CAP:E * CAP + 1, :], in_=zr_s),
                 deps=[zr1], tag="zr2")
        ag_all = [t_agYk[kk] for kk in range(ELOC)] + [zr2]
        rg0P = [G3b[:, 0:1024], G2b[:, 0:1024]]
        rg1P = [G3b[:, 1024:2048], G2b[:, 1024:2048]]
        x1t2P = [G3b[:, 2048:3072], G2b[:, 2048:3072]]
        adt2P = [G3b[:, 3072:4096], G2b[:, 3072:4096]]
        y2fP = [G4f[:, 0:1024], G4f[:, 1024:2048]]
        outfP = [G4f[:, 2048:3072], G4f[:, 3072:4096]]
        colAP = [colA, colC]
        colBP = [colB, colD]
        colXP = [colE, colF]
        bn6P = [bn6, bn6b]
        mv2P = [mv2, mv2b]
        dep_init = [t_stage_sx[3][-1], prev_upmm, prev_dnev, t_idsT[3]]
        t_out = []
        hist = {}
        for tc in range(NTC):
            p = tc % 2
            rg0, rg1, x1t2, adt2 = rg0P[p], rg1P[p], x1t2P[p], adt2P[p]
            y2f, outf = y2fP[p], outfP[p]
            cA, cB, cX, bnb, mvb = colAP[p], colBP[p], colXP[p], bn6P[p], mv2P[p]
            pv = hist.get(tc - 2, {})
            d_ag = ag_all if tc < 2 else []
            d_in = dep_init if tc < 2 else []
            g0 = QG(lambda g, tc=tc, rg0=rg0: g.indirect_dma_start(
                rg0, None, d_ywg[:, :],
                bass.IndirectOffsetOnAxis(ap=gsc_u[:, tc * 2:tc * 2 + 1], axis=0),
                bounds_check=rB_map, oob_is_err=False),
                deps=d_ag + d_in + ([pv["s2"]] if pv else []), tag=f"fg0{tc}")
            g1 = QG(lambda g, tc=tc, rg1=rg1: g.indirect_dma_start(
                rg1, None, d_ywg[:, :],
                bass.IndirectOffsetOnAxis(ap=gsc_u[:, tc * 2 + 1:tc * 2 + 2], axis=0),
                bounds_check=rB_map, oob_is_err=False),
                deps=[g0] + ([pv["s1"]] if pv else []), tag=f"fg1{tc}")
            lx = QS(lambda s, tc=tc, x1t2=x1t2: s.dma_start(
                out=x1t2, in_=d_x1tm[tc * 128:(tc + 1) * 128, :]),
                deps=d_in + ([pv["s1"]] if pv else []), tag=f"fx{tc}")
            la = QV(lambda s, tc=tc, adt2=adt2: s.dma_start(
                out=adt2, in_=d_ad[tc * 128:(tc + 1) * 128, :]),
                deps=[t_ad_st[-1]] + d_in + ([pv["s3"]] if pv else []), tag=f"fa{tc}")
            s1 = V(lambda v, tc=tc, y2f=y2f, rg1=rg1, x1t2=x1t2: v.scalar_tensor_tensor(
                y2f, rg1, w01_s[:, tc * 2 + 1:tc * 2 + 2], x1t2, OP.mult, OP.add),
                deps=[g1, lx] + ([pv["nm"]] if pv else []), tag=f"fs1{tc}")
            s2 = V(lambda v, tc=tc, y2f=y2f, rg0=rg0: v.scalar_tensor_tensor(
                y2f, rg0, w01_s[:, tc * 2:tc * 2 + 1], y2f, OP.mult, OP.add),
                deps=[s1, g0], tag=f"fs2{tc}")
            s3 = V(lambda v, y2f=y2f, adt2=adt2: v.tensor_tensor(y2f, y2f, adt2, OP.add),
                   deps=[s2, la], tag=f"fs3{tc}")
            a2 = V(lambda v, y2f=y2f, bnb=bnb: v.bn_stats(bnb[:, 0:6], y2f[:, 0:512]),
                   deps=[s3] + ([pv["a4"]] if pv else []))
            a3 = V(lambda v, y2f=y2f, bnb=bnb: v.bn_stats(bnb[:, 6:12], y2f[:, 512:1024]), deps=[a2])
            a4 = V(lambda v, bnb=bnb, mvb=mvb: v.bn_aggr(mvb[:], bnb[:]),
                   deps=[a3] + ([pv["nm"]] if pv else []), tag=f"fbn{tc}")
            a7 = V(lambda v, cA=cA, mvb=mvb: v.tensor_scalar(cA[:], mvb[:, 1:2], EPS, None, OP.add),
                   deps=[a4] + ([pv["f1"]] if pv else []))
            f1 = SC(lambda s, cA=cA, cB=cB: s.activation(out=cB[:], in_=cA[:], func=AF.Ln),
                    deps=[a7] + ([pv["nm"]] if pv else []), tag=f"fln{tc}")
            f2 = SC(lambda s, cB=cB: s.activation(out=cB[:], in_=cB[:], func=AF.Exp, scale=-0.5),
                    deps=[f1])
            cx = V(lambda v, cX=cX, cB=cB, mvb=mvb: v.tensor_scalar(
                cX[:], mvb[:, 0:1], cB[:, :], -1.0, OP.mult, OP.mult),
                deps=[f2] + ([pv["nm"]] if pv else []), tag=f"fcx{tc}")
            nm = SC(lambda s, outf=outf, y2f=y2f, cB=cB, cX=cX: s.activation(
                out=outf, in_=y2f, func=AF.Identity, bias=cX[:, :], scale=cB[:, :]),
                deps=[cx] + ([pv["st"]] if pv else []), tag=f"fnm{tc}")
            st_ = QS(lambda s, tc=tc, outf=outf: s.dma_start(
                out=o_out[tc * 128:(tc + 1) * 128, :], in_=outf), deps=[nm], tag=f"fout{tc}")
            t_out.append(st_)
            hist[tc] = dict(s1=s1, s2=s2, s3=s3, a4=a4, f1=f1, nm=nm, st=st_)
            if "y2pre" in dbg:
                t_y2p = QS(lambda s, tc=tc, y2f=y2f: s.dma_start(
                    out=dbg["y2pre"][tc * 128:(tc + 1) * 128, :], in_=y2f),
                    deps=[nm], tag=f"dbgy2p{tc}")
                t_out.append(t_y2p)
                hist[tc]["nm"] = t_y2p
        fin_deps = t_out
        if "xtm" in dbg:
            ln_done.append(QG(lambda g: g.dma_start(out=dbg["xtm"][:, :], in_=d_xtm[:, :]),
                              deps=[t_xtm], tag="dbgxtm"))
        if "ypred" in dbg:
            ln_done.append(QG(lambda g: g.dma_start(out=dbg["ypred"][:, :], in_=d_ypre[:, :]),
                              deps=[t_ypst], tag="dbgypred"))
        if "poolcq" in dbg:
            ln_done.append(QS(lambda s: s.dma_start(out=dbg["poolcq"][:, :], in_=poolCq),
                              deps=[t_qpool], tag="dbgpcq"))
        if "poolck" in dbg:
            ln_done.append(QS(lambda s: s.dma_start(out=dbg["poolck"][:, :], in_=poolCk),
                              deps=[t_kpool], tag="dbgpck"))
        if "mqd" in dbg:
            ln_done.append(QS(lambda s: s.dma_start(out=dbg["mqd"][:, :], in_=G1b[:, 0:4096]),
                              deps=[t_qpool], tag="dbgmqd"))
        if "hrow" in dbg:
            ln_done.append(QS(lambda s: s.dma_start(out=dbg["hrow"][:, :], in_=G4b[0:16, 0:4096]),
                              deps=[t_hev[(0, tt)] for tt in range(NTT)] + [sel_done], tag="dbgh"))
        if "lgt" in dbg:
            ln_done.append(QS(lambda s: s.dma_start(out=dbg["lgt"][:, :], in_=lgt),
                              deps=[t_lgt[tt] for tt in range(NTT)] + [sel_done], tag="dbglgt"))
        if "probs" in dbg:
            ln_done.append(QS(lambda s: s.dma_start(
                out=dbg["probs"].rearrange("(tc p) e -> p tc e", p=128),
                in_=probs_s.ap().rearrange("p (tc e) -> p tc e", e=E)),
                deps=t_probs, tag="dbgprobs"))
        if "w01" in dbg:
            ln_done.append(QS(lambda s: s.dma_start(out=dbg["w01"][:, :], in_=w01_s[:]),
                              deps=[sel_done], tag="dbgw01"))
        if "theta" in dbg:
            ln_done.append(QG(lambda g: g.dma_start(out=dbg["theta"][:, :], in_=d_thout[:, :]),
                              deps=[t_agT], tag="dbgtheta"))
        if "cnt" in dbg:
            ln_done.append(QG(lambda g: g.dma_start(out=dbg["cnt"][:, :], in_=d_cnout[:, :]),
                              deps=[t_agC], tag="dbgcnt"))
        if "mfull" in dbg:
            ln_done.append(QS(lambda s: s.dma_start(
                out=dbg["mfull"].rearrange("(tc p) e -> p tc e", p=128),
                in_=mfull_s.ap().rearrange("p (tc e) -> p tc e", e=E)),
                deps=[sel_done], tag="dbgmfull"))
        if "gscat" in dbg:
            ln_done.append(QS(lambda s: s.dma_start(
                out=dbg["gscat"].rearrange("(tc p) e -> p tc e", p=128).bitcast(U32),
                in_=gsc_u.ap().rearrange("p (tc e) -> p tc e", e=2)),
                deps=[gsc_done], tag="dbggsc"))
        if "idxg" in dbg:
            ln_done.append(QG(lambda g: g.dma_start(out=dbg["idxg"][:, :], in_=d_idxg[:, :]),
                              deps=[t_arI], tag="dbgidxg"))
        if "yw" in dbg:
            ln_done.append(QG(lambda g: g.dma_start(out=dbg["yw"][:, :], in_=d_yw[:, :]),
                              deps=[t_agYk[kk] for kk in range(ELOC)], tag="dbgyw"))
        if "advec" in dbg:
            ln_done.append(QG(lambda g: g.dma_start(out=dbg["advec"][:, :], in_=d_ad[:, :]),
                              deps=[t_ad_st[-1]], tag="dbgadvec"))
        if "x1" in dbg:
            t_dx1 = [QG(lambda g: g.dma_start(out=dbg["x1"][:, :], in_=d_x1tm[:, :]),
                        deps=ln_done, tag="dbgx1")]
            ln_done = ln_done + t_dx1
        if "stats" in dbg:
            t_dst = QS(lambda s: s.dma_start(out=dbg["stats"][:, :], in_=stats16[0:8, :]),
                       deps=ln_done, tag="dbgstats")
            ln_done = ln_done + [t_dst]

        S.op("sync", lambda s: s.nop(), deps=fin_deps + ln_done + stageb_done, amt=1, tag="fence")

        S.finalize(block)
    return nc



def _prep_inputs(inputs):
    f32 = np.float32
    x = np.ascontiguousarray(inputs["L0_attn_hidden_states"].reshape(TGLOB, D), f32)
    wq = np.ascontiguousarray(inputs["L0_attn_Wq"], f32)
    wk = np.ascontiguousarray(inputs["L0_attn_Wk"], f32)
    wt = np.ascontiguousarray(inputs["L0_attn_Wt"], f32)

    def bias_cols(b):
        return np.ascontiguousarray(b.reshape(8, 128).T, f32)

    bqc = bias_cols(inputs["L0_attn_bq"]); bkc = bias_cols(inputs["L0_attn_bk"])
    btc = bias_cols(inputs["L0_attn_bt"])
    wqa = np.ascontiguousarray(inputs["L0_attn_Wqa"], f32)
    wka = np.ascontiguousarray(inputs["L0_attn_Wka"], f32)
    bqa = np.ascontiguousarray(inputs["L0_attn_bqa"].reshape(H, 1), f32)
    bka = np.ascontiguousarray(inputs["L0_attn_bka"].reshape(H, 1), f32)
    r1w = inputs["L0_ffn_r1_w"]
    r1wt = np.zeros((1040, 512), f32)
    r1wt[0:1030, :] = r1w.T
    r1bc = np.ascontiguousarray(inputs["L0_ffn_r1_b"].reshape(4, 128).T, f32)
    r2wt = np.ascontiguousarray(inputs["L0_ffn_r2_w"].T, f32)
    wupt = np.ascontiguousarray(inputs["L0_ffn_w_up"].transpose(0, 2, 1), f32)
    wdnt = np.ascontiguousarray(inputs["L0_ffn_w_down"].transpose(0, 2, 1), f32)
    aupt = np.ascontiguousarray(inputs["L0_ffn_a_up"].transpose(0, 2, 1), f32)
    adnt = np.ascontiguousarray(inputs["L0_ffn_a_down"].transpose(0, 2, 1), f32)

    in_maps = []
    for c in range(NCORE):
        b = c // 2
        half = c % 2
        im = dict(
            xh=np.ascontiguousarray(x[c * TCORE:(c + 1) * TCORE]),
            wq=wq, wk=wk, wt=wt, bqc=bqc, bkc=bkc, btc=btc,
            wqa=wqa, wka=wka, bqa=bqa, bka=bka,
            r1wt=r1wt, r1bc=r1bc, r2wt=r2wt,
            wupt=np.ascontiguousarray(wupt[c * ELOC:(c + 1) * ELOC]),
            wdnt=np.ascontiguousarray(wdnt[c * ELOC:(c + 1) * ELOC]),
            aupt=aupt, adnt=adnt,
            c_mask8=(np.arange(8) < c).astype(f32).reshape(8, 1),
            c_tokbase=np.full((128, 1), c * TCORE, f32),
            c_poolA=(32 * b + np.arange(16)).astype(np.uint32).reshape(16, 1),
            c_poolB=(32 * b + 16 + np.arange(16)).astype(np.uint32).reshape(16, 1),
            c_sgem=np.array([[(32 * ((p % 32) // 4) + 4 * c + (p // 32)) * 4 + (p % 4)]
                             for p in range(128)], np.uint32),
            c_idxl=np.array([80 * k + 10 * c + gi for k in range(4)
                             for gi in range(10)], np.uint32).reshape(40, 1),
            c_eoff=np.array([[(e % 4) * 10240 + (e // 4) * 1280
                              for e in range(32)]], np.float32),
        )
        in_maps.append(im)
    return in_maps


_NC_CACHE = {}


def kernel(**inputs):
    debug = tuple(sorted(DEBUG)) if isinstance(DEBUG, (list, tuple, set)) else ()
    key = debug
    if key not in _NC_CACHE:
        _NC_CACHE[key] = build_nc(debug)
    nc = _NC_CACHE[key]
    in_maps = _prep_inputs(inputs)
    res = run_bass_kernel_spmd(nc, in_maps, core_ids=list(range(NCORE)))
    out = np.concatenate([np.asarray(res.results[c]["o_out"], np.float32)
                          for c in range(NCORE)], axis=0)
    kernel.last_results = res
    return out.reshape(4, 8192, D)


DEBUG = ()


# version 05
# speedup vs baseline: 1.2599x; 1.2599x over previous
import numpy as np
import ml_dtypes
import concourse.bass as bass
import concourse.mybir as mybir
from concourse.bass_utils import run_bass_kernel_spmd
from concourse.alu_op_type import AluOpType as OP

F32 = mybir.dt.float32
BF16 = mybir.dt.bfloat16
U32 = mybir.dt.uint32
AF = mybir.ActivationFunctionType
AX = mybir.AxisListType
BF = ml_dtypes.bfloat16

NCORE = 8
TCORE = 4096
TGLOB = 32768
D = 1024
NCHK = 8
NTT = 8
NTC = 32
H, DH = 16, 64
E, ELOC, CAP = 32, 4, 1280
INTER = 2048
NG, AI = 16, 128
EPS = 1e-5
SCALE_QK = 0.125
BISECT = 28

ENGINES = ("tensor", "vector", "scalar", "gpsimd", "sync")


class Sched:

    def __init__(self, nc, ctx):
        self.nc = nc
        self.ops = {e: [] for e in ENGINES}
        self.sems = {e: ctx.enter_context(nc.semaphore("sem_" + e))
                     for e in ENGINES}
        self.cc_sem = ctx.enter_context(nc.semaphore("sem_cc"))
        self.cc_count = 0

    def op(self, eng, emit, deps=(), amt=1, tag=""):
        tok = (eng, len(self.ops[eng]))
        self.ops[eng].append(dict(emit=emit, deps=list(deps), amt=amt,
                                  marked=False, tag=tag))
        return tok

    def dma(self, eng, emit, deps=(), tag=""):
        return self.op(eng, emit, deps, amt=16, tag=tag)

    def barrier_tokens(self):
        toks = []
        for e in ENGINES:
            if self.ops[e]:
                toks.append((e, len(self.ops[e]) - 1))
        return toks

    def finalize(self, block):
        for e in ENGINES:
            for o in self.ops[e]:
                if o["amt"] == 16:
                    o["marked"] = True
                for (pe, pi) in o["deps"]:
                    self.ops[pe][pi]["marked"] = True
        for e in ENGINES:
            c = 0
            for o in self.ops[e]:
                if o["marked"]:
                    c += o["amt"]
                o["cum"] = c
        ptr = {e: 0 for e in ENGINES}
        done_cum = {e: 0 for e in ENGINES}
        total = sum(len(v) for v in self.ops.values())
        ndone = 0
        while ndone < total:
            progressed = False
            for e in ENGINES:
                while ptr[e] < len(self.ops[e]):
                    o = self.ops[e][ptr[e]]
                    if all(self.ops[pe][pi]["cum"] <= done_cum[pe]
                           for (pe, pi) in o["deps"]):
                        done_cum[e] = o["cum"]
                        ptr[e] += 1
                        ndone += 1
                        progressed = True
                    else:
                        break
            if not progressed:
                stuck = {e: (ptr[e], len(self.ops[e])) for e in ENGINES}
                for e in ENGINES:
                    if ptr[e] < len(self.ops[e]):
                        o = self.ops[e][ptr[e]]
                        raise RuntimeError(
                            f"deadlock: {e}[{ptr[e]}] tag={o['tag']} deps="
                            f"{[(pe, pi, self.ops[pe][pi]['tag']) for pe, pi in o['deps']]} {stuck}")
        sched = self

        def emit_engine(name):
            def body(eng):
                waited = {e: 0 for e in ENGINES}
                for o in sched.ops[name]:
                    need = {}
                    for (pe, pi) in o["deps"]:
                        v = sched.ops[pe][pi]["cum"]
                        if v > need.get(pe, 0):
                            need[pe] = v
                    for pe, v in need.items():
                        if v > waited[pe]:
                            eng.wait_ge(sched.sems[pe], v)
                            waited[pe] = v
                    instr = o["emit"](eng)
                    if o["marked"]:
                        assert instr is not None, f"marked op {o['tag']} returned None"
                        instr.then_inc(sched.sems[name], o["amt"])
            return body

        for name in ENGINES:
            getattr(block, name)(emit_engine(name))


def build_nc(debug=()):
    nc = bass.Bass("TRN2", target_bir_lowering=False)

    def din(name, shape, dtype=F32):
        return nc.dram_tensor(name, shape, dtype, kind="ExternalInput").ap()

    xh = din("xh", [TCORE, D])
    wq = din("wq", [D, D]); wk = din("wk", [D, D]); wt = din("wt", [D, D])
    bqc = din("bqc", [128, 8]); bkc = din("bkc", [128, 8]); btc = din("btc", [128, 8])
    wqa = din("wqa", [D, H]); wka = din("wka", [D, H])
    bqa = din("bqa", [H, 1]); bka = din("bka", [H, 1])
    r1wt = din("r1wt", [1040, 512]); r1bc = din("r1bc", [128, 4])
    r2wt = din("r2wt", [512, 32])
    wupt = din("wupt", [ELOC, D, 2 * INTER])
    wdnt = din("wdnt", [ELOC, INTER, D])
    aupt = din("aupt", [NG, D, 2 * AI])
    adnt = din("adnt", [NG, AI, D])
    c_mask8 = din("c_mask8", [8, 1])
    c_tokbase = din("c_tokbase", [128, 1])
    c_poolA = din("c_poolA", [16, 1], U32)
    c_poolB = din("c_poolB", [16, 1], U32)
    c_sgem = din("c_sgem", [128, 1], U32)
    c_idxl = din("c_idxl", [40, 1], U32)
    c_eoff = din("c_eoff", [1, 32])

    o_out = nc.dram_tensor("o_out", [TCORE, D], F32, kind="ExternalOutput").ap()
    dbg = {}
    dbg_shapes = {
        "x1": ([TCORE, D], BF16), "stats": ([8, TCORE], BF16),
        "probs": ([TCORE, E], F32), "theta": ([E, 8], F32),
        "mfull": ([TCORE, E], F32), "cnt": ([NCORE, E], F32),
        "gscat": ([TCORE, 2], F32), "idxg": ([41088, 2], F32),
        "yw": ([ELOC * CAP, D], BF16), "advec": ([TCORE, D], BF16),
        "hrow": ([16, TCORE], BF16), "lgt": ([E, TCORE], F32),
        "xtm": ([TCORE, D], BF16), "ypred": ([D, TCORE], BF16),
        "poolcq": ([128, 8], F32), "poolck": ([128, 8], F32),
        "mqd": ([128, 4096], BF16),
        "x1fm": ([128, 32768], BF16), "stats2": ([16, 4096], BF16),
        "hfm": ([128, 16384], BF16), "lgtt": ([32, 4096], F32),
        "w01": ([128, 64], F32), "y2pre": ([TCORE, D], F32),
        "xe0": ([CAP, D], BF16), "idsT": ([128, 40], U32),
        "xe2": ([CAP, D], BF16),
        "xef2": ([128, 10240], BF16), "hact2": ([128, 20480], BF16),
    }
    for name in debug:
        shp, dt = dbg_shapes[name]
        dbg[name] = nc.dram_tensor("dbg_" + name, shp, dt, kind="ExternalOutput").ap()

    def dint(name, shape, dtype):
        return nc.dram_tensor(name, shape, dtype, kind="Internal").ap()

    d_xtm = dint("d_xtm", [TCORE, D], BF16)
    d_ypre = dint("d_ypre", [D, TCORE], BF16)
    d_x1tm = dint("d_x1tm", [TCORE, D], BF16)
    d_nq = dint("d_nq", [8, 128], F32)
    d_pin = [dint(f"d_pin{i}", [16, 72], F32) for i in range(2)]
    d_pout = [dint(f"d_pout{i}", [128, 72], F32) for i in range(2)]
    d_sem_ = dint("d_sem", [E, TCORE], F32)
    d_sgem = dint("d_sgem", [NCORE * E, TCORE], F32)
    d_thin = dint("d_thin", [ELOC, 8], F32)
    d_thout = dint("d_thout", [E, 8], F32)
    d_cnin = dint("d_cnin", [1, E], F32)
    d_cnout = dint("d_cnout", [NCORE, E], F32)
    MROW = 41088
    d_idx = dint("d_idx", [MROW, 2], F32)
    d_idxg = dint("d_idxg", [MROW, 2], F32)
    d_xg = dint("d_xg", [TGLOB, D], BF16)
    d_yw = dint("d_yw", [ELOC * CAP, D], BF16)
    d_ywg = dint("d_ywg", [NCORE * ELOC * CAP + 128, D], BF16)
    d_ad = dint("d_ad", [TCORE, D], BF16)
    d_xe = [dint(f"d_xe{i}", [CAP, D], BF16) for i in range(2)]
    d_yefm = dint("d_yefm", [D, CAP], BF16)
    d_stat8 = dint("d_stat8", [TCORE, 8], BF16)

    from contextlib import ExitStack
    with ExitStack() as ctx:
        def sbt(name, shape, dtype=F32):
            return ctx.enter_context(nc.sbuf_tensor(name, shape, dtype))

        def pst(name, shape, dtype=F32):
            return ctx.enter_context(nc.psum_tensor(name, shape, dtype))

        G1 = sbt("G1", [128, 16384]); G1b = G1.ap().bitcast(BF16); G1f = G1.ap()
        G2 = sbt("G2", [128, 16384]); G2b = G2.ap().bitcast(BF16); G2f = G2.ap()
        G3 = sbt("G3", [128, 5120]);  G3b = G3.ap().bitcast(BF16); G3f = G3.ap()
        G4 = sbt("G4", [128, 8192]);  G4b = G4.ap().bitcast(BF16); G4f = G4.ap()

        ip128 = sbt("ip128", [128, 1])
        if128 = sbt("if128", [128, 128])
        identf = sbt("identf", [128, 128])
        identb = sbt("identb", [128, 128], BF16)
        lstrict = sbt("lstrict", [128, 128])
        dup64 = sbt("dup64", [64, 128])
        maps16 = sbt("maps16", [16, 1024], BF16)
        iexp = sbt("iexp", [128, 32])
        itokg = sbt("itokg", [128, 32])
        blockones = sbt("blockones", [128, 4])
        blockT = sbt("blockT", [4, 128])
        scr4 = sbt("scr4", [4, 128])
        ip4 = sbt("ip4", [4, 1])
        i128r4 = sbt("i128r4", [4, 128])
        i32m = sbt("i32m", [128, 4])
        ip16 = sbt("ip16", [16, 1])
        ip64 = sbt("ip64", [64, 1])
        m64 = sbt("m64", [64, 128])
        ones_col = sbt("ones_col", [128, 1])
        ones_row = sbt("ones_row", [1, 128])
        wqa_b = sbt("wqa_b", [128, 8 * H], BF16)
        wka_b = sbt("wka_b", [128, 8 * H], BF16)
        r2w_b = sbt("r2w_b", [128, 4 * E], BF16)
        bqa_s = sbt("bqa_s", [16, 1])
        bka_s = sbt("bka_s", [16, 1])
        bq_sb = sbt("bq_sb", [128, 8])
        bk_sb = sbt("bk_sb", [128, 8])
        bt_sb = sbt("bt_sb", [128, 8])
        r1b_sb = sbt("r1b_sb", [128, 4])
        mask8 = sbt("mask8", [8, 1])
        tokbase = sbt("tokbase", [128, 1])
        poolA_i = sbt("poolA_i", [16, 1], U32)
        poolB_i = sbt("poolB_i", [16, 1], U32)
        sgem_i = sbt("sgem_i", [128, 1], U32)
        idxl_i = sbt("idxl_i", [40, 1], U32)
        eoff_s = sbt("eoff_s", [1, 32])
        probs_s = sbt("probs_s", [128, NTC * E])
        mfull_s = sbt("mfull_s", [128, NTC * E])
        rts_s = sbt("rts_s", [128, NTC])
        t2_s = sbt("t2_s", [128, NTC])
        e12_s = sbt("e12_s", [128, NTC * 2])
        w01_s = sbt("w01_s", [128, NTC * 2])
        gsc_u = sbt("gsc_u", [128, NTC * 2], U32)
        threp = sbt("threp", [128, E])
        colA = sbt("colA", [128, 1]); colB = sbt("colB", [128, 1])
        colC = sbt("colC", [128, 1]); colD = sbt("colD", [128, 1])
        colE = sbt("colE", [128, 1]); colF = sbt("colF", [128, 1])
        colG = sbt("colG", [128, 1]); colH = sbt("colH", [128, 1])
        col_ep = sbt("col_ep", [128, 1]); col_en = sbt("col_en", [128, 1])
        bn6 = sbt("bn6", [128, 12])
        mv2 = sbt("mv2", [128, 2])
        bn6b = sbt("bn6b", [128, 12])
        mv2b = sbt("mv2b", [128, 2])
        cols2 = sbt("cols2", [128, 8])
        cols3 = sbt("cols3", [128, 4])
        stat8tb = sbt("stat8tb", [128, 8], BF16)
        cnt8 = sbt("cnt8", [8, 32])
        scv = sbt("scv", [128, 128])
        sl2 = sbt("sl2", [128, NTC * 2])
        oh16 = sbt("oh16", [16, 2048], BF16)
        idsT = sbt("idsT", [128, 40], U32)
        idxl4 = [sbt(f"idxl4_{k}", [10, 1], U32) for k in range(4)]
        poolCq_s = sbt("poolCq_s", [128, 8])
        poolCk_s = sbt("poolCk_s", [128, 8])
        stat8t = sbt("stat8t", [128, 8], BF16)

        PS = [pst(f"ps{i}", [128, 512]) for i in range(8)]

        imap = G3f[0:16, 0:1024]
        stats16 = G2b[0:16, 20480:24576]
        i16b = G1f[0:16, 0:2048]
        gwT_b = G3b[0:16, 5632:9728]
        wrow = G3f[0:1, 0:1280]

        block = ctx.enter_context(nc.Block())
        S = Sched(nc, ctx)

        def T(emit, deps=(), tag=""):
            return S.op("tensor", emit, deps, 1, tag)

        def V(emit, deps=(), tag=""):
            return S.op("vector", emit, deps, 1, tag)

        def SC(emit, deps=(), tag=""):
            return S.op("scalar", emit, deps, 1, tag)

        def GP(emit, deps=(), tag=""):
            return S.op("gpsimd", emit, deps, 1, tag)

        def QS(emit, deps=(), tag=""):
            return S.op("sync", emit, deps, 16, tag)

        def QG(emit, deps=(), tag=""):
            return S.op("gpsimd", emit, deps, 16, tag)

        def QV(emit, deps=(), tag=""):
            return S.op("scalar", emit, deps, 16, tag)

        def CC(kind, op_, ins, outs, deps=(), tag="cc"):
            def em(g):
                return g.collective_compute(kind, op_, ins=ins, outs=outs,
                                            replica_groups=[list(range(NCORE))])
            return S.op("gpsimd", em, deps, 1, tag)

        rB_map = nc.gpsimd.alloc_register()
        rB_scat = nc.gpsimd.alloc_register()
        t_regs = GP(lambda g: g.reg_mov(rB_map, E * CAP), tag="regsetup")
        t_regs2 = GP(lambda g: g.reg_mov(rB_scat, E * CAP - 1), tag="regsetup2")
        IK = dict(allow_small_or_imprecise_dtypes=True)
        t_iota = []
        t_iota.append(GP(lambda g: g.iota(ip128[:], pattern=[[0, 1]], base=0, channel_multiplier=1, **IK), tag="iota"))
        t_iota.append(GP(lambda g: g.iota(if128[:], pattern=[[1, 128]], base=0, channel_multiplier=0, **IK)))
        t_iota.append(GP(lambda g: g.iota(iexp[:], pattern=[[1, 32]], base=0, channel_multiplier=0, **IK)))
        t_iota.append(GP(lambda g: g.iota(itokg[:], pattern=[[128, 32]], base=0, channel_multiplier=1, **IK)))
        t_iota.append(GP(lambda g: g.iota(imap[:], pattern=[[2, 8], [1, 2], [0, 64]], base=0, channel_multiplier=0, **IK)))
        t_iota.append(GP(lambda g: g.iota(ip16[:], pattern=[[0, 1]], base=0, channel_multiplier=1, **IK)))
        t_iota.append(GP(lambda g: g.iota(ip4[:], pattern=[[0, 1]], base=0, channel_multiplier=1, **IK)))
        t_iota.append(GP(lambda g: g.iota(i128r4[:], pattern=[[1, 128]], base=0, channel_multiplier=0, **IK)))
        t_iota.append(GP(lambda g: g.iota(i32m[:], pattern=[[32, 4]], base=0, channel_multiplier=0, **IK)))
        t_iota.append(GP(lambda g: g.iota(ip64[:], pattern=[[0, 1]], base=0, channel_multiplier=1, **IK)))
        t_iota.append(GP(lambda g: g.iota(m64[:], pattern=[[0, 2], [1, 64]], base=0, channel_multiplier=0, **IK)))
        t_iota.append(GP(lambda g: g.memset(ones_col[:], 1.0)))
        t_iota.append(GP(lambda g: g.memset(ones_row[:], 1.0)))
        t_iota.append(GP(lambda g: g.memset(col_ep[:], 1e-6)))
        t_iota.append(GP(lambda g: g.memset(col_en[:], -1e-6)))
        t_iota.append(GP(lambda g: g.iota(i16b[:], pattern=[[1, 16], [0, 128]], base=0, channel_multiplier=0, **IK)))

        t_xload = QG(lambda g: g.dma_start(out=G2b[:, :].rearrange("p (c d) -> p c d", d=D), in_=xh.rearrange("(c p) d -> p c d", p=128)), tag="xload")
        t_wqld = QG(lambda g: g.dma_start(out=G4b[:, 0:8192].rearrange("p (k o) -> p k o", o=D), in_=wq.rearrange("(k p) o -> p k o", p=128)), tag="wq")
        t_wkld = QG(lambda g: g.dma_start(out=G4b[:, 8192:16384].rearrange("p (k o) -> p k o", o=D), in_=wk.rearrange("(k p) o -> p k o", p=128)), tag="wk")
        t_wqald = QG(lambda g: g.dma_start(out=wqa_b.ap().rearrange("p (k h) -> p k h", h=H), in_=wqa.rearrange("(k p) h -> p k h", p=128)))
        t_wkald = QG(lambda g: g.dma_start(out=wka_b.ap().rearrange("p (k h) -> p k h", h=H), in_=wka.rearrange("(k p) h -> p k h", p=128)))

        t_small = []
        for dst, src in [(bq_sb, bqc), (bk_sb, bkc), (bt_sb, btc),
                         (bqa_s, bqa), (bka_s, bka), (r1b_sb, r1bc),
                         (mask8, c_mask8), (tokbase, c_tokbase),
                         (poolA_i, c_poolA), (poolB_i, c_poolB),
                         (sgem_i, c_sgem), (idxl_i, c_idxl),
                         (eoff_s, c_eoff)]:
            t_small.append(QS(lambda s, dst=dst, src=src: s.dma_start(out=dst[:], in_=src[:, :]), tag="small"))
        t_idxl4 = [QS(lambda s, k=k: s.dma_start(out=idxl4[k][:], in_=c_idxl[10 * k:10 * (k + 1), :]), tag=f"idxl4{k}")
                   for k in range(4)]

        t_idf = V(lambda v: v.tensor_scalar(identf[:], if128[:], ip128[:, :], None, OP.is_equal), deps=t_iota[:2], tag="identf")
        t_idb = V(lambda v: v.tensor_scalar(identb[:], if128[:], ip128[:, :], None, OP.is_equal), deps=t_iota[:2])
        t_ls = V(lambda v: v.tensor_scalar(lstrict[:], if128[:], ip128[:, :], None, OP.is_gt), deps=t_iota[:2])
        t_dup = V(lambda v: v.tensor_scalar(dup64[:], m64[:], ip64[:, :], None, OP.is_equal), deps=t_iota)
        t_maps = V(lambda v: v.tensor_scalar(maps16[:], imap[:], ip16[:, :], None, OP.is_equal), deps=t_iota)
        b1 = V(lambda v: v.tensor_scalar(blockones[:], i32m[:], ip128[:, :], None, OP.is_le), deps=t_iota)
        b2 = V(lambda v: v.tensor_scalar(i32m[:], i32m[:], 32.0, ip128[:, :], OP.add, OP.is_gt), deps=t_iota + [b1])
        t_bo = V(lambda v: v.tensor_tensor(blockones[:], blockones[:], i32m[:], OP.mult), deps=[b1, b2])
        c1 = V(lambda v: v.tensor_scalar(ip4[:], ip4[:], 32.0, None, OP.mult), deps=t_iota)
        c2 = V(lambda v: v.tensor_scalar(blockT[:], i128r4[:], ip4[:, :], None, OP.is_ge), deps=[c1])
        c3 = V(lambda v: v.tensor_scalar(scr4[:], i128r4[:], ip4[:, :], 32.0, OP.subtract, OP.is_lt), deps=[c1])
        t_bt = V(lambda v: v.tensor_tensor(blockT[:], blockT[:], scr4[:], OP.mult), deps=[c2, c3])
        t_bqa = V(lambda v: v.tensor_scalar(bqa_s[:], bqa_s[:], SCALE_QK, None, OP.mult), deps=[t_small[3]])
        t_bka = V(lambda v: v.tensor_scalar(bka_s[:], bka_s[:], SCALE_QK, None, OP.mult), deps=[t_small[4]])
        t_itok = V(lambda v: v.tensor_scalar(itokg[:], itokg[:], tokbase[:, :], None, OP.add), deps=t_iota + [t_small[7]])
        t_oh16 = V(lambda v: v.tensor_scalar(oh16[:], i16b[:], ip16[:, :], None, OP.is_equal), deps=t_iota)

        t_xtm = QS(lambda s: s.dma_start(out=d_xtm.rearrange("(c p) d -> p c d", p=128), in_=G2b[:, :].rearrange("p (c d) -> p c d", d=D)), deps=[t_xload], tag="xtm")

        XT = [0, 4096]
        t_xtile = []
        t_projq = {}
        t_evq = {}
        t_projk = {}
        t_evk = {}

        for tt in range(NTT):
            deps = [t_xtm, t_maps, t_dup]
            if tt >= 2:
                deps = [t_projk[(tt - 2, NCHK - 1)]]
            last = None
            for ci in range(NCHK):
                last = QS(lambda s, tt=tt, ci=ci: s.dma_start(
                    out=G3b[:, XT[tt % 2] + ci * 512: XT[tt % 2] + (ci + 1) * 512],
                    in_=d_xtm[tt * 512:(tt + 1) * 512, ci * 128:(ci + 1) * 128],
                    transpose=True), deps=deps if ci == 0 else [], tag=f"xt{tt}")
            t_xtile.append(last)

            for mo in range(NCHK):
                bank = mo % 2
                depq = [t_xtile[tt], t_wqld]
                if (tt, mo) not in [(0, 0), (0, 1)]:
                    prev = (tt, mo - 2) if mo >= 2 else (tt - 1, NCHK - 2 + mo)
                    depq.append(t_evq[prev])
                tok = None
                for ki in range(NCHK):
                    tok = T(lambda t, tt=tt, mo=mo, ki=ki, bank=bank: t.matmul(
                        PS[bank][:, :],
                        G4b[:, ki * 1024 + mo * 128: ki * 1024 + (mo + 1) * 128],
                        G3b[:, XT[tt % 2] + ki * 512: XT[tt % 2] + (ki + 1) * 512],
                        start=(ki == 0), stop=(ki == NCHK - 1)),
                        deps=depq if ki == 0 else [], tag=f"mq{tt},{mo}")
                t_projq[(tt, mo)] = tok
                t_evq[(tt, mo)] = SC(lambda s, tt=tt, mo=mo, bank=bank: s.activation(
                    out=G1b[:, mo * 4096 + tt * 512: mo * 4096 + (tt + 1) * 512],
                    in_=PS[bank][:, :], func=AF.Identity, bias=bq_sb[:, mo:mo + 1]),
                    deps=[tok, t_xtm], tag=f"evq{tt},{mo}")

                bank2 = 2 + mo % 2
                depk = [t_xtile[tt], t_wkld, t_xtm]
                if (tt, mo) not in [(0, 0), (0, 1)]:
                    prev = (tt, mo - 2) if mo >= 2 else (tt - 1, NCHK - 2 + mo)
                    depk.append(t_evk[prev])
                tok = None
                for ki in range(NCHK):
                    tok = T(lambda t, tt=tt, mo=mo, ki=ki, bank2=bank2: t.matmul(
                        PS[bank2][:, :],
                        G4b[:, 8192 + ki * 1024 + mo * 128: 8192 + ki * 1024 + (mo + 1) * 128],
                        G3b[:, XT[tt % 2] + ki * 512: XT[tt % 2] + (ki + 1) * 512],
                        start=(ki == 0), stop=(ki == NCHK - 1)),
                        deps=depk if ki == 0 else [], tag=f"mk{tt},{mo}")
                t_projk[(tt, mo)] = tok
                t_evk[(tt, mo)] = SC(lambda s, tt=tt, mo=mo, bank2=bank2: s.activation(
                    out=G2b[:, mo * 4096 + tt * 512: mo * 4096 + (tt + 1) * 512],
                    in_=PS[bank2][:, :], func=AF.Identity, bias=bk_sb[:, mo:mo + 1]),
                    deps=[tok], tag=f"evk{tt},{mo}")

        proj_done = [t_evq[(tt, mo)] for tt in range(NTT) for mo in range(NCHK)] + \
                    [t_evk[(tt, mo)] for tt in range(NTT) for mo in range(NCHK)]

        qs = G4f[0:16, 0:4096]
        e_bf = G4b[0:16, 12288:16384]
        c_m = G4f[0:16, 4100:4101]
        c_negm = G4f[0:16, 4101:4102]
        c_den = G4f[0:16, 4102:4103]
        npart = G4f[:, 4104:4112]
        n_all = G4f[:, 4112:4120]
        pool2 = G4f[:, 4120:4136]
        poolCq = poolCq_s.ap()
        poolCk = poolCk_s.ap()
        nq_tr = G4f[0:8, 4160:4288]
        pldA = G4f[0:16, 4288:4360]
        pldB = G4f[0:16, 4360:4432]
        pld = G4f[0:16, 4432:4504]
        pooled_hm = G4f[0:16, 4504:4568]
        c_eA = G4f[0:16, 4568:4569]
        c_eB = G4f[0:16, 4569:4570]
        c_M2 = G4f[0:16, 4570:4571]
        scrP = G4f[0:16, 4576:4640]
        pooledT_s = G4f[0:64, 4640:4656]

        def pool_phase(which, wab, bias_s, src_b, poolC, pre_deps):
            t_sc = {}
            t_ev = {}
            for tt in range(NTT):
                deps = list(pre_deps) + [t_wqald, t_wkald, t_bqa, t_bka]
                if tt >= 1:
                    deps.append(t_ev[tt - 1])
                tok = None
                for ki in range(NCHK):
                    tok = T(lambda t, tt=tt, ki=ki: t.matmul(
                        PS[4][0:16, :],
                        wab[:, ki * H:(ki + 1) * H],
                        src_b[:, ki * 4096 + tt * 512: ki * 4096 + (tt + 1) * 512],
                        start=(ki == 0), stop=(ki == NCHK - 1)),
                        deps=deps if ki == 0 else [], tag=f"sc{which}{tt}")
                t_sc[tt] = tok
                t_ev[tt] = SC(lambda s, tt=tt: s.activation(
                    out=qs[:, tt * 512:(tt + 1) * 512], in_=PS[4][0:16, :],
                    func=AF.Identity, bias=bias_s[:, :], scale=SCALE_QK),
                    deps=[tok], tag=f"sev{which}{tt}")

            ev_all = [t_ev[tt] for tt in range(NTT)]
            t_max = V(lambda v: v.tensor_reduce(c_m, qs, AX.X, OP.max), deps=ev_all, tag="smax")
            t_neg = V(lambda v: v.tensor_scalar(c_negm, c_m, -1.0, None, OP.mult), deps=[t_max])
            t_exp = SC(lambda s: s.activation(out=qs, in_=qs, func=AF.Exp,
                                              bias=c_negm[:, :], accum_out=c_den),
                       deps=[t_neg], tag="sexp")
            t_ebf = V(lambda v: v.tensor_copy(e_bf, qs), deps=[t_exp], tag="ebf")

            t_stt = {}
            t_nred = []
            for ci in range(NCHK):
                for tt in range(NTT):
                    bank = 5 + (tt % 2)
                    depb = [t_ebf, t_maps]
                    idx = ci * NTT + tt
                    if idx >= 2:
                        pi = idx - 2
                        depb.append(t_stt[(pi // NTT, pi % NTT)])
                    t_bc = T(lambda t, ci=ci, tt=tt, bank=bank: t.matmul(
                        PS[bank][:, :], maps16[:, ci * 128:(ci + 1) * 128],
                        e_bf[:, tt * 512:(tt + 1) * 512], start=True, stop=True),
                        deps=depb, tag=f"ebc{ci},{tt}")
                    t_stt[(ci, tt)] = V(lambda v, ci=ci, tt=tt, bank=bank: v.scalar_tensor_tensor(
                        out=G3b[:, 8192 + (tt % 4) * 512:8192 + (tt % 4 + 1) * 512],
                        in0=src_b[:, ci * 4096 + tt * 512: ci * 4096 + (tt + 1) * 512],
                        scalar=1.0, in1=PS[bank][:, :], op0=OP.mult, op1=OP.mult,
                        accum_out=npart[:, tt:tt + 1]),
                        deps=[t_bc], tag=f"stt{ci},{tt}")
                t_nred.append(V(lambda v, ci=ci: v.tensor_reduce(
                    n_all[:, ci:ci + 1], npart, AX.X, OP.add),
                    deps=[t_stt[(ci, tt)] for tt in range(NTT)], tag=f"nred{ci}"))

            t_ntr = T(lambda t: t.transpose(PS[7][0:8, 0:128], n_all, identf[:]),
                      deps=t_nred + [t_idf], tag="ntr")
            t_ntc = V(lambda v: v.tensor_copy(nq_tr, PS[7][0:8, 0:128]), deps=[t_ntr])
            t_nst = QS(lambda s: s.dma_start(out=d_nq[:, :], in_=nq_tr), deps=[t_ntc])
            t_nld = QS(lambda s: s.dma_start(
                out=pooled_hm, in_=d_nq.rearrange("a b -> (a b)").rearrange("(h d) -> h d", h=16)),
                deps=[t_nst], tag="nld")

            p1 = V(lambda v: v.tensor_copy(pld[:, 0:64], pooled_hm), deps=[t_nld])
            p2 = V(lambda v: v.tensor_copy(pld[:, 64:65], c_m), deps=[t_max, p1])
            p3 = V(lambda v: v.tensor_copy(pld[:, 65:66], c_den), deps=[t_exp, p2])
            p4 = V(lambda v: v.memset(pld[:, 66:72], 0.0), deps=[p3])
            t_pst = QS(lambda s: s.dma_start(out=d_pin[which][:, :], in_=pld), deps=[p1, p2, p3, p4])
            t_cc = CC("AllGather", OP.bypass, [d_pin[which][:, :]], [d_pout[which][:, :]],
                      deps=[t_pst], tag=f"agpool{which}")
            gA = QG(lambda g: g.indirect_dma_start(
                pldA, None, d_pout[which][:, :],
                bass.IndirectOffsetOnAxis(ap=poolA_i[:, :], axis=0)), deps=[t_cc])
            gB = QG(lambda g: g.indirect_dma_start(
                pldB, None, d_pout[which][:, :],
                bass.IndirectOffsetOnAxis(ap=poolB_i[:, :], axis=0)), deps=[t_cc])

            m1 = V(lambda v: v.tensor_tensor(c_M2, pldA[:, 64:65], pldB[:, 64:65], OP.max), deps=[gA, gB])
            m2 = V(lambda v: v.tensor_scalar(c_M2, c_M2, -1.0, None, OP.mult), deps=[m1])
            e1 = SC(lambda s: s.activation(out=c_eA, in_=pldA[:, 64:65], func=AF.Exp, bias=c_M2[:, :]), deps=[m2])
            e2 = SC(lambda s: s.activation(out=c_eB, in_=pldB[:, 64:65], func=AF.Exp, bias=c_M2[:, :]), deps=[m2, e1])
            d1 = V(lambda v: v.tensor_scalar(c_den, pldA[:, 65:66], c_eA[:, :], None, OP.mult), deps=[e1])
            d2 = V(lambda v: v.scalar_tensor_tensor(c_den, pldB[:, 65:66], c_eB[:, :], c_den, OP.mult, OP.add), deps=[d1, e2])
            n1 = V(lambda v: v.tensor_scalar(scrP, pldA[:, 0:64], c_eA[:, :], None, OP.mult), deps=[e1, d2])
            n2 = V(lambda v: v.scalar_tensor_tensor(pooled_hm, pldB[:, 0:64], c_eB[:, :], scrP, OP.mult, OP.add), deps=[n1])
            r1_ = V(lambda v: v.reciprocal(c_den, c_den), deps=[d2, n2])
            pm = V(lambda v: v.tensor_scalar(pooled_hm, pooled_hm, c_den[:, :], None, OP.mult), deps=[r1_], tag="pooled")

            tr1 = T(lambda t: t.transpose(PS[7][0:64, 0:16], pooled_hm, identf[0:16, 0:16]), deps=[pm], tag="ptr")
            tc1 = V(lambda v: v.tensor_copy(pooledT_s, PS[7][0:64, 0:16]), deps=[tr1])
            tr2 = T(lambda t: t.matmul(PS[7][:, 0:16], dup64[:], pooledT_s, start=True, stop=True), deps=[tc1, t_dup], tag="pdup")
            tc2 = V(lambda v: v.tensor_copy(pool2, PS[7][:, 0:16]), deps=[tr2])
            f1 = V(lambda v: v.tensor_copy(poolC[0:64, :], pool2[0:64, 0:16:2]), deps=[tc2])
            f2 = V(lambda v: v.tensor_copy(poolC[64:128, :], pool2[64:128, 1:16:2]), deps=[tc2, f1], tag="poolC")
            return f2

        t_qpool = pool_phase(0, wqa_b, bqa_s, G1b, poolCq, proj_done)

        t_mqk = []
        for ci in range(NCHK):
            t_mqk.append(V(lambda v, ci=ci: v.tensor_scalar(
                G2b[:, ci * 4096:(ci + 1) * 4096], G2b[:, ci * 4096:(ci + 1) * 4096],
                poolCq[:, ci:ci + 1], None, OP.mult), deps=[t_qpool], tag=f"mqk{ci}"))

        t_kpool = pool_phase(1, wka_b, bka_s, G2b, poolCk, t_mqk)

        t_wv = []
        for ci in range(NCHK):
            t_wv.append(V(lambda v, ci=ci: v.tensor_scalar(
                G2b[:, ci * 4096:(ci + 1) * 4096], G1b[:, ci * 4096:(ci + 1) * 4096],
                poolCk[:, ci:ci + 1], None, OP.mult), deps=[t_kpool], tag=f"wv{ci}"))

        t_wtld = QG(lambda g: g.dma_start(out=G4b[:, 0:8192].rearrange("p (k o) -> p k o", o=D),
                                          in_=wt.rearrange("(k p) o -> p k o", p=128)),
                    deps=[t_kpool], tag="wtld")

        t_yev = {}
        for tt in range(NTT):
            for mo in range(NCHK):
                bank = mo % 2
                deps = [t_wtld] + t_wv
                if (tt, mo) not in [(0, 0), (0, 1)]:
                    prev = (tt, mo - 2) if mo >= 2 else (tt - 1, NCHK - 2 + mo)
                    deps.append(t_yev[prev])
                tok = None
                for ki in range(NCHK):
                    tok = T(lambda t, tt=tt, mo=mo, ki=ki, bank=bank: t.matmul(
                        PS[bank][:, :],
                        G4b[:, ki * 1024 + mo * 128: ki * 1024 + (mo + 1) * 128],
                        G2b[:, ki * 4096 + tt * 512: ki * 4096 + (tt + 1) * 512],
                        start=(ki == 0), stop=(ki == NCHK - 1)),
                        deps=deps if ki == 0 else [], tag=f"wt{tt},{mo}")
                t_yev[(tt, mo)] = V(lambda v, tt=tt, mo=mo, bank=bank: v.scalar_tensor_tensor(
                    out=G1b[:, mo * 4096 + tt * 512: mo * 4096 + (tt + 1) * 512],
                    in0=PS[bank][:, :], scalar=bt_sb[:, mo:mo + 1],
                    in1=G1b[:, mo * 4096 + tt * 512: mo * 4096 + (tt + 1) * 512],
                    op0=OP.add, op1=OP.add), deps=[tok], tag=f"yev{tt},{mo}")

        t_ypst = QS(lambda s: s.dma_start(
            out=d_ypre.rearrange("(c p) t -> p c t", p=128), in_=G1b[:, :].rearrange("p (c t) -> p c t", t=TCORE)),
            deps=[t_yev[(tt, mo)] for tt in range(NTT) for mo in range(NCHK)], tag="ypst")

        YPo = [0, 2048]; XCo = [4096, 6144]; X1o = [8192, 9216]
        t_z16 = V(lambda v: v.memset(stats16[:, :], 0.0),
                  deps=[t_yev[(NTT - 1, NCHK - 1)]], tag="z16")
        colP = [[colA, colB, colC, colD, colE, colF, colG, colH],
                [cols2[:, i:i + 1] for i in range(8)]]
        bn6P_ = [bn6, bn6b]; mv2P_ = [mv2, mv2b]; st8P = [stat8t, stat8tb]
        t_lnst = []
        t_x1st = []
        t_sgn = []
        lhist = {}

        def ln_chunk(tc):
            pb = tc % 2
            yp = G3b[:, YPo[pb]:YPo[pb] + 1024]
            xc = G3b[:, XCo[pb]:XCo[pb] + 1024]
            x1t = G3b[:, X1o[pb]:X1o[pb] + 1024]
            cA, cB, cC, cD, cE, cF, cG, cH = colP[pb]
            bnb, mvb, s8t = bn6P_[pb], mv2P_[pb], st8P[pb]
            pv = lhist.get(tc - 2, {})
            depl = [t_ypst]
            if tc >= 2:
                depl.append(t_sgn[tc - 2])
            l1 = QS(lambda s, tc=tc, yp=yp: s.dma_start(
                out=yp, in_=d_ypre[:, tc * 128:(tc + 1) * 128], transpose=True),
                deps=depl, tag=f"ypl{tc}")
            l2 = QG(lambda s, tc=tc, xc=xc: s.dma_start(
                out=xc, in_=d_xtm[tc * 128:(tc + 1) * 128, :]), deps=depl, tag=f"xcl{tc}")
            a1 = V(lambda v, yp=yp, xc=xc: v.tensor_tensor(yp, yp, xc, OP.add), deps=[l1, l2], tag=f"lnadd{tc}")
            a2 = V(lambda v, yp=yp, bnb=bnb: v.bn_stats(bnb[:, 0:6], yp[:, 0:512]), deps=[a1])
            a3 = V(lambda v, yp=yp, bnb=bnb: v.bn_stats(bnb[:, 6:12], yp[:, 512:1024]), deps=[a2])
            a4 = V(lambda v, bnb=bnb, mvb=mvb: v.bn_aggr(mvb[:], bnb[:]), deps=[a3], tag=f"bn{tc}")
            a5 = V(lambda v, yp=yp, cC=cC: v.tensor_reduce(cC[:], yp, AX.X, OP.min), deps=[a4])
            a6 = V(lambda v, yp=yp, cD=cD: v.tensor_reduce(cD[:], yp, AX.X, OP.max), deps=[a5])
            a7 = V(lambda v, cA=cA, mvb=mvb: v.tensor_scalar(cA[:], mvb[:, 1:2], EPS, None, OP.add),
                   deps=[a6] + ([pv["s8"]] if pv else []))
            s1 = SC(lambda s, cA=cA, cB=cB: s.activation(out=cB[:], in_=cA[:], func=AF.Ln), deps=[a7], tag=f"ln{tc}")
            s2 = SC(lambda s, cB=cB: s.activation(out=cB[:], in_=cB[:], func=AF.Exp, scale=-0.5), deps=[s1])
            depx1 = [s2]
            if tc >= 2:
                depx1 += [t_x1st[tc - 2], t_sgn[tc - 2]]
            a8 = V(lambda v, yp=yp, x1t=x1t, cB=cB, mvb=mvb: v.tensor_scalar(
                x1t, yp, mvb[:, 0:1], cB[:, :], OP.subtract, OP.mult), deps=depx1, tag=f"x1{tc}")
            a9a = V(lambda v, cA=cA, cE=cE: v.reciprocal(cE[:], cA[:]), deps=[a8])
            a9 = V(lambda v, cE=cE, mvb=mvb: v.tensor_tensor(cE[:], cE[:], mvb[:, 1:2], OP.mult), deps=[a9a])
            s3 = SC(lambda s, cE=cE, cF=cF: s.activation(out=cF[:], in_=cE[:], func=AF.Ln, scale=1024.0 / 1023.0), deps=[a9])
            s4 = SC(lambda s, cF=cF: s.activation(out=cF[:], in_=cF[:], func=AF.Exp, scale=0.5), deps=[s3], tag=f"std{tc}")
            s5 = SC(lambda s, cE=cE, cG=cG: s.activation(out=cG[:], in_=cE[:], func=AF.Ln, scale=1024.0), deps=[s4])
            s6 = SC(lambda s, cG=cG: s.activation(out=cG[:], in_=cG[:], func=AF.Exp, scale=0.5), deps=[s5])
            a10 = V(lambda v, cC=cC, cB=cB, mvb=mvb: v.tensor_scalar(cC[:], cC[:], mvb[:, 0:1], cB[:, :], OP.subtract, OP.mult), deps=[s2, a8])
            a11 = V(lambda v, cD=cD, cB=cB, mvb=mvb: v.tensor_scalar(cD[:], cD[:], mvb[:, 0:1], cB[:, :], OP.subtract, OP.mult), deps=[a10])
            s7 = SC(lambda s, x1t=x1t, xc=xc, cH=cH: s.activation(out=xc, in_=x1t, func=AF.Sign,
                                                    bias=col_ep[:, :], accum_out=cH[:]), deps=[a8, s6], tag=f"sp1{tc}")
            s8 = SC(lambda s, x1t=x1t, xc=xc, cA=cA: s.activation(out=xc, in_=x1t, func=AF.Sign,
                                                    bias=col_en[:, :], accum_out=cA[:]), deps=[s7], tag=f"sp2{tc}")
            t_sgn.append(s8)
            a12 = V(lambda v, cH=cH, cA=cA: v.tensor_scalar(cH[:], cH[:], cA[:, :], 1.0 / 2048.0, OP.subtract, OP.mult), deps=[s8])
            b0 = V(lambda v, s8t=s8t: v.memset(s8t[:, 0:1], 0.0), deps=[a12])
            b1_ = V(lambda v, s8t=s8t, cF=cF: v.tensor_copy(s8t[:, 1:2], cF[:]), deps=[s4, b0])
            b2_ = V(lambda v, s8t=s8t, cC=cC: v.tensor_copy(s8t[:, 2:3], cC[:]), deps=[b1_])
            b3_ = V(lambda v, s8t=s8t, cD=cD: v.tensor_copy(s8t[:, 3:4], cD[:]), deps=[b2_])
            b4_ = V(lambda v, s8t=s8t, cG=cG: v.tensor_copy(s8t[:, 4:5], cG[:]), deps=[s6, b3_])
            b5_ = V(lambda v, s8t=s8t, cH=cH: v.tensor_copy(s8t[:, 5:6], cH[:]), deps=[b4_])
            b6_ = V(lambda v, s8t=s8t: v.memset(s8t[:, 6:8], 0.0), deps=[b5_])
            tr = T(lambda t, s8t=s8t, pb=pb: t.transpose(
                PS[7].bitcast(BF16)[0:8, pb * 128:(pb + 1) * 128], s8t[:], identb[:]),
                deps=[b6_, t_idb] + ([pv["te"]] if pv else []), tag=f"sttr{tc}")
            te = V(lambda v, tc=tc, pb=pb: v.tensor_copy(stats16[0:8, tc * 128:(tc + 1) * 128],
                                                  PS[7].bitcast(BF16)[0:8, pb * 128:(pb + 1) * 128]), deps=[tr, t_z16])
            t_lnst.append(te)
            st = QS(lambda s, tc=tc, x1t=x1t: s.dma_start(
                out=d_x1tm[tc * 128:(tc + 1) * 128, :], in_=x1t), deps=[a8, a12], tag=f"x1st{tc}")
            t_x1st.append(st)
            lhist[tc] = dict(s8=s8, te=te)

        t_r1w = QG(lambda g: g.dma_start(
            out=G2b[:, 0:4096].rearrange("p (k m) -> p k m", m=512),
            in_=r1wt[0:1024, :].rearrange("(k p) m -> p k m", p=128)),
            deps=[t_ypst], tag="r1w")
        t_r1wt2 = QG(lambda g: g.dma_start(
            out=G2b[0:16, 4096:4608], in_=r1wt[1024:1040, :]), deps=[t_r1w], tag="r1wt2")
        t_r2w = QG(lambda g: g.dma_start(
            out=r2w_b.ap().rearrange("p (k e) -> p k e", e=E),
            in_=r2wt.rearrange("(k p) e -> p k e", p=128)), deps=[], tag="r2w")
        t_x1fm_t = {}

        def x1fm_tile(tt):
            for ci in range(NCHK):
                t_x1fm_t[(ci, tt)] = QV(lambda s, ci=ci, tt=tt: s.dma_start(
                    out=G1b[:, ci * 4096 + tt * 512: ci * 4096 + (tt + 1) * 512],
                    in_=d_x1tm[tt * 512:(tt + 1) * 512, ci * 128:(ci + 1) * 128], transpose=True),
                    deps=([t_ypst] + [t_x1st[4 * tt + j] for j in range(4)]) if ci == 0 else [],
                    tag=f"x1fm{ci},{tt}")

        t_hev = {}
        hstate = dict(n=0, order=[])

        def h_tile(tt):
            for mo2 in range(4):
                bank = hstate["n"] % 2
                deps = [t_x1fm_t[(ki2, tt)] for ki2 in range(NCHK)] + \
                    [t_r1w, t_r1wt2, t_z16] + [t_lnst[4 * tt + j] for j in range(4)]
                if hstate["n"] >= 2:
                    deps.append(t_hev[hstate["order"][hstate["n"] - 2]])
                tok = None
                for ki in range(NCHK):
                    tok = T(lambda t, mo2=mo2, tt=tt, ki=ki, bank=bank: t.matmul(
                        PS[bank][:, :],
                        G2b[:, ki * 512 + mo2 * 128: ki * 512 + (mo2 + 1) * 128],
                        G1b[:, ki * 4096 + tt * 512: ki * 4096 + (tt + 1) * 512],
                        start=(ki == 0), stop=False),
                        deps=deps if ki == 0 else [], tag=f"r1m{mo2},{tt}")
                tok = T(lambda t, mo2=mo2, tt=tt, bank=bank: t.matmul(
                    PS[bank][:, :],
                    G2b[0:16, 4096 + mo2 * 128: 4096 + (mo2 + 1) * 128],
                    stats16[0:16, tt * 512:(tt + 1) * 512],
                    start=False, stop=True), tag=f"r1s{mo2},{tt}")
                t_hev[(mo2, tt)] = SC(lambda s, mo2=mo2, tt=tt, bank=bank: s.activation(
                    out=G4b[:, mo2 * 4096 + tt * 512: mo2 * 4096 + (tt + 1) * 512],
                    in_=PS[bank][:, :], func=AF.Gelu, bias=r1b_sb[:, mo2:mo2 + 1]),
                    deps=[tok], tag=f"hev{mo2},{tt}")
                hstate["order"].append((mo2, tt))
                hstate["n"] += 1

        lgt = G2f[0:32, 2304:6400]
        t_lgt = {}

        def lgt_tile(tt):
            deps = [t_r2w] + [t_hev[(k2, tt)] for k2 in range(4)]
            if tt >= 1:
                deps.append(t_lgt[tt - 1])
            tok = None
            for k2 in range(4):
                tok = T(lambda t, tt=tt, k2=k2: t.matmul(
                    PS[2][0:32, :],
                    r2w_b[:, k2 * E:(k2 + 1) * E],
                    G4b[:, k2 * 4096 + tt * 512: k2 * 4096 + (tt + 1) * 512],
                    start=(k2 == 0), stop=(k2 == 3)),
                    deps=deps if k2 == 0 else [], tag=f"r2m{tt}")
            t_lgt[tt] = V(lambda v, tt=tt: v.tensor_copy(
                lgt[:, tt * 512:(tt + 1) * 512], PS[2][0:32, :]), deps=[tok], tag=f"lev{tt}")

        sem_b = G2f[0:32, 12288:16384]
        ptm = G3f[:, 512:544]
        stile = G3f[:, 544:576]
        rtile = G3f[:, 576:608]
        mx8c = G3f[:, 608:616]
        ix8c = G3f[:, 616:624].bitcast(U32)[:, 0:8]
        sm_a = cols3[:, 0:1]
        sm_b = cols3[:, 1:2]
        sm_c = cols3[:, 2:3]
        t_sem = []
        t_probs = []
        pstate = dict(prev=None)

        def probs_chunk(tc):
            prev = pstate["prev"]
            deps = [t_lgt[tc // 4]]
            if prev is not None:
                deps.append(prev)
            tr = T(lambda t, tc=tc: t.transpose(
                PS[3][:, 0:32], lgt[:, tc * 128:(tc + 1) * 128], identf[0:32, 0:32]),
                deps=deps + [t_idf], tag=f"ltr{tc}")
            ev = V(lambda v: v.tensor_copy(ptm, PS[3][:, 0:32]), deps=[tr])
            r1_ = V(lambda v: v.tensor_reduce(sm_a[:], ptm, AX.X, OP.max), deps=[ev])
            r2_ = V(lambda v: v.tensor_scalar(sm_a[:], sm_a[:], -1.0, None, OP.mult), deps=[r1_])
            ex = SC(lambda s: s.activation(out=ptm, in_=ptm, func=AF.Exp,
                                           bias=sm_a[:, :], accum_out=sm_b[:]), deps=[r2_], tag=f"pexp{tc}")
            rc = V(lambda v: v.reciprocal(sm_b[:], sm_b[:]), deps=[ex])
            pn = V(lambda v, tc=tc: v.tensor_scalar(
                probs_s[:, tc * E:(tc + 1) * E], ptm, sm_b[:, :], None, OP.mult),
                deps=[rc], tag=f"probs{tc}")
            t_probs.append(pn)
            m8 = V(lambda v, tc=tc: v.max(mx8c, probs_s[:, tc * E:(tc + 1) * E]), deps=[pn])
            c1_ = V(lambda v, tc=tc: v.tensor_copy(t2_s[:, tc:tc + 1], mx8c[:, 1:2]), deps=[m8])
            c2_ = V(lambda v, tc=tc: v.tensor_tensor(sm_c[:], mx8c[:, 0:1], mx8c[:, 1:2], OP.add), deps=[m8, c1_])
            c3_ = V(lambda v, tc=tc: v.reciprocal(rts_s[:, tc:tc + 1], sm_c[:]), deps=[c2_])
            i8 = V(lambda v, tc=tc: v.max_index(ix8c, mx8c, probs_s[:, tc * E:(tc + 1) * E]), deps=[m8, c3_])
            e1c = V(lambda v, tc=tc: v.tensor_copy(e12_s[:, tc * 2:tc * 2 + 1], ix8c[:, 0:1]), deps=[i8])
            e2c = V(lambda v, tc=tc: v.tensor_copy(e12_s[:, tc * 2 + 1:tc * 2 + 2], ix8c[:, 1:2]), deps=[i8, e1c])
            rr = V(lambda v, tc=tc: v.tensor_scalar(rtile, probs_s[:, tc * E:(tc + 1) * E],
                                                    t2_s[:, tc:tc + 1], None, OP.is_ge), deps=[c1_, pn, e2c])
            s1_ = V(lambda v, tc=tc: v.scalar_tensor_tensor(
                stile, probs_s[:, tc * E:(tc + 1) * E], 1.0, rtile, OP.add, OP.mult), deps=[rr])
            s2_ = V(lambda v: v.tensor_scalar(stile, stile, 1.0, None, OP.subtract), deps=[s1_])
            str_ = T(lambda t, tc=tc: t.transpose(PS[3][0:32, 0:128], stile, identf[:]),
                     deps=[s2_, tr], tag=f"str{tc}")
            sev = V(lambda v, tc=tc: v.tensor_copy(sem_b[:, tc * 128:(tc + 1) * 128],
                                                   PS[3][0:32, 0:128]), deps=[str_], tag=f"sev{tc}")
            t_sem.append(sev)
            pstate["prev"] = sev

        for tt in range(NTT):
            for j in range(4):
                ln_chunk(4 * tt + j)
            x1fm_tile(tt)
            h_tile(tt)
            lgt_tile(tt)
            for j in range(4):
                probs_chunk(4 * tt + j)
        ln_done = t_x1st + [t_lnst[-1]]
        t_agX = CC("AllGather", OP.bypass, [d_x1tm[:, :]], [d_xg[:, :]],
                   deps=list(t_x1st), tag="agX")
        t_x1fm = [t_x1fm_t[(ci, NTT - 1)] for ci in range(NCHK)]
        t_dbg_x1fm = []
        if "x1fm" in dbg:
            t_dbg_x1fm = [QS(lambda s: s.dma_start(out=dbg["x1fm"][:, :], in_=G1b[:, :]),
                             deps=list(t_x1fm), tag="dbgx1fm")]
        t_dbg_stats2 = []
        if "stats2" in dbg:
            t_dbg_stats2 = [QS(lambda s: s.dma_start(out=dbg["stats2"][:, :], in_=stats16[:, :]),
                               deps=list(t_lnst), tag="dbgstats2")]

        t_dbg_hfm = []
        if "hfm" in dbg:
            t_dbg_hfm = [QS(lambda s: s.dma_start(out=dbg["hfm"][:, :], in_=G4b[:, 0:16384]),
                            deps=[t_hev[(m2, t2)] for m2 in range(4) for t2 in range(NTT)],
                            tag="dbghfm")]
        t_dbg_lgt = []
        if "lgtt" in dbg:
            t_dbg_lgt = [QS(lambda s: s.dma_start(out=dbg["lgtt"][:, :], in_=lgt),
                            deps=[t_lgt[t2] for t2 in range(NTT)], tag="dbglgtt")]
        t_sst = QS(lambda s: s.dma_start(out=d_sem_[:, :], in_=sem_b), deps=t_sem, tag="sst")
        t_agS = CC("AllGather", OP.bypass, [d_sem_[:, :]], [d_sgem[:, :]], deps=[t_sst], tag="agS")

        svals = G3f[:, 0:1024]
        t_sv = QG(lambda g: g.indirect_dma_start(
            svals, None, d_sgem.rearrange("r (b j) -> (r b) j", j=1024),
            bass.IndirectOffsetOnAxis(ap=sgem_i[:, :], axis=0)), deps=[t_agS], tag="svgather")

        bi_lo, bi_mid = colB, colA
        z3 = V(lambda v: v.memset(bi_mid[:], 0.0), deps=[t_sv], tag="bi0")
        prev = [z3]
        cntscr = G3f[:, 2048:3072]
        for it in range(BISECT):
            q1 = V(lambda v: v.tensor_scalar(cntscr, svals, bi_mid[:, :], 0.0,
                                             OP.is_ge, OP.add, accum_out=colD[:]),
                   deps=prev, tag=f"bi{it}")
            q2 = T(lambda t: t.matmul(PS[6][0:4, 0:1], blockones[:], colD[:],
                                      start=True, stop=True), deps=[q1, t_bo], tag=f"bim{it}")
            q3 = V(lambda v: v.tensor_copy(scr4[:, 0:1], PS[6][0:4, 0:1]), deps=[q2])
            q4 = T(lambda t: t.matmul(PS[6][:, 0:1], blockT[:], scr4[:, 0:1],
                                      start=True, stop=True), deps=[q3, t_bt], tag=f"bib{it}")
            q5 = V(lambda v: v.tensor_scalar(colF[:], PS[6][:, 0:1], float(CAP), None, OP.is_ge), deps=[q4])
            w_it = 2.0 * (0.5 ** it)
            u1 = V(lambda v, w_it=w_it: v.tensor_scalar(colG[:], colF[:], 0.5, w_it, OP.subtract, OP.mult), deps=[q5])
            u2 = V(lambda v: v.tensor_tensor(bi_mid[:], bi_mid[:], colG[:], OP.add), deps=[u1])
            prev = [u2]
        w_fin = 2.0 * (0.5 ** (BISECT - 1))
        tl_ = V(lambda v: v.tensor_scalar(bi_lo[:], bi_mid[:], w_fin, None, OP.subtract), deps=prev, tag="bifin")
        tq = T(lambda t: t.matmul(PS[6][0:4, 0:1], blockones[:], bi_lo[:],
                                  start=True, stop=True), deps=[tl_], tag="thmm")
        th8 = scr4[0:4, 1:9]
        tz = V(lambda v: v.memset(scr4[:, 1:9], 0.0), deps=[tl_])
        tv = V(lambda v: v.tensor_scalar(scr4[0:4, 1:2], PS[6][0:4, 0:1], 1.0 / 32.0, None, OP.mult), deps=[tq, tz], tag="theval")
        t_thst = QS(lambda s: s.dma_start(out=d_thin[:, :], in_=th8), deps=[tv], tag="thst")
        t_agT = CC("AllGather", OP.bypass, [d_thin[:, :]], [d_thout[:, :]], deps=[t_thst], tag="agT")
        t_thld = QS(lambda s: s.dma_start(out=G3f[0:32, 3072:3080], in_=d_thout[:, :]), deps=[t_agT], tag="thld")
        th32 = G3f[0:32, 3072:3073]
        tt1 = T(lambda t: t.transpose(PS[3][0:1, 0:32], th32, identf[0:32, 0:32]), deps=[t_thld], tag="thtr")
        tt2 = V(lambda v: v.tensor_copy(G3f[0:1, 3080:3112], PS[3][0:1, 0:32]), deps=[tt1])
        tt3 = T(lambda t: t.matmul(PS[3][:, 0:32], ones_row[:], G3f[0:1, 3080:3112],
                                   start=True, stop=True), deps=[tt2], tag="thbc")
        tt4 = V(lambda v: v.tensor_copy(threp[:], PS[3][:, 0:32]), deps=[tt3], tag="threp")

        selt = G3f[:, 1024:1056]
        rloc = G3f[:, 1056:1088]
        ohot = G3f[:, 1088:1120]
        base_row = G3f[0:1, 3112:3144]
        zb = V(lambda v: v.memset(base_row, 0.0), deps=[tt4], tag="zbase")
        t_cnt_mm = None
        rl_cols = G3f[:, 3200:3264]
        sl_cols = sl2.ap()
        prev = zb
        for tc in range(NTC):
            p0 = V(lambda v, tc=tc: v.tensor_tensor(selt, probs_s[:, tc * E:(tc + 1) * E],
                                                    threp[:], OP.is_ge), deps=[prev], tag=f"sel{tc}")
            p1 = V(lambda v, tc=tc: v.tensor_scalar(rloc, probs_s[:, tc * E:(tc + 1) * E],
                                                    t2_s[:, tc:tc + 1], None, OP.is_ge), deps=[p0])
            p2 = V(lambda v: v.tensor_tensor(selt, selt, rloc, OP.mult), deps=[p1], tag=f"selm{tc}")
            p3 = V(lambda v, tc=tc: v.scalar_tensor_tensor(
                mfull_s[:, tc * E:(tc + 1) * E], probs_s[:, tc * E:(tc + 1) * E],
                rts_s[:, tc:tc + 1], selt, OP.mult, OP.mult), deps=[p2], tag=f"mf{tc}")
            t_cnt_mm = T(lambda t, tc=tc: t.matmul(PS[4][0:1, 0:32], ones_col[:], selt,
                                                   start=(tc == 0), stop=(tc == NTC - 1)),
                         deps=[p3], tag=f"cnt{tc}")
            r1m = T(lambda t: t.matmul(PS[5][:, 0:32], lstrict[:], selt,
                                       start=True, stop=False), deps=[p3, t_ls], tag=f"rk{tc}")
            r2m = T(lambda t: t.matmul(PS[5][:, 0:32], ones_row[:], base_row,
                                       start=False, stop=True), deps=[r1m], tag=f"rkb{tc}")
            b1m = T(lambda t: t.matmul(PS[6][0:1, 0:32], ones_col[:], selt,
                                       start=True, stop=True), deps=[p3, r2m], tag=f"cs{tc}")
            b2v = V(lambda v: v.tensor_tensor(base_row, base_row, PS[6][0:1, 0:32], OP.add), deps=[b1m], tag=f"bup{tc}")
            exts = []
            for par in range(2):
                o1 = V(lambda v, tc=tc, par=par: v.tensor_scalar(
                    ohot, iexp[:], e12_s[:, tc * 2 + par:tc * 2 + par + 1], None, OP.is_equal),
                    deps=[b2v] if par == 0 else [exts[-1]], tag=f"oh{tc},{par}")
                o2 = V(lambda v, tc=tc, par=par: v.scalar_tensor_tensor(
                    cntscr[:, 0:32], ohot, 1.0, PS[5][:, 0:32], OP.mult, OP.mult,
                    accum_out=rl_cols[:, tc * 2 + par:tc * 2 + par + 1]), deps=[o1], tag=f"rl{tc},{par}")
                o3 = V(lambda v, tc=tc, par=par: v.scalar_tensor_tensor(
                    cntscr[:, 0:32], ohot, 1.0, selt, OP.mult, OP.mult,
                    accum_out=sl_cols[:, tc * 2 + par:tc * 2 + par + 1]), deps=[o2])
                o4 = V(lambda v, tc=tc, par=par: v.scalar_tensor_tensor(
                    cntscr[:, 0:32], ohot, 1.0, mfull_s[:, tc * E:(tc + 1) * E], OP.mult, OP.mult,
                    accum_out=w01_s[:, tc * 2 + par:tc * 2 + par + 1]), deps=[o3], tag=f"wp{tc},{par}")
                exts.append(o4)
            prev = exts[-1]
        sel_done = prev

        cnt_ev = V(lambda v: v.tensor_copy(G3f[0:1, 3144:3176], PS[4][0:1, 0:32]), deps=[t_cnt_mm], tag="cntev")
        t_cst = QS(lambda s: s.dma_start(out=d_cnin[:, :], in_=G3f[0:1, 3144:3176]), deps=[cnt_ev])
        t_agC = CC("AllGather", OP.bypass, [d_cnin[:, :]], [d_cnout[:, :]], deps=[t_cst], tag="agC")
        t_cld = QS(lambda s: s.dma_start(out=cnt8[:], in_=d_cnout[:, :]), deps=[t_agC], tag="cld")
        g1m = T(lambda t: t.matmul(PS[6][0:1, 0:32], mask8[:], cnt8[:], start=True, stop=True),
                deps=[t_cld, t_small[6]], tag="gbase")
        g2v = V(lambda v: v.tensor_tensor(G3f[0:1, 4000:4032], eoff_s[:],
                                          PS[6][0:1, 0:32], OP.add),
                deps=[g1m, t_small[12]], tag="gbrow")
        g3m = T(lambda t: t.matmul(PS[6][:, 0:32], ones_row[:], G3f[0:1, 4000:4032],
                                   start=True, stop=True), deps=[g2v], tag="gbbc")
        gbrep = G3f[:, 3328:3360]
        g4v = V(lambda v: v.tensor_copy(gbrep, PS[6][:, 0:32]), deps=[g3m], tag="gbrep")
        limrep = G3f[:, 3264:3296]
        lm1 = V(lambda v: v.tensor_scalar(G3f[0:1, 4032:4064], eoff_s[:], float(CAP), None, OP.add),
                deps=[t_small[12], g4v], tag="limrow")
        lm2 = T(lambda t: t.matmul(PS[6][:, 0:32], ones_row[:], G3f[0:1, 4032:4064],
                                   start=True, stop=True), deps=[lm1, g4v], tag="limbc")
        lm3 = V(lambda v: v.tensor_copy(limrep, PS[6][:, 0:32]), deps=[lm2], tag="limrep")

        ZROWI = float(E * CAP)
        t_gsc = []
        prev = lm3
        for tc in range(NTC):
            for par in range(2):
                o1 = V(lambda v, tc=tc, par=par: v.tensor_scalar(
                    ohot, iexp[:], e12_s[:, tc * 2 + par:tc * 2 + par + 1], None, OP.is_equal),
                    deps=[prev], tag=f"goh{tc},{par}")
                o2 = V(lambda v: v.scalar_tensor_tensor(
                    cntscr[:, 0:32], ohot, 1.0, gbrep, OP.mult, OP.mult,
                    accum_out=colA[:]), deps=[o1])
                o3 = V(lambda v, tc=tc, par=par: v.tensor_tensor(
                    colB[:], rl_cols[:, tc * 2 + par:tc * 2 + par + 1], colA[:], OP.add), deps=[o2])
                o4b = V(lambda v: v.scalar_tensor_tensor(
                    cntscr[:, 0:32], ohot, 1.0, limrep, OP.mult, OP.mult,
                    accum_out=colE[:]), deps=[o3])
                o4c = V(lambda v: v.tensor_tensor(colF[:], colB[:], colE[:], OP.is_lt), deps=[o4b])
                o4 = V(lambda v, tc=tc: v.tensor_scalar(colC[:], itokg[:, tc:tc + 1], 0.0, ZROWI, OP.mult, OP.add), deps=[o4c])
                o5a = V(lambda v, tc=tc, par=par: v.tensor_tensor(
                    colH[:], sl_cols[:, tc * 2 + par:tc * 2 + par + 1], colF[:], OP.mult), deps=[o4])
                o5 = V(lambda v: v.tensor_scalar(
                    colD[:], colH[:], -1.0, 1.0, OP.mult, OP.add),
                    deps=[o5a])
                o6 = V(lambda v: v.tensor_tensor(colC[:], colC[:], colB[:], OP.subtract), deps=[o5])
                o7 = V(lambda v: v.scalar_tensor_tensor(colB[:], colC[:], colD[:, :], colB[:], OP.mult, OP.add), deps=[o6])
                o8 = V(lambda v, tc=tc, par=par: v.tensor_copy(
                    gsc_u[:, tc * 2 + par:tc * 2 + par + 1], colB[:]), deps=[o7], tag=f"gsc{tc},{par}")
                t_gsc.append(o8)
                prev = o8
        gsc_done = prev

        zt = V(lambda v: v.memset(G3f[:, 3360:4002], 0.0), deps=[t_sv], tag="zidx")
        t_zidx = QS(lambda s: s.dma_start(
            out=d_idx.rearrange("r c -> (r c)").rearrange("(p q) -> p q", p=128),
            in_=G3f[:, 3360:4002]), deps=[zt], tag="zidxst")
        t_scat = []
        prevs = [t_zidx, gsc_done, t_itok, t_regs, t_regs2]
        for tc in range(NTC):
            for par in range(2):
                j = tc * 2 + par
                slot8 = j % 8
                depv = list(prevs)
                if len(t_scat) >= 8:
                    depv.append(t_scat[-8])
                b1v = V(lambda v, tc=tc, slot8=slot8: v.tensor_copy(
                    scv[:, slot8 * 2:slot8 * 2 + 1], itokg[:, tc:tc + 1]), deps=depv, tag=f"svi{j}")
                b2v2 = V(lambda v, j=j, slot8=slot8: v.tensor_copy(
                    scv[:, slot8 * 2 + 1:slot8 * 2 + 2], w01_s[:, j:j + 1]), deps=[b1v], tag=f"svw{j}")
                sc_ = QG(lambda g, j=j, slot8=slot8: g.indirect_dma_start(
                    d_idx[:, :], bass.IndirectOffsetOnAxis(ap=gsc_u[:, j:j + 1], axis=0),
                    scv[:, slot8 * 2:slot8 * 2 + 2], None,
                    bounds_check=rB_scat, oob_is_err=False),
                    deps=[b2v2], tag=f"scat{j}")
                t_scat.append(sc_)
                prevs = []
        t_arI = CC("AllReduce", OP.add, [d_idx[:, :]], [d_idxg[:, :]],
                   deps=[t_scat[-1]], tag="arI")

        stageb_done = [t_arI, gsc_done, sel_done]

        t_aupl = []
        for g_ in range(NG):
            t_aupl.append(QG(lambda g, g_=g_: g.dma_start(
                out=G2b[:, g_ * 2048:(g_ + 1) * 2048].rearrange("p (k j) -> p k j", j=256),
                in_=aupt[g_].rearrange("(k p) j -> p k j", p=128)),
                deps=[t_sst, t_lgt[NTT - 1]] + t_dbg_stats2 + t_dbg_lgt, tag=f"aupl{g_}"))
        t_adnl = []
        for g_ in range(NG):
            t_adnl.append(QG(lambda g, g_=g_: g.dma_start(
                out=G4b[:, g_ * 1024:(g_ + 1) * 1024],
                in_=adnt[g_]), deps=[t_lgt[NTT - 1], sel_done] + t_dbg_hfm, tag=f"adnl{g_}"))

        gwtmp = G3f[:, 1024:1040]
        t_gw = []
        prevg = [sel_done, t_oh16]
        for tc in range(NTC):
            w1 = V(lambda v, tc=tc: v.tensor_tensor(
                gwtmp, mfull_s[:, tc * E:(tc + 1) * E:2], mfull_s[:, tc * E + 1:(tc + 1) * E:2], OP.add),
                deps=prevg, tag=f"gw{tc}")
            w2 = V(lambda v: v.tensor_scalar(gwtmp, gwtmp, 0.05, None, OP.mult), deps=[w1])
            w3 = T(lambda t: t.transpose(PS[7][0:16, 0:128], gwtmp, identf[:]), deps=[w2], tag=f"gwt{tc}")
            w4 = V(lambda v, tc=tc: v.tensor_copy(gwT_b[0:16, tc * 128:(tc + 1) * 128], PS[7][0:16, 0:128]), deps=[w3])
            prevg = [w4]
            t_gw.append(w4)

        gwsb = G3b[:, 0:4096]
        silsb = G3b[:, 4096:4352]
        ahp = G3b[:, 4352:4608]
        adt0 = G3b[:, 4608:5632]
        probs_bf = probs_s.ap().bitcast(BF16)
        mfull_bf = mfull_s.ap().bitcast(BF16)
        t_ad_st = []
        t_up_last = None
        t_dn_last = None
        prev_gwev = [t_gw[-1], lm3]
        prev_dn_ev = None
        for sg in range(16):
            gw_evs = []
            for g_ in range(16):
                mmgw = T(lambda t, sg=sg, g_=g_: t.matmul(
                    PS[6][:, 0:256], oh16[:, g_ * 128:(g_ + 1) * 128],
                    gwT_b[0:16, sg * 256:(sg + 1) * 256], start=True, stop=True),
                    deps=prev_gwev + [t_gw[-1]], tag=f"gwm{sg},{g_}")
                ev = V(lambda v, g_=g_: v.tensor_copy(gwsb[:, g_ * 256:(g_ + 1) * 256], PS[6][:, 0:256]),
                       deps=[mmgw], tag=f"gwe{sg},{g_}")
                prev_gwev = [ev]
                gw_evs.append(ev)
            for g_ in range(16):
                depg = [t_aupl[g_], t_x1fm[-1] if False else t_x1fm[0], gw_evs[g_]]
                depg += t_x1fm
                if t_up_last is not None:
                    depg.append(t_up_last)
                tokg = None
                for ki in range(NCHK):
                    tokg = T(lambda t, sg=sg, g_=g_, ki=ki: t.matmul(
                        PS[0][:, 0:256],
                        G2b[:, g_ * 2048 + ki * 256: g_ * 2048 + ki * 256 + 128],
                        G1b[:, ki * 4096 + sg * 256: ki * 4096 + (sg + 1) * 256],
                        start=(ki == 0), stop=(ki == NCHK - 1)),
                        deps=depg if ki == 0 else [], tag=f"aug{sg},{g_}")
                tokv = None
                for ki in range(NCHK):
                    tokv = T(lambda t, sg=sg, g_=g_, ki=ki: t.matmul(
                        PS[1][:, 0:256],
                        G2b[:, g_ * 2048 + ki * 256 + 128: g_ * 2048 + ki * 256 + 256],
                        G1b[:, ki * 4096 + sg * 256: ki * 4096 + (sg + 1) * 256],
                        start=(ki == 0), stop=(ki == NCHK - 1)), tag=f"auv{sg},{g_}")
                s_sil = SC(lambda s: s.activation(out=silsb, in_=PS[0][:, 0:256], func=AF.Silu),
                           deps=[tokg], tag=f"asil{sg},{g_}")
                p1 = V(lambda v: v.tensor_tensor(ahp, silsb, PS[1][:, 0:256], OP.mult),
                       deps=[s_sil, tokv], tag=f"ap1{sg},{g_}")
                p2 = V(lambda v, g_=g_: v.tensor_tensor(ahp, ahp, gwsb[:, g_ * 256:(g_ + 1) * 256], OP.mult),
                       deps=[p1, gw_evs[g_]], tag=f"ap2{sg},{g_}")
                t_up_last = p2
                dn_dep = [p2, t_adnl[g_]]
                if g_ == 0 and prev_dn_ev is not None:
                    dn_dep.append(prev_dn_ev)
                tokd = None
                for tg in range(2):
                    for nt in range(2):
                        tokd = T(lambda t, sg=sg, g_=g_, tg=tg, nt=nt: t.matmul(
                            PS[2 + tg * 2 + nt][:, 0:512],
                            ahp[:, tg * 128:(tg + 1) * 128],
                            G4b[:, g_ * 1024 + nt * 512: g_ * 1024 + (nt + 1) * 512],
                            start=(g_ == 0), stop=(g_ == NG - 1)),
                            deps=dn_dep if (tg == 0 and nt == 0) else [], tag=f"adn{sg},{g_}")
                t_dn_last = tokd
            evs = []
            for tg in range(2):
                e1_ = V(lambda v, tg=tg: v.tensor_copy(adt0[:, 0:512], PS[2 + tg * 2][:, 0:512]),
                        deps=[t_dn_last] + evs, tag=f"ade{sg},{tg}a")
                e2_ = V(lambda v, tg=tg: v.tensor_copy(adt0[:, 512:1024], PS[2 + tg * 2 + 1][:, 0:512]),
                        deps=[e1_], tag=f"ade{sg},{tg}b")
                st_ = QS(lambda s, sg=sg, tg=tg: s.dma_start(
                    out=d_ad[sg * 256 + tg * 128: sg * 256 + (tg + 1) * 128, :], in_=adt0),
                    deps=[e2_], tag=f"adst{sg},{tg}")
                evs = [st_]
                t_ad_st.append(st_)
            prev_dn_ev = t_ad_st[-1]
        adapters_done = [t_ad_st[-1], t_up_last]

        ST3 = [(0, 512), (512, 512), (1024, 256)]
        t_yw_st = []
        prev_upmm = adapters_done[1]
        prev_dnev = None
        prev_xe_tp = {}
        prev_gather_store = None
        t_wupl = {}
        t_wdnl = {}
        t_idsT = {}
        expert_wst = {}

        t_stage_sx = {}

        def declare_stage(k):
            stage_region = ((probs_bf[:, 0:1024], probs_bf[:, 1024:2048]) if k == 0
                            else (G3b[:, 0:1024], G3b[:, 1024:2048]))
            slab4 = mfull_s.ap()[0:10, 0:256] if k == 0 else G3f[0:10, 1024:1280]
            ids10 = mfull_s.ap()[0:10, 256:384] if k == 0 else G3f[0:10, 1280:1408]
            depsl = [t_arI, t_idxl4[k], tq, t_gw[-1]]
            if k == 0:
                depsl += [sel_done]
            else:
                depsl += [gsc_done]
            if k >= 1:
                depsl.append(t_idsT[k - 1])
            gsl = QG(lambda g, k=k, slab4=slab4: g.indirect_dma_start(
                slab4, None, d_idxg.rearrange("r c -> (r c)").rearrange("(b q) -> b q", q=256),
                bass.IndirectOffsetOnAxis(ap=idxl4[k][:, :], axis=0)), deps=depsl, tag=f"slab{k}")
            c1_ = V(lambda v, slab4=slab4, ids10=ids10: v.tensor_copy(ids10, slab4[:, 0:256:2]), deps=[gsl], tag=f"ids{k}")
            trk = T(lambda t, ids10=ids10: t.transpose(PS[7][:, 0:10], ids10, identf[0:10, 0:10]),
                    deps=[c1_], tag=f"idtr{k}")
            t_idsT[k] = V(lambda v, k=k: v.tensor_copy(idsT[:, k * 10:(k + 1) * 10], PS[7][:, 0:10]),
                          deps=[trk], tag=f"idsT{k}")
            t_sx = []
            for gi in range(10):
                stg = stage_region[gi % 2]
                depg = [t_idsT[k], t_agX]
                if gi >= 2:
                    depg.append(t_sx[gi - 2])
                if gi < 2 and k >= 2:
                    depg.append(t_tp[k - 2][-1])
                gx = QG(lambda g, k=k, gi=gi, stg=stg: g.indirect_dma_start(
                    stg, None, d_xg[:, :],
                    bass.IndirectOffsetOnAxis(ap=idsT[:, k * 10 + gi:k * 10 + gi + 1], axis=0)),
                    deps=depg, tag=f"xeg{k},{gi}")
                sx = QG(lambda s, k=k, gi=gi, stg=stg: s.dma_start(
                    out=d_xe[k % 2][gi * 128:(gi + 1) * 128, :], in_=stg), deps=[gx], tag=f"xes{k},{gi}")
                t_sx.append(sx)
            t_stage_sx[k] = t_sx
            if k == 0 and "xe0" in dbg:
                ln_done.append(QS(lambda s: s.dma_start(out=dbg["xe0"][:, :], in_=d_xe[0][:, :]),
                                  deps=list(t_sx), tag="dbgxe0"))
            if k == 2 and "xe2" in dbg:
                ln_done.append(QS(lambda s: s.dma_start(out=dbg["xe2"][:, :], in_=d_xe[0][:, :]),
                                  deps=list(t_sx), tag="dbgxe2"))

        t_tp = {}
        dbg_gate = {}

        def declare_tp(k, extra_deps):
            tps = []
            for ci in range(NCHK):
                dept = list(t_stage_sx[k]) + list(extra_deps) if ci == 0 else []
                if ci == 0 and (k - 1) in dbg_gate:
                    dept += dbg_gate[k - 1]
                tx = QV(lambda s, k=k, ci=ci: s.dma_start(
                    out=G2b[:, ci * 1280:(ci + 1) * 1280],
                    in_=d_xe[k % 2][:, ci * 128:(ci + 1) * 128], transpose=True),
                    deps=dept, tag=f"xet{k},{ci}")
                tps.append(tx)
            t_tp[k] = tps
            if k == 2 and "xef2" in dbg:
                tk = QS(lambda s: s.dma_start(out=dbg["xef2"][:, :], in_=G2b[:, 0:10240]),
                        deps=list(tps), tag="dbgxef2")
                dbg_gate[2] = dbg_gate.get(2, []) + [tk]
                ln_done.append(tk)
            return tps

        declare_stage(0)
        declare_stage(1)
        declare_tp(0, [adapters_done[1]])
        t_agYk = {}

        for k in range(ELOC):
            t_xe_tp = t_tp[k]
            if k == 0:
                t_wupl[0] = QG(lambda g: g.dma_start(
                    out=G1b[:, :].rearrange("p (a j) -> p a j", j=4096),
                    in_=wupt[0].rearrange("(a p) j -> p a j", p=128)),
                    deps=[adapters_done[1]] + t_dbg_x1fm, tag="wup0")
                t_wdnl[0] = QG(lambda g: g.dma_start(
                    out=G4b[:, :].rearrange("p (a j) -> p a j", j=1024),
                    in_=wdnt[0].rearrange("(a p) j -> p a j", p=128)),
                    deps=adapters_done, tag="wdn0")

            t_silu = {}
            hact_done = []
            for jo2 in range(16):
                depu = [t_wupl[k]] + t_xe_tp
                if prev_upmm is not None:
                    depu.append(prev_upmm)
                if jo2 == 0:
                    depu.append(t_ad_st[-1] if k == 0 else prev_dnev)
                    depu += dbg_gate.get(("h", k - 1), [])
                tokg = None
                for ki in range(NCHK):
                    for sti, (so, sn) in enumerate(ST3):
                        tokg = T(lambda t, jo2=jo2, ki=ki, so=so, sn=sn, sti=sti: t.matmul(
                            PS[sti][:, 0:sn],
                            G1b[:, ki * 4096 + jo2 * 128: ki * 4096 + (jo2 + 1) * 128],
                            G2b[:, ki * 1280 + so: ki * 1280 + so + sn],
                            start=(ki == 0), stop=(ki == NCHK - 1)),
                            deps=depu if (ki == 0 and sti == 0) else [], tag=f"eu{k},{jo2},{ki},{sti}")
                sil3 = []
                for sti, (so, sn) in enumerate(ST3):
                    sil3.append(SC(lambda s, so=so, sn=sn, sti=sti: s.activation(
                        out=G2b[:, 10240 + so: 10240 + so + sn], in_=PS[sti][:, 0:sn], func=AF.Silu),
                        deps=[tokg], tag=f"esil{k},{jo2},{sti}"))
                tokv = None
                for ki in range(NCHK):
                    for sti, (so, sn) in enumerate(ST3):
                        tokv = T(lambda t, jo2=jo2, ki=ki, so=so, sn=sn, sti=sti: t.matmul(
                            PS[3 + sti][:, 0:sn],
                            G1b[:, ki * 4096 + (16 + jo2) * 128: ki * 4096 + (17 + jo2) * 128],
                            G2b[:, ki * 1280 + so: ki * 1280 + so + sn],
                            start=(ki == 0), stop=(ki == NCHK - 1)),
                            deps=[sil3[-1]] if (ki == 0 and sti == 0) else [], tag=f"ev{k},{jo2},{ki},{sti}")
                hp = []
                for sti, (so, sn) in enumerate(ST3):
                    hp.append(V(lambda v, jo2=jo2, so=so, sn=sn, sti=sti: v.tensor_tensor(
                        G2b[:, 11520 + jo2 * 1280 + so: 11520 + jo2 * 1280 + so + sn],
                        G2b[:, 10240 + so: 10240 + so + sn],
                        PS[3 + sti][:, 0:sn], OP.mult),
                        deps=[tokv] + sil3 if sti == 0 else [], tag=f"eh{k},{jo2},{sti}"))
                prev_upmm = hp[-1]
                hact_done.append(hp[-1])

            if k == 2 and "hact2" in dbg:
                tk = QS(lambda s: s.dma_start(out=dbg["hact2"][:, :], in_=G2b[:, 11520:32000]),
                        deps=list(hact_done), tag="dbghact2")
                dbg_gate[("h", 2)] = [tk]
                ln_done.append(tk)

            if k + 2 < ELOC:
                declare_stage(k + 2)
            if k + 1 < ELOC:
                declare_tp(k + 1, [prev_upmm])
                t_wupl[k + 1] = QG(lambda g, k=k: g.dma_start(
                    out=G1b[:, :].rearrange("p (a j) -> p a j", j=4096),
                    in_=wupt[k + 1].rearrange("(a p) j -> p a j", p=128)),
                    deps=[prev_upmm, t_wupl[k]], tag=f"wup{k + 1}")

            ye_reg = [G3b[:, 5632:6912], G3b[:, 6912:8192]]
            sty_par = {}
            for oo in range(NCHK):
                depd = [t_wdnl[k]] + hact_done
                if prev_dnev is not None:
                    depd.append(prev_dnev)
                tokd = None
                for ki16 in range(16):
                    for sti, (so, sn) in enumerate(ST3):
                        tokd = T(lambda t, oo=oo, ki16=ki16, so=so, sn=sn, sti=sti: t.matmul(
                            PS[(oo % 2) * 3 + sti][:, 0:sn],
                            G4b[:, ki16 * 1024 + oo * 128: ki16 * 1024 + (oo + 1) * 128],
                            G2b[:, 11520 + ki16 * 1280 + so: 11520 + ki16 * 1280 + so + sn],
                            start=(ki16 == 0), stop=(ki16 == 15)),
                            deps=depd if (ki16 == 0 and sti == 0) else [], tag=f"ed{k},{oo},{ki16},{sti}")
                ev3 = []
                for sti, (so, sn) in enumerate(ST3):
                    evd = [tokd]
                    if sti == 0 and oo % 2 in sty_par:
                        evd.append(sty_par[oo % 2])
                    if sti == 0 and oo < 2 and k > 0 and t_yw_st:
                        evd.append(t_yw_st[-1])
                    ev3.append(V(lambda v, oo=oo, so=so, sn=sn, sti=sti: v.tensor_copy(
                        ye_reg[oo % 2][:, so:so + sn], PS[(oo % 2) * 3 + sti][:, 0:sn]),
                        deps=evd, tag=f"ee{k},{oo},{sti}"))
                prev_dnev = ev3[-1]
                sty = QS(lambda s, oo=oo, k=k: s.dma_start(
                    out=d_yefm[oo * 128:(oo + 1) * 128, :], in_=ye_reg[oo % 2]),
                    deps=ev3 + ([t_yw_st[-1]] if (k > 0 or oo > 1) and t_yw_st else []), tag=f"yest{k},{oo}")
                t_yw_st.append(sty)
                sty_par[oo % 2] = sty
            ywt = [G3b[:, 8192:9216], G3b[:, 9216:10240]]
            ye_stores = t_yw_st[-NCHK:]
            yws_k = []
            for gi in range(10):
                dept = list(ye_stores) if gi == 0 else []
                if gi >= 2:
                    dept.append(t_yw_st[-1])
                tl = QS(lambda s, gi=gi, k=k: s.dma_start(
                    out=ywt[gi % 2], in_=d_yefm[:, gi * 128:(gi + 1) * 128], transpose=True),
                    deps=dept, tag=f"ywt{k},{gi}")
                st_ = QS(lambda s, gi=gi, k=k: s.dma_start(
                    out=d_yw[k * CAP + gi * 128: k * CAP + (gi + 1) * 128, :], in_=ywt[gi % 2]),
                    deps=[tl], tag=f"yws{k},{gi}")
                t_yw_st.append(st_)
                yws_k.append(st_)
            t_agYk[k] = CC("AllGather", OP.bypass,
                           [d_yw[k * CAP:(k + 1) * CAP, :]],
                           [d_ywg[k * NCORE * CAP:(k + 1) * NCORE * CAP, :]],
                           deps=yws_k, tag=f"agY{k}")
            if k + 1 < ELOC:
                t_wdnl[k + 1] = QG(lambda g, k=k: g.dma_start(
                    out=G4b[:, :].rearrange("p (a j) -> p a j", j=1024),
                    in_=wdnt[k + 1].rearrange("(a p) j -> p a j", p=128)),
                    deps=[prev_dnev, t_wdnl[k]], tag=f"wdn{k + 1}")

        if "idsT" in dbg:
            ln_done.append(QS(lambda s: s.dma_start(out=dbg["idsT"][:, :], in_=idsT[:]),
                              deps=[t_idsT[kk] for kk in range(ELOC)], tag="dbgidsT"))

        zr_s = G3b[0:1, 4096:5120]
        zr1 = V(lambda v: v.memset(zr_s, 0.0), deps=[gsc_done, prev_upmm], tag="zr1")
        zr2 = QS(lambda s: s.dma_start(out=d_ywg[E * CAP:E * CAP + 1, :], in_=zr_s),
                 deps=[zr1], tag="zr2")
        ag_all = [t_agYk[kk] for kk in range(ELOC)] + [zr2]
        rg0P = [G3b[:, 0:1024], G2b[:, 0:1024]]
        rg1P = [G3b[:, 1024:2048], G2b[:, 1024:2048]]
        x1t2P = [G3b[:, 2048:3072], G2b[:, 2048:3072]]
        adt2P = [G3b[:, 3072:4096], G2b[:, 3072:4096]]
        y2fP = [G4f[:, 0:1024], G4f[:, 1024:2048]]
        outfP = [G4f[:, 2048:3072], G4f[:, 3072:4096]]
        colAP = [colA, colC]
        colBP = [colB, colD]
        colXP = [colE, colF]
        bn6P = [bn6, bn6b]
        mv2P = [mv2, mv2b]
        dep_init = [t_stage_sx[3][-1], prev_upmm, prev_dnev, t_idsT[3]]
        t_out = []
        hist = {}
        for tc in range(NTC):
            p = tc % 2
            rg0, rg1, x1t2, adt2 = rg0P[p], rg1P[p], x1t2P[p], adt2P[p]
            y2f, outf = y2fP[p], outfP[p]
            cA, cB, cX, bnb, mvb = colAP[p], colBP[p], colXP[p], bn6P[p], mv2P[p]
            pv = hist.get(tc - 2, {})
            d_ag = ag_all if tc < 2 else []
            d_in = dep_init if tc < 2 else []
            g0 = QG(lambda g, tc=tc, rg0=rg0: g.indirect_dma_start(
                rg0, None, d_ywg[:, :],
                bass.IndirectOffsetOnAxis(ap=gsc_u[:, tc * 2:tc * 2 + 1], axis=0),
                bounds_check=rB_map, oob_is_err=False),
                deps=d_ag + d_in + ([pv["s2"]] if pv else []), tag=f"fg0{tc}")
            g1 = QG(lambda g, tc=tc, rg1=rg1: g.indirect_dma_start(
                rg1, None, d_ywg[:, :],
                bass.IndirectOffsetOnAxis(ap=gsc_u[:, tc * 2 + 1:tc * 2 + 2], axis=0),
                bounds_check=rB_map, oob_is_err=False),
                deps=[g0] + ([pv["s1"]] if pv else []), tag=f"fg1{tc}")
            lx = QS(lambda s, tc=tc, x1t2=x1t2: s.dma_start(
                out=x1t2, in_=d_x1tm[tc * 128:(tc + 1) * 128, :]),
                deps=d_in + ([pv["s1"]] if pv else []), tag=f"fx{tc}")
            la = QV(lambda s, tc=tc, adt2=adt2: s.dma_start(
                out=adt2, in_=d_ad[tc * 128:(tc + 1) * 128, :]),
                deps=[t_ad_st[-1]] + d_in + ([pv["s3"]] if pv else []), tag=f"fa{tc}")
            s1 = V(lambda v, tc=tc, y2f=y2f, rg1=rg1, x1t2=x1t2: v.scalar_tensor_tensor(
                y2f, rg1, w01_s[:, tc * 2 + 1:tc * 2 + 2], x1t2, OP.mult, OP.add),
                deps=[g1, lx] + ([pv["nm"]] if pv else []), tag=f"fs1{tc}")
            s2 = V(lambda v, tc=tc, y2f=y2f, rg0=rg0: v.scalar_tensor_tensor(
                y2f, rg0, w01_s[:, tc * 2:tc * 2 + 1], y2f, OP.mult, OP.add),
                deps=[s1, g0], tag=f"fs2{tc}")
            s3 = V(lambda v, y2f=y2f, adt2=adt2: v.tensor_tensor(y2f, y2f, adt2, OP.add),
                   deps=[s2, la], tag=f"fs3{tc}")
            a2 = V(lambda v, y2f=y2f, bnb=bnb: v.bn_stats(bnb[:, 0:6], y2f[:, 0:512]),
                   deps=[s3] + ([pv["a4"]] if pv else []))
            a3 = V(lambda v, y2f=y2f, bnb=bnb: v.bn_stats(bnb[:, 6:12], y2f[:, 512:1024]), deps=[a2])
            a4 = V(lambda v, bnb=bnb, mvb=mvb: v.bn_aggr(mvb[:], bnb[:]),
                   deps=[a3] + ([pv["nm"]] if pv else []), tag=f"fbn{tc}")
            a7 = V(lambda v, cA=cA, mvb=mvb: v.tensor_scalar(cA[:], mvb[:, 1:2], EPS, None, OP.add),
                   deps=[a4] + ([pv["f1"]] if pv else []))
            f1 = SC(lambda s, cA=cA, cB=cB: s.activation(out=cB[:], in_=cA[:], func=AF.Ln),
                    deps=[a7] + ([pv["nm"]] if pv else []), tag=f"fln{tc}")
            f2 = SC(lambda s, cB=cB: s.activation(out=cB[:], in_=cB[:], func=AF.Exp, scale=-0.5),
                    deps=[f1])
            cx = V(lambda v, cX=cX, cB=cB, mvb=mvb: v.tensor_scalar(
                cX[:], mvb[:, 0:1], cB[:, :], -1.0, OP.mult, OP.mult),
                deps=[f2] + ([pv["nm"]] if pv else []), tag=f"fcx{tc}")
            nm = SC(lambda s, outf=outf, y2f=y2f, cB=cB, cX=cX: s.activation(
                out=outf, in_=y2f, func=AF.Identity, bias=cX[:, :], scale=cB[:, :]),
                deps=[cx] + ([pv["st"]] if pv else []), tag=f"fnm{tc}")
            st_ = QS(lambda s, tc=tc, outf=outf: s.dma_start(
                out=o_out[tc * 128:(tc + 1) * 128, :], in_=outf), deps=[nm], tag=f"fout{tc}")
            t_out.append(st_)
            hist[tc] = dict(s1=s1, s2=s2, s3=s3, a4=a4, f1=f1, nm=nm, st=st_)
            if "y2pre" in dbg:
                t_y2p = QS(lambda s, tc=tc, y2f=y2f: s.dma_start(
                    out=dbg["y2pre"][tc * 128:(tc + 1) * 128, :], in_=y2f),
                    deps=[nm], tag=f"dbgy2p{tc}")
                t_out.append(t_y2p)
                hist[tc]["nm"] = t_y2p
        fin_deps = t_out
        if "xtm" in dbg:
            ln_done.append(QG(lambda g: g.dma_start(out=dbg["xtm"][:, :], in_=d_xtm[:, :]),
                              deps=[t_xtm], tag="dbgxtm"))
        if "ypred" in dbg:
            ln_done.append(QG(lambda g: g.dma_start(out=dbg["ypred"][:, :], in_=d_ypre[:, :]),
                              deps=[t_ypst], tag="dbgypred"))
        if "poolcq" in dbg:
            ln_done.append(QS(lambda s: s.dma_start(out=dbg["poolcq"][:, :], in_=poolCq),
                              deps=[t_qpool], tag="dbgpcq"))
        if "poolck" in dbg:
            ln_done.append(QS(lambda s: s.dma_start(out=dbg["poolck"][:, :], in_=poolCk),
                              deps=[t_kpool], tag="dbgpck"))
        if "mqd" in dbg:
            ln_done.append(QS(lambda s: s.dma_start(out=dbg["mqd"][:, :], in_=G1b[:, 0:4096]),
                              deps=[t_qpool], tag="dbgmqd"))
        if "hrow" in dbg:
            ln_done.append(QS(lambda s: s.dma_start(out=dbg["hrow"][:, :], in_=G4b[0:16, 0:4096]),
                              deps=[t_hev[(0, tt)] for tt in range(NTT)] + [sel_done], tag="dbgh"))
        if "lgt" in dbg:
            ln_done.append(QS(lambda s: s.dma_start(out=dbg["lgt"][:, :], in_=lgt),
                              deps=[t_lgt[tt] for tt in range(NTT)] + [sel_done], tag="dbglgt"))
        if "probs" in dbg:
            ln_done.append(QS(lambda s: s.dma_start(
                out=dbg["probs"].rearrange("(tc p) e -> p tc e", p=128),
                in_=probs_s.ap().rearrange("p (tc e) -> p tc e", e=E)),
                deps=t_probs, tag="dbgprobs"))
        if "w01" in dbg:
            ln_done.append(QS(lambda s: s.dma_start(out=dbg["w01"][:, :], in_=w01_s[:]),
                              deps=[sel_done], tag="dbgw01"))
        if "theta" in dbg:
            ln_done.append(QG(lambda g: g.dma_start(out=dbg["theta"][:, :], in_=d_thout[:, :]),
                              deps=[t_agT], tag="dbgtheta"))
        if "cnt" in dbg:
            ln_done.append(QG(lambda g: g.dma_start(out=dbg["cnt"][:, :], in_=d_cnout[:, :]),
                              deps=[t_agC], tag="dbgcnt"))
        if "mfull" in dbg:
            ln_done.append(QS(lambda s: s.dma_start(
                out=dbg["mfull"].rearrange("(tc p) e -> p tc e", p=128),
                in_=mfull_s.ap().rearrange("p (tc e) -> p tc e", e=E)),
                deps=[sel_done], tag="dbgmfull"))
        if "gscat" in dbg:
            ln_done.append(QS(lambda s: s.dma_start(
                out=dbg["gscat"].rearrange("(tc p) e -> p tc e", p=128).bitcast(U32),
                in_=gsc_u.ap().rearrange("p (tc e) -> p tc e", e=2)),
                deps=[gsc_done], tag="dbggsc"))
        if "idxg" in dbg:
            ln_done.append(QG(lambda g: g.dma_start(out=dbg["idxg"][:, :], in_=d_idxg[:, :]),
                              deps=[t_arI], tag="dbgidxg"))
        if "yw" in dbg:
            ln_done.append(QG(lambda g: g.dma_start(out=dbg["yw"][:, :], in_=d_yw[:, :]),
                              deps=[t_agYk[kk] for kk in range(ELOC)], tag="dbgyw"))
        if "advec" in dbg:
            ln_done.append(QG(lambda g: g.dma_start(out=dbg["advec"][:, :], in_=d_ad[:, :]),
                              deps=[t_ad_st[-1]], tag="dbgadvec"))
        if "x1" in dbg:
            t_dx1 = [QG(lambda g: g.dma_start(out=dbg["x1"][:, :], in_=d_x1tm[:, :]),
                        deps=ln_done, tag="dbgx1")]
            ln_done = ln_done + t_dx1
        if "stats" in dbg:
            t_dst = QS(lambda s: s.dma_start(out=dbg["stats"][:, :], in_=stats16[0:8, :]),
                       deps=ln_done, tag="dbgstats")
            ln_done = ln_done + [t_dst]

        S.op("sync", lambda s: s.nop(), deps=fin_deps + ln_done + stageb_done, amt=1, tag="fence")

        S.finalize(block)
    return nc



def _prep_inputs(inputs):
    f32 = np.float32
    x = np.ascontiguousarray(inputs["L0_attn_hidden_states"].reshape(TGLOB, D), f32)
    wq = np.ascontiguousarray(inputs["L0_attn_Wq"], f32)
    wk = np.ascontiguousarray(inputs["L0_attn_Wk"], f32)
    wt = np.ascontiguousarray(inputs["L0_attn_Wt"], f32)

    def bias_cols(b):
        return np.ascontiguousarray(b.reshape(8, 128).T, f32)

    bqc = bias_cols(inputs["L0_attn_bq"]); bkc = bias_cols(inputs["L0_attn_bk"])
    btc = bias_cols(inputs["L0_attn_bt"])
    wqa = np.ascontiguousarray(inputs["L0_attn_Wqa"], f32)
    wka = np.ascontiguousarray(inputs["L0_attn_Wka"], f32)
    bqa = np.ascontiguousarray(inputs["L0_attn_bqa"].reshape(H, 1), f32)
    bka = np.ascontiguousarray(inputs["L0_attn_bka"].reshape(H, 1), f32)
    r1w = inputs["L0_ffn_r1_w"]
    r1wt = np.zeros((1040, 512), f32)
    r1wt[0:1030, :] = r1w.T
    r1bc = np.ascontiguousarray(inputs["L0_ffn_r1_b"].reshape(4, 128).T, f32)
    r2wt = np.ascontiguousarray(inputs["L0_ffn_r2_w"].T, f32)
    wupt = np.ascontiguousarray(inputs["L0_ffn_w_up"].transpose(0, 2, 1), f32)
    wdnt = np.ascontiguousarray(inputs["L0_ffn_w_down"].transpose(0, 2, 1), f32)
    aupt = np.ascontiguousarray(inputs["L0_ffn_a_up"].transpose(0, 2, 1), f32)
    adnt = np.ascontiguousarray(inputs["L0_ffn_a_down"].transpose(0, 2, 1), f32)

    in_maps = []
    for c in range(NCORE):
        b = c // 2
        half = c % 2
        im = dict(
            xh=np.ascontiguousarray(x[c * TCORE:(c + 1) * TCORE]),
            wq=wq, wk=wk, wt=wt, bqc=bqc, bkc=bkc, btc=btc,
            wqa=wqa, wka=wka, bqa=bqa, bka=bka,
            r1wt=r1wt, r1bc=r1bc, r2wt=r2wt,
            wupt=np.ascontiguousarray(wupt[c * ELOC:(c + 1) * ELOC]),
            wdnt=np.ascontiguousarray(wdnt[c * ELOC:(c + 1) * ELOC]),
            aupt=aupt, adnt=adnt,
            c_mask8=(np.arange(8) < c).astype(f32).reshape(8, 1),
            c_tokbase=np.full((128, 1), c * TCORE, f32),
            c_poolA=(32 * b + np.arange(16)).astype(np.uint32).reshape(16, 1),
            c_poolB=(32 * b + 16 + np.arange(16)).astype(np.uint32).reshape(16, 1),
            c_sgem=np.array([[(32 * ((p % 32) // 4) + 4 * c + (p // 32)) * 4 + (p % 4)]
                             for p in range(128)], np.uint32),
            c_idxl=np.array([80 * k + 10 * c + gi for k in range(4)
                             for gi in range(10)], np.uint32).reshape(40, 1),
            c_eoff=np.array([[(e % 4) * 10240 + (e // 4) * 1280
                              for e in range(32)]], np.float32),
        )
        in_maps.append(im)
    return in_maps


_NC_CACHE = {}


def kernel(**inputs):
    debug = tuple(sorted(DEBUG)) if isinstance(DEBUG, (list, tuple, set)) else ()
    key = debug
    if key not in _NC_CACHE:
        _NC_CACHE[key] = build_nc(debug)
    nc = _NC_CACHE[key]
    in_maps = _prep_inputs(inputs)
    res = run_bass_kernel_spmd(nc, in_maps, core_ids=list(range(NCORE)))
    out = np.concatenate([np.asarray(res.results[c]["o_out"], np.float32)
                          for c in range(NCORE)], axis=0)
    kernel.last_results = res
    return out.reshape(4, 8192, D)


DEBUG = ()
